# version 32
# baseline (speedup 1.0000x reference)
import contextlib

import ml_dtypes
import numpy as np

import concourse.bass as bass
import concourse.mybir as mybir
from concourse.alu_op_type import AluOpType as OP
from concourse.bass_utils import run_bass_kernel_spmd

F32 = mybir.dt.float32
BF16 = mybir.dt.bfloat16
AF = mybir.ActivationFunctionType
AX = mybir.AxisListType.X

N_CORES = 8
T, D, H = 16384, 1024, 16
TC = T // N_CORES
C = 128
NCH = TC // C
DH = 4096
MID = C // 2 - 1

_DEBUG_SCRATCH = set()
_PHASES = (1, 2, 3, 4, 5, 6)
_P3_CHUNKS = NCH
_P3_BWD = True
_P3_LEVEL = 9
_P4_LEVEL = 8



_ENGINE_OF = {
    "pe": "tensor", "act": "scalar", "dve": "vector", "pool": "gpsimd",
    "sp": "sync", "actdma": "scalar", "pooldma": "gpsimd",
}
_DMA_QUEUES = {"sp", "actdma", "pooldma"}


_DMA_SUBSEMS = 6


class _Op:
    __slots__ = ("q", "fn", "deps", "needs_inc", "inc_val", "sem_idx", "gate",
                 "pre_waits")

    def __init__(self, q, fn):
        self.q = q
        self.fn = fn
        self.deps = []
        self.needs_inc = q in _DMA_QUEUES
        self.inc_val = None
        self.sem_idx = 0
        self.gate = None
        self.pre_waits = ()


class Sched:

    def __init__(self, nc, sems, dma_sems):
        self.nc = nc
        self.sems = sems
        self.dma_sems = dma_sems
        self.inc_count = {q: 0 for q in _ENGINE_OF}
        self.dma_count = {q: 0 for q in _DMA_QUEUES}
        self.ops = []
        self.last_writer = {}
        self.readers = {}

    def dma_drain(self):
        for q in _DMA_QUEUES:
            waits = []
            n_dma = self.dma_count[q]
            for idx in range(_DMA_SUBSEMS):
                cnt = (n_dma - idx + _DMA_SUBSEMS - 1) // _DMA_SUBSEMS
                if cnt > 0:
                    waits.append((self.dma_sems[q][idx], 16 * cnt, (q, idx)))
            if waits:
                o = _Op(q, None)
                o.needs_inc = False
                o.pre_waits = waits
                self.ops.append(o)

    def op(self, q, fn, reads=(), writes=(), force_inc=False):
        o = _Op(q, fn)
        if q in _DMA_QUEUES:
            i = self.dma_count[q]
            self.dma_count[q] = i + 1
            o.sem_idx = i % _DMA_SUBSEMS
            o.inc_val = 16 * (i // _DMA_SUBSEMS + 1)
            if i >= _DMA_SUBSEMS:
                o.gate = o.inc_val - 16
        for k in reads:
            w = self.last_writer.get(k)
            if w is not None:
                o.deps.append(w)
            self.readers.setdefault(k, []).append(o)
        for k in writes:
            w = self.last_writer.get(k)
            if w is not None:
                o.deps.append(w)
            o.deps.extend(self.readers.get(k, ()))
            self.last_writer[k] = o
            self.readers[k] = []
        self.ops.append(o)
        return o

    def emit(self, block):
        ops = self.ops
        self.ops = []
        for o in ops:
            o.deps = [d for d in o.deps
                      if d is not o and not (d.q == "pe" and o.q == "pe")]
            for d in o.deps:
                d.needs_inc = True
        for o in ops:
            if o.needs_inc and o.inc_val is None:
                self.inc_count[o.q] += 1
                o.inc_val = self.inc_count[o.q]
        per_engine = {}
        for o in ops:
            per_engine.setdefault(_ENGINE_OF[o.q], []).append(o)

        def dep_sem(d):
            if d.q in _DMA_QUEUES:
                return self.dma_sems[d.q][d.sem_idx], (d.q, d.sem_idx)
            return self.sems[d.q], (d.q, 0)

        def mk_section(eng_ops):
            def section(eng):
                seen = {}
                for o in eng_ops:
                    for sem, v, key in o.pre_waits:
                        if seen.get(key, 0) < v:
                            seen[key] = v
                            eng.wait_ge(sem, v)
                    if o.fn is None:
                        continue
                    if o.gate is not None:
                        sem = self.dma_sems[o.q][o.sem_idx]
                        key = (o.q, o.sem_idx)
                        if seen.get(key, 0) < o.gate:
                            seen[key] = o.gate
                            eng.wait_ge(sem, o.gate)
                    for d in o.deps:
                        if d.inc_val is None:
                            continue
                        sem, key = dep_sem(d)
                        if seen.get(key, 0) >= d.inc_val:
                            continue
                        seen[key] = d.inc_val
                        eng.wait_ge(sem, d.inc_val)
                    inst = o.fn(eng)
                    if o.needs_inc:
                        if o.q in _DMA_QUEUES:
                            inst.then_inc(self.dma_sems[o.q][o.sem_idx], 16)
                        else:
                            inst.then_inc(self.sems[o.q], 1)
            return section

        for name, eng_ops in per_engine.items():
            getattr(block, name)(mk_section(eng_ops))



def _chunk_cols(v):
    return np.ascontiguousarray(np.asarray(v, np.float32).reshape(8, 128).T)


def _swiz(w):
    w = np.asarray(w, np.float32)
    ni = w.shape[0] // 128
    return np.ascontiguousarray(
        w.reshape(ni, 128, w.shape[1]).transpose(1, 0, 2).reshape(128, -1))


def _mw2d(w):
    out = np.zeros((64, 3 * D), np.float32)
    for f in range(5):
        blk, par = f // 2, f % 2
        out[par * 32:(par + 1) * 32, blk * D:(blk + 1) * D] = w[f]
    return np.ascontiguousarray(out)


def _blkw(wT, nblk_o):
    inn, outn = wT.shape
    ni = inn // 128
    a = wT.reshape(ni, 128, nblk_o, 128).transpose(2, 1, 0, 3)
    return np.ascontiguousarray(a.reshape(nblk_o, 128, inn))


def prep_inputs(ins):
    x = np.asarray(ins["L0_attn_x"], np.float32)[0]
    conv_w = np.asarray(ins["L0_attn_conv_w"], np.float32)
    j = np.arange(C)[:, None]
    i = np.arange(C)[None, :]
    tri_ci = -(j <= i).astype(np.float32)
    tri_cs = -(j > i).astype(np.float32)
    tri_cc = ((j <= MID).astype(np.float32) - (j <= i))
    tri_cb = ((j > MID).astype(np.float32) - (j > i))
    aux = np.zeros((C, 4), np.float32)
    aux[:, 0] = -(np.arange(C) <= MID).astype(np.float32)
    aux[:, 1] = -1.0
    aux[:, 2] = -(np.arange(C) > MID).astype(np.float32)
    mask_f = (j <= i).astype(np.float32)
    mask_b = (j > i).astype(np.float32)
    pmask = np.zeros((128, 2), np.float32)
    pmask[0:64, 0] = 1.0
    pmask[64:128, 1] = 1.0
    colmask = np.zeros((128, 2048), np.float32)
    cols = np.arange(1024)
    colmask[:, 0:1024] = ((cols % 128) < 64).astype(np.float32)[None, :]
    colmask[:, 1024:2048] = ((cols % 128) >= 64).astype(np.float32)[None, :]

    def rep(v):
        return np.ascontiguousarray(
            np.broadcast_to(np.asarray(v, np.float32)[None, :], (128, D)))

    w1T = np.asarray(ins["L0_ffn_w1_w"], np.float32).T
    w2T = np.asarray(ins["L0_ffn_w2_w"], np.float32).T
    h1T = np.asarray(ins["L0_ffn_h1_w"], np.float32).T
    h2T = np.asarray(ins["L0_ffn_h2_w"], np.float32).T
    w3T = np.asarray(ins["L0_ffn_w3_w"], np.float32).T

    def scale_cols(s, nblk):
        return np.ascontiguousarray(np.asarray(s, np.float32).reshape(nblk, 128).T)

    def _bf(a):
        return np.ascontiguousarray(np.asarray(a).astype(ml_dtypes.bfloat16))

    shared = {
        "c0": _chunk_cols(conv_w[:, 0, 0]),
        "c1": _chunk_cols(conv_w[:, 0, 1]),
        "c2": _chunk_cols(conv_w[:, 0, 2]),
        "maa_x": _chunk_cols(ins["L0_attn_maa_x"]),
        "maa_w": _chunk_cols(ins["L0_attn_maa_w"]),
        "maa_k": _chunk_cols(ins["L0_attn_maa_k"]),
        "maa_v": _chunk_cols(ins["L0_attn_maa_v"]),
        "maa_r": _chunk_cols(ins["L0_attn_maa_r"]),
        "maa_g": _chunk_cols(ins["L0_attn_maa_g"]),
        "maa_w1": _bf(_swiz(np.asarray(ins["L0_attn_maa_w1"], np.float32))),
        "maa_w2": _bf(_mw2d(np.asarray(ins["L0_attn_maa_w2"], np.float32))),
        "dec_w1": _bf(_swiz(np.asarray(ins["L0_attn_decay_w1"], np.float32))),
        "dec_w2": _bf(np.asarray(ins["L0_attn_decay_w2"], np.float32)),
        "td_rep": rep(ins["L0_attn_time_decay"]),
        "Wr": _bf(_swiz(ins["L0_attn_Wr"])), "Wk": _bf(_swiz(ins["L0_attn_Wk"])),
        "Wv": _bf(_swiz(ins["L0_attn_Wv"])), "Wg": _bf(_swiz(ins["L0_attn_Wg"])),
        "Wo": _bf(_swiz(ins["L0_attn_Wo"])), "gate_w": _bf(_swiz(ins["L0_attn_gate_w"])),
        "gate_b_rep": rep(ins["L0_attn_gate_b"]),
        "ln_w_rep": rep(ins["L0_attn_ln_w"]),
        "ln_b_rep": rep(ins["L0_attn_ln_b"]),
        "tri_ci": tri_ci, "tri_cs": tri_cs, "tri_cc": tri_cc, "tri_cb": tri_cb,
        "pmask": pmask, "colmask": _bf(colmask),
        "aux_cols": np.ascontiguousarray(aux),
        "mask_f": mask_f, "mask_b": mask_b,
        "w1b": _bf(_blkw(w1T, 32)), "w2b": _bf(_blkw(w2T, 32)),
        "h1b": _bf(_blkw(h1T, 32)), "h2b": _bf(_blkw(h2T, 32)),
        "w3t": _bf(_swiz(w3T)),
        "s1c": scale_cols(ins["L0_ffn_w1_s"], 32),
        "s2c": scale_cols(ins["L0_ffn_w2_s"], 32),
        "sh1c": scale_cols(ins["L0_ffn_h1_s"], 32),
        "sh2c": scale_cols(ins["L0_ffn_h2_s"], 32),
        "s3_rep": rep(ins["L0_ffn_w3_s"]),
    }

    in_maps = []
    for c in range(N_CORES):
        s, e = c * TC, (c + 1) * TC
        xh = np.zeros((TC + 2, D), np.float32)
        xh[1:-1] = x[s:e]
        if s > 0:
            xh[0] = x[s - 1]
        if e < T:
            xh[-1] = x[e]
        xf = np.ascontiguousarray(
            xh.T.reshape(8, 128, TC + 2).transpose(1, 0, 2))
        cmf = np.zeros((128, 8), np.float32)
        cmb = np.zeros((128, 8), np.float32)
        cmf[:, :c] = 1.0
        cmb[:, c + 1:] = 1.0
        m = dict(shared)
        m["xf"] = xf
        m["xt"] = np.ascontiguousarray(x[s:e])
        m["cmask_f"] = cmf
        m["cmask_b"] = cmb
        in_maps.append(m)
    return in_maps



_BF16_INPUTS = {
    "maa_w1", "maa_w2", "dec_w1", "dec_w2",
    "Wr", "Wk", "Wv", "Wg", "Wo", "gate_w",
    "w1b", "w2b", "h1b", "h2b", "w3t", "colmask",
}


def build_nc():
    nc = bass.Bass("TRN2", target_bir_lowering=False)

    EI = dict(kind="ExternalInput")
    inp = {}
    for name, shape in [
        ("xf", [128, 8, TC + 2]), ("xt", [TC, D]),
        ("c0", [128, 8]), ("c1", [128, 8]), ("c2", [128, 8]),
        ("maa_x", [128, 8]), ("maa_w", [128, 8]), ("maa_k", [128, 8]),
        ("maa_v", [128, 8]), ("maa_r", [128, 8]), ("maa_g", [128, 8]),
        ("maa_w1", [128, 8 * 160]), ("maa_w2", [64, 3 * D]),
        ("dec_w1", [128, 8 * 64]), ("dec_w2", [64, D]), ("td_rep", [128, D]),
        ("Wr", [128, 8 * D]), ("Wk", [128, 8 * D]), ("Wv", [128, 8 * D]),
        ("Wg", [128, 8 * D]), ("Wo", [128, 8 * D]), ("gate_w", [128, 8 * D]),
        ("gate_b_rep", [128, D]), ("ln_w_rep", [128, D]), ("ln_b_rep", [128, D]),
        ("tri_ci", [C, C]), ("tri_cs", [C, C]), ("tri_cc", [C, C]),
        ("tri_cb", [C, C]), ("aux_cols", [C, 4]),
        ("mask_f", [C, C]), ("mask_b", [C, C]),
        ("pmask", [128, 2]), ("colmask", [128, 2048]),
        ("cmask_f", [128, 8]), ("cmask_b", [128, 8]),
        ("w1b", [32, 128, D]), ("w2b", [32, 128, D]),
        ("h1b", [32, 128, DH]), ("h2b", [32, 128, DH]), ("w3t", [128, 32 * D]),
        ("s1c", [128, 32]), ("s2c", [128, 32]), ("sh1c", [128, 32]),
        ("sh2c", [128, 32]), ("s3_rep", [128, D]),
    ]:
        dt = BF16 if name in _BF16_INPUTS else F32
        inp[name] = nc.dram_tensor(name, shape, dt, **EI)
    out = nc.dram_tensor("out", [TC, D], F32, kind="ExternalOutput")

    def scratch(name, shape, dt):
        kind = "ExternalOutput" if name in _DEBUG_SCRATCH else "Internal"
        return nc.dram_tensor(name, shape, dt, kind=kind)

    sc = {
        "xw": scratch("sc_xw", [D, TC], BF16),
        "xk": scratch("sc_xk", [D, TC], BF16),
        "xv": scratch("sc_xv", [D, TC], BF16),
        "xr": scratch("sc_xr", [D, TC], BF16),
        "xg": scratch("sc_xg", [D, TC], BF16),
        "dx": scratch("sc_dx", [D, TC], BF16),
        "r": scratch("sc_r", [NCH, 128, 8 * C], BF16),
        "kf": scratch("sc_kf", [NCH, 128, 8 * C], BF16),
        "k": scratch("sc_k", [NCH, D, C], BF16),
        "v": scratch("sc_v", [TC, D], BF16),
        "p": scratch("sc_p", [TC, D], F32),
        "g": scratch("sc_g", [TC, D], F32),
        "sig": scratch("sc_sig", [TC, D], F32),
        "x2": scratch("sc_x2", [TC, D], F32),
        "x2b": scratch("sc_x2b", [TC, D], BF16),
        "zb": scratch("sc_zb", [TC, D], BF16),
        "y": scratch("sc_y", [TC, D], F32),
        "ccin": scratch("cc_in", [2064, 64], F32),
        "ccout": scratch("cc_out", [2064 * N_CORES, 64], F32),
    }

    with contextlib.ExitStack() as stack:
        sems = {}
        dma_sems = {}
        for q in _ENGINE_OF:
            if q in _DMA_QUEUES:
                dma_sems[q] = [stack.enter_context(nc.semaphore(f"sem_{q}{i}"))
                               for i in range(_DMA_SUBSEMS)]
            else:
                sems[q] = stack.enter_context(nc.semaphore("sem_" + q))
        sch = Sched(nc, sems, dma_sems)

        def sbt(st, name, shape, dt):
            return st.enter_context(nc.sbuf_tensor(name, shape, dt))

        consts = sbt(stack, "consts", [128, 8], F32)
        CONST_COLS = {"zero": 0, "one": 1, "neg1": 2, "i1024": 3,
                      "eps_gn": 4, "eps_ln": 5, "i64": 6, "neg_half": 7}
        CONST_VALS = {"zero": 0.0, "one": 1.0, "neg1": -1.0, "i1024": 1.0 / 1024,
                      "eps_gn": 1e-5 * 64.0, "eps_ln": 1e-5, "i64": 1.0 / 64,
                      "neg_half": -0.5}

        def ccol(name):
            return consts[:, CONST_COLS[name]:CONST_COLS[name] + 1]

        def cread(name):
            return ("consts", CONST_COLS[name])

        smalls = {}
        for name, shape in [
            ("c0", [128, 8]), ("c1", [128, 8]), ("c2", [128, 8]),
            ("maa_x", [128, 8]), ("maa_w", [128, 8]), ("maa_k", [128, 8]),
            ("maa_v", [128, 8]), ("maa_r", [128, 8]), ("maa_g", [128, 8]),
            ("tri_ci", [C, C]), ("tri_cs", [C, C]), ("tri_cc", [C, C]),
            ("tri_cb", [C, C]), ("aux_cols", [C, 4]),
            ("mask_f", [C, C]), ("mask_b", [C, C]),
            ("pmask", [128, 2]),
            ("cmask_f", [128, 8]), ("cmask_b", [128, 8]),
        ]:
            smalls[name] = sbt(stack, "s_" + name, shape, F32)

        for name, col in CONST_COLS.items():
            sch.op("pool", lambda e, c=col, v=CONST_VALS[name]: e.memset(
                consts[:, c:c + 1], v), writes=[("consts", col)])
        for name in smalls:
            sch.op("sp", lambda e, n=name: e.dma_start(smalls[n][:, :], inp[n].ap()),
                   writes=[("small", name)])
        sch.dma_drain()
        with nc.Block() as blk:
            sch.emit(blk)

        if 1 in _PHASES:
            with contextlib.ExitStack() as phs:
                xin = sbt(phs, "p1_xin", [128, 8, TC + 2], F32)
                dxb = sbt(phs, "p1_dx", [128, 8 * TC], BF16)
                xxx = sbt(phs, "p1_xxx", [128, 8 * TC], BF16)
                tmpas = [sbt(phs, "p1_tmpa0", [128, TC], F32)] * 2
                tmpbs = [sbt(phs, "p1_tmpb0", [128, TC], F32)] * 2
                mw1 = sbt(phs, "p1_mw1", [128, 8 * 160], BF16)
                mw2 = sbt(phs, "p1_mw2", [64, 3 * D], BF16)
                t5a = sbt(phs, "p1_t5a", [64, TC], BF16)
                t5b = sbt(phs, "p1_t5b", [64, TC], BF16)
                t5c = sbt(phs, "p1_t5c", [32, TC], BF16)
                xog = [sbt(phs, f"p1_xo{i}", [128, 2048], BF16) for i in range(5)]
                mtmp = [sbt(phs, f"p1_mt{i}", [128, 512], F32) for i in range(2)]
                p_t5 = [phs.enter_context(
                    nc.psum_tensor(f"pp_t5{i}", [64, 512], F32)) for i in range(2)]
                p_t5x = phs.enter_context(nc.psum_tensor("pp_t5x", [32, 512], F32))
                p_m = [phs.enter_context(
                    nc.psum_tensor(f"pp_m{i}", [128, 512], F32)) for i in range(5)]

                sch.op("pooldma", lambda e: e.dma_start(mw1[:, :], inp["maa_w1"].ap()),
                       writes=["mw1"])
                sch.op("pooldma", lambda e: e.dma_start(mw2[:, :], inp["maa_w2"].ap()),
                       writes=["mw2"])
                for j in range(8):
                    sch.op("sp", lambda e, j=j: e.dma_start(
                        xin[:, j, :], inp["xf"].ap()[:, j, :]),
                        writes=[("xin", j)])

                for j in range(8):
                    jb2 = 0
                    tmpa, tmpb = tmpas[jb2], tmpbs[jb2]
                    ka, kb = ("tmpa", jb2), ("tmpb", jb2)
                    x_l = xin[:, j, 0:TC]
                    x_c = xin[:, j, 1:TC + 1]
                    x_r = xin[:, j, 2:TC + 2]
                    sch.op("dve", lambda e, x_l=x_l, j=j, tmpa=tmpa: e.tensor_scalar(
                        tmpa[:, :], x_l, smalls["c0"][:, j:j + 1], None, OP.mult),
                        reads=[("xin", j), ("small", "c0")], writes=[ka])
                    sch.op("dve", lambda e, x_c=x_c, j=j, tmpa=tmpa, tmpb=tmpb: e.scalar_tensor_tensor(
                        tmpb[:, :], x_c, smalls["c1"][:, j:j + 1], tmpa[:, :],
                        OP.mult, OP.add),
                        reads=[("xin", j), ("small", "c1"), ka], writes=[kb])
                    sch.op("dve", lambda e, x_r=x_r, j=j, tmpa=tmpa, tmpb=tmpb: e.scalar_tensor_tensor(
                        tmpa[:, :], x_r, smalls["c2"][:, j:j + 1], tmpb[:, :],
                        OP.mult, OP.add),
                        reads=[("xin", j), ("small", "c2"), kb], writes=[ka])
                    sch.op("dve", lambda e, x_c=x_c, j=j, tmpa=tmpa: e.tensor_tensor(
                        dxb[:, j * TC:(j + 1) * TC], tmpa[:, :], x_c, OP.subtract),
                        reads=[ka, ("xin", j)], writes=[("dxb", j)])
                    sch.op("dve", lambda e, x_c=x_c, j=j: e.scalar_tensor_tensor(
                        xxx[:, j * TC:(j + 1) * TC], dxb[:, j * TC:(j + 1) * TC],
                        smalls["maa_x"][:, j:j + 1], x_c, OP.mult, OP.add),
                        reads=[("dxb", j), ("small", "maa_x"), ("xin", j)],
                        writes=[("xxx", j)])
                    sch.op("sp", lambda e, j=j: e.dma_start(
                        sc["dx"].ap()[128 * j:128 * (j + 1), :],
                        dxb[:, j * TC:(j + 1) * TC]),
                        reads=[("dxb", j)], writes=[("scr", "dx")])

                for tt in range(4):
                    for ic in range(8):
                        for g in range(2):
                            sch.op("pe", lambda e, ic=ic, tt=tt, g=g: e.matmul(
                                p_t5[g][:, :],
                                mw1[:, ic * 160 + g * 64:ic * 160 + g * 64 + 64],
                                xxx[:, ic * TC + tt * 512: ic * TC + (tt + 1) * 512],
                                start=(ic == 0), stop=(ic == 7)),
                                reads=["mw1", ("xxx", ic)], writes=[("p_t5", g)])
                        sch.op("pe", lambda e, ic=ic, tt=tt: e.matmul(
                            p_t5x[:, :], mw1[:, ic * 160 + 128:ic * 160 + 160],
                            xxx[:, ic * TC + tt * 512: ic * TC + (tt + 1) * 512],
                            start=(ic == 0), stop=(ic == 7)),
                            reads=["mw1", ("xxx", ic)], writes=["p_t5x"])
                    sch.op("act", lambda e, tt=tt: e.activation(
                        t5a[:, tt * 512:(tt + 1) * 512], p_t5[0][:, :], AF.Tanh),
                        reads=[("p_t5", 0)], writes=["t5a"])
                    sch.op("act", lambda e, tt=tt: e.activation(
                        t5b[:, tt * 512:(tt + 1) * 512], p_t5[1][:, :], AF.Tanh),
                        reads=[("p_t5", 1)], writes=["t5b"])
                    sch.op("act", lambda e, tt=tt: e.activation(
                        t5c[:, tt * 512:(tt + 1) * 512], p_t5x[:, :], AF.Tanh),
                        reads=["p_t5x"], writes=["t5c"])

                t5block = [(t5a, 0, "t5a"), (t5a, 32, "t5a"), (t5b, 0, "t5b"),
                           (t5b, 32, "t5b"), (t5c, 0, "t5c")]
                names = ["xw", "xk", "xv", "xr", "xg"]
                maas = ["maa_w", "maa_k", "maa_v", "maa_r", "maa_g"]
                for oc in range(8):
                    for f in range(5):
                        tile, base, tkey = t5block[f]
                        blk5 = f // 2
                        xot = xog[f]
                        for tt in range(4):
                            sch.op("pe", lambda e, f=f, oc=oc, tt=tt, tile=tile, base=base, blk5=blk5: e.matmul(
                                p_m[f][:, :],
                                mw2[base:base + 32,
                                    blk5 * D + oc * 128: blk5 * D + (oc + 1) * 128],
                                tile[base:base + 32, tt * 512:(tt + 1) * 512],
                                start=True, stop=True),
                                reads=["mw2", tkey], writes=[("p_m", f)])
                            mt = mtmp[(f + tt) % 2]
                            sch.op("dve", lambda e, f=f, oc=oc, tt=tt, mt=mt: e.scalar_tensor_tensor(
                                mt[:, :], p_m[f][:, :],
                                smalls[maas[f]][:, oc:oc + 1],
                                dxb[:, oc * TC + tt * 512: oc * TC + (tt + 1) * 512],
                                OP.add, OP.mult),
                                reads=[("p_m", f), ("small", maas[f]),
                                       ("dxb", oc)],
                                writes=[("mtmp", (f + tt) % 2)])
                            sch.op("pool", lambda e, oc=oc, tt=tt, mt=mt, xot=xot: e.tensor_tensor(
                                xot[:, tt * 512:(tt + 1) * 512], mt[:, :],
                                xin[:, oc, 1 + tt * 512: 1 + (tt + 1) * 512],
                                OP.add),
                                reads=[("mtmp", (f + tt) % 2), ("xin", oc)],
                                writes=[("xog", f)])
                        sch.op("sp" if f % 2 == 0 else "actdma",
                               lambda e, f=f, oc=oc, xot=xot: e.dma_start(
                                   sc[names[f]].ap()[oc * 128:(oc + 1) * 128, :],
                                   xot[:, :]),
                               reads=[("xog", f)],
                               writes=[("scr", names[f])])
                sch.dma_drain()
                with nc.Block() as blk:
                    sch.emit(blk)

        if 2 in _PHASES:
            with contextlib.ExitStack() as phs:
                wt = [sbt(phs, f"p2_wt{i}", [128, 8 * D], BF16) for i in range(2)]
                xs = [sbt(phs, f"p2_xs{i}", [128, 8 * TC], BF16) for i in range(2)]
                ot = [sbt(phs, f"p2_ot{i}", [128, 1024], BF16) for i in range(4)]
                of = [sbt(phs, f"p2_of{i}", [128, 1024], F32) for i in range(2)]
                hdec = sbt(phs, "p2_hdec", [64, TC], BF16)
                dw1 = sbt(phs, "p2_dw1", [128, 8 * 64], BF16)
                dw2 = sbt(phs, "p2_dw2", [64, D], BF16)
                gb = sbt(phs, "p2_gb", [128, D], F32)
                td = sbt(phs, "p2_td", [128, D], F32)
                pp = [phs.enter_context(
                    nc.psum_tensor(f"pp2_{i}", [128, 512], F32)) for i in range(4)]

                sch.op("sp", lambda e: e.dma_start(gb[:, :], inp["gate_b_rep"].ap()),
                       writes=["gb"])
                sch.op("sp", lambda e: e.dma_start(td[:, :], inp["td_rep"].ap()),
                       writes=["td"])
                sch.op("pooldma", lambda e: e.dma_start(dw1[:, :], inp["dec_w1"].ap()),
                       writes=["dw1"])
                sch.op("pooldma", lambda e: e.dma_start(dw2[:, :], inp["dec_w2"].ap()),
                       writes=["dw2"])

                def load_xs(key, xb):
                    for j in range(8):
                        sch.op("pooldma", lambda e, j=j, key=key, xb=xb: e.dma_start(
                            xs[xb][:, j * TC:(j + 1) * TC],
                            sc[key].ap()[128 * j:128 * (j + 1), :]),
                            reads=[("scr", key)], writes=[("xs", xb, j)])

                def load_w(wname, buf):
                    sch.op("pooldma", lambda e, wname=wname, buf=buf: e.dma_start(
                        wt[buf][:, :], inp[wname].ap()), writes=[("wt", buf)])

                for wname, skey, okey in [("Wr", "xr", "r"), ("Wk", "xk", "k")]:
                    buf = 0 if wname == "Wr" else 1
                    xb = buf
                    load_w(wname, buf)
                    load_xs(skey, xb)
                    for oc in range(8):
                        for tt in range(4):
                            bi = (oc * 4 + tt) % 4
                            pb = pp[bi]
                            for ic in range(8):
                                sch.op("pe", lambda e, oc=oc, tt=tt, ic=ic, pb=pb, buf=buf, xb=xb: e.matmul(
                                    pb[:, :],
                                    wt[buf][:, ic * D + oc * 128: ic * D + (oc + 1) * 128],
                                    xs[xb][:, ic * TC + tt * 512: ic * TC + (tt + 1) * 512],
                                    start=(ic == 0), stop=(ic == 7)),
                                    reads=[("wt", buf), ("xs", xb, ic)],
                                    writes=[("pp", bi)])
                            obuf = ot[bi]
                            sch.op("act", lambda e, pb=pb, obuf=obuf: e.activation(
                                obuf[:, 0:512], pb[:, :], AF.Copy),
                                reads=[("pp", bi)],
                                writes=[("ot", bi)])
                            if okey == "r":
                                sch.op("sp", lambda e, tt=tt, oc=oc, obuf=obuf: e.dma_start(
                                    sc["r"].ap()[4 * tt:4 * tt + 4, :, oc * C:(oc + 1) * C]
                                    .rearrange("n p c -> p n c"),
                                    obuf[:, 0:512].rearrange("p (n c) -> p n c", c=C)),
                                    reads=[("ot", bi)],
                                    writes=[("scr", "r")])
                            else:
                                sch.op("sp", lambda e, tt=tt, oc=oc, obuf=obuf: e.dma_start(
                                    sc["kf"].ap()[4 * tt:4 * tt + 4, :, oc * C:(oc + 1) * C]
                                    .rearrange("n p c -> p n c"),
                                    obuf[:, 0:512].rearrange("p (n c) -> p n c", c=C)),
                                    reads=[("ot", bi)],
                                    writes=[("scr", "kf")])
                                sch.op("actdma", lambda e, tt=tt, oc=oc, obuf=obuf: e.dma_start(
                                    sc["k"].ap()[4 * tt:4 * tt + 4, oc * 128:(oc + 1) * 128, :]
                                    .rearrange("n p c -> p n c"),
                                    obuf[:, 0:512].rearrange("p (n c) -> p n c", c=C)),
                                    reads=[("ot", bi)],
                                    writes=[("scr", "k")])

                for wi, (wname, skey, okey) in enumerate([
                        ("Wv", "xv", "v"), ("Wg", "xg", "g"),
                        ("gate_w", "dx", "sig")]):
                    buf = wi % 2
                    xb = wi % 2
                    load_w(wname, buf)
                    load_xs(skey, xb)
                    for tc in range(NCH):
                        pbase = (tc % 2) * 2
                        for half in range(2):
                            pb = pp[pbase + half]
                            for ic in range(8):
                                sch.op("pe", lambda e, tc=tc, half=half, ic=ic, pb=pb, buf=buf, xb=xb: e.matmul(
                                    pb[:, :],
                                    xs[xb][:, ic * TC + tc * 128: ic * TC + (tc + 1) * 128],
                                    wt[buf][:, ic * D + half * 512: ic * D + half * 512 + 512],
                                    start=(ic == 0), stop=(ic == 7)),
                                    reads=[("wt", buf), ("xs", xb, ic)],
                                    writes=[("pp", pbase + half)])
                        if okey == "v":
                            obuf = ot[tc % 2]
                            for half in range(2):
                                sch.op("act", lambda e, half=half, obuf=obuf, pbase=pbase: e.activation(
                                    obuf[:, half * 512:half * 512 + 512],
                                    pp[pbase + half][:, :], AF.Copy),
                                    reads=[("pp", pbase + half)], writes=[("ot", tc % 2)])
                            sch.op("sp", lambda e, tc=tc, obuf=obuf: e.dma_start(
                                sc["v"].ap()[tc * 128:(tc + 1) * 128, :], obuf[:, :]),
                                reads=[("ot", tc % 2)], writes=[("scr", "v")])
                        elif okey == "g":
                            obuf = of[tc % 2]
                            for half in range(2):
                                sch.op("act", lambda e, half=half, obuf=obuf, pbase=pbase: e.activation(
                                    obuf[:, half * 512:half * 512 + 512],
                                    pp[pbase + half][:, :], AF.Silu),
                                    reads=[("pp", pbase + half)], writes=[("of", tc % 2)])
                            sch.op("sp", lambda e, tc=tc, obuf=obuf: e.dma_start(
                                sc["g"].ap()[tc * 128:(tc + 1) * 128, :], obuf[:, :]),
                                reads=[("of", tc % 2)], writes=[("scr", "g")])
                        else:
                            obuf = of[tc % 2]
                            for half in range(2):
                                sch.op("dve", lambda e, half=half, obuf=obuf, pbase=pbase: e.tensor_tensor(
                                    obuf[:, half * 512:half * 512 + 512],
                                    pp[pbase + half][:, :],
                                    gb[:, half * 512:half * 512 + 512], OP.add),
                                    reads=[("pp", pbase + half), "gb"],
                                    writes=[("of", tc % 2)])
                            sch.op("act", lambda e, obuf=obuf: e.activation(
                                obuf[:, :], obuf[:, :], AF.Sigmoid),
                                reads=[("of", tc % 2)], writes=[("of", tc % 2)])
                            sch.op("sp", lambda e, tc=tc, obuf=obuf: e.dma_start(
                                sc["sig"].ap()[tc * 128:(tc + 1) * 128, :], obuf[:, :]),
                                reads=[("of", tc % 2)], writes=[("scr", "sig")])

                load_xs("xw", 1)
                for tt in range(4):
                    pb = pp[0]
                    for ic in range(8):
                        sch.op("pe", lambda e, tt=tt, ic=ic, pb=pb: e.matmul(
                            pb[0:64, :], dw1[:, ic * 64:(ic + 1) * 64],
                            xs[1][:, ic * TC + tt * 512: ic * TC + (tt + 1) * 512],
                            start=(ic == 0), stop=(ic == 7)),
                            reads=["dw1", ("xs", 1, ic)], writes=[("pp", 0)])
                    sch.op("act", lambda e, tt=tt, pb=pb: e.activation(
                        hdec[:, tt * 512:(tt + 1) * 512], pb[0:64, :], AF.Tanh),
                        reads=[("pp", 0)], writes=["hdec"])
                for tc in range(NCH):
                    pbase = (tc % 2) * 2
                    for half in range(2):
                        pb = pp[pbase + half]
                        sch.op("pe", lambda e, tc=tc, half=half, pb=pb: e.matmul(
                            pb[:, :], hdec[:, tc * 128:(tc + 1) * 128],
                            dw2[:, half * 512:half * 512 + 512],
                            start=True, stop=True),
                            reads=["hdec", "dw2"], writes=[("pp", pbase + half)])
                    obuf = of[tc % 2]
                    for half in range(2):
                        sch.op("dve", lambda e, half=half, obuf=obuf, pbase=pbase: e.tensor_tensor(
                            obuf[:, half * 512:half * 512 + 512], pp[pbase + half][:, :],
                            td[:, half * 512:half * 512 + 512], OP.add),
                            reads=[("pp", pbase + half), "td"], writes=[("of", tc % 2)])
                    sch.op("act", lambda e, obuf=obuf: e.activation(
                        obuf[:, :], obuf[:, :], AF.Exp),
                        reads=[("of", tc % 2)], writes=[("of", tc % 2)])
                    sch.op("sp", lambda e, tc=tc, obuf=obuf: e.dma_start(
                        sc["p"].ap()[tc * 128:(tc + 1) * 128, :], obuf[:, :]),
                        reads=[("of", tc % 2)], writes=[("scr", "p")])
                sch.dma_drain()
                with nc.Block() as blk:
                    sch.emit(blk)

        with contextlib.ExitStack() as mid:
            ybuf = None
            if any(p in _PHASES for p in (3, 4, 5)):
                ybuf = sbt(mid, "ybuf", [128, NCH * D], F32)

            if 3 in _PHASES:
                p3 = contextlib.ExitStack()
                mid.enter_context(p3)
                swp = contextlib.ExitStack()
                r16 = [sbt(p3, f"p3_r{i}", [128, D], BF16) for i in range(3)]
                ptok = [sbt(p3, f"p3_pt{i}", [128, D], F32) for i in range(3)]
                Stmp = sbt(p3, "p3_Stmp", [128, 512], F32)
                DnAll = sbt(p3, "p3_DnAll", [128, 8 * NCH], F32)
                BsAll = sbt(p3, "p3_BsAll", [128, 8 * NCH], F32)
                E1u = [sbt(p3, f"p3_E1u{i}", [128, D], BF16) for i in range(2)]
                E3t = [sbt(p3, f"p3_E3t{i}", [128, D], BF16) for i in range(2)]
                Rt = [sbt(p3, f"p3_Rt{i}", [128, D], BF16) for i in range(2)]
                R2 = [sbt(p3, f"p3_R2{i}", [128, D], BF16) for i in range(2)]
                KtP = [sbt(p3, f"p3_KtP{i}", [128, D], BF16) for i in range(4)]
                R2P = [sbt(p3, f"p3_R2P{i}", [128, D], BF16) for i in range(4)]
                k16 = [sbt(swp, f"p3_k{i}", [128, D], BF16) for i in range(3)]
                ktok = [sbt(swp, f"p3_kt{i}", [128, D], BF16) for i in range(3)]
                vtok = [sbt(swp, f"p3_vt{i}", [128, D], BF16) for i in range(3)]
                St = sbt(swp, "p3_S", [128, 512], F32)
                S16 = sbt(swp, "p3_S16", [128, 512], BF16)
                Dn = sbt(swp, "p3_Dn", [128, 8], F32)
                auxsb = [sbt(swp, f"p3_aux{i}", [128, 32], F32) for i in range(2)]
                eW8 = [sbt(swp, f"p3_eW8{i}", [128, 8], F32) for i in range(2)]
                emid = [sbt(swp, f"p3_emid{i}", [128, 8], BF16) for i in range(2)]
                E1c = [sbt(swp, f"p3_E1c{i}", [128, D], BF16) for i in range(2)]
                Kt = [sbt(swp, f"p3_Kt{i}", [128, D], BF16) for i in range(2)]
                Kh = [sbt(swp, f"p3_Kh{i}", [128, D], BF16) for i in range(2)]
                colmask = sbt(swp, "p3_colmask", [128, 2048], BF16)
                sch.op("sp", lambda e: e.dma_start(
                    colmask[:, :], inp["colmask"].ap()),
                    writes=[("small", "colmask")])
                KhP = [sbt(swp, f"p3_KhP{i}", [128, D], BF16) for i in range(4)]
                Pt = [sbt(swp, f"p3_P{i}", [128, 2 * 1024], BF16) for i in range(2)]
                pA = p3.enter_context(nc.psum_tensor("pp3_A", [128, 1024], F32))
                pB = p3.enter_context(nc.psum_tensor("pp3_B", [128, 1024], F32))
                pP = p3.enter_context(nc.psum_tensor("pp3_P", [128, 512], F32))
                pY = p3.enter_context(nc.psum_tensor("pp3_Y", [128, 1024], F32))
                pSt = p3.enter_context(nc.psum_tensor("pp3_St", [128, 512], F32))

                def aux3(b, colidx):
                    return auxsb[b][:, :].rearrange("p (j c) -> p j c", c=4)[:, :, colidx]

                def sweep(fwd):
                    tri_c = smalls["tri_cc"] if fwd else smalls["tri_cb"]
                    tri_ck = "tri_cc" if fwd else "tri_cb"
                    tri_tok = smalls["tri_cs"] if fwd else smalls["tri_ci"]
                    tri_tk = "tri_cs" if fwd else "tri_ci"
                    mask = smalls["mask_f"] if fwd else smalls["mask_b"]
                    mkey = "mask_f" if fwd else "mask_b"
                    mid_col = 0 if fwd else 2
                    All = DnAll if fwd else BsAll
                    akey = "DnAll" if fwd else "BsAll"
                    chunks = (range(_P3_CHUNKS) if fwd
                              else range(_P3_CHUNKS - 1, -1, -1))
                    L = _P3_LEVEL

                    def op3(lvl, *a, **k):
                        if L >= lvl:
                            return sch.op(*a, **k)
                    op3(0, "dve", lambda e: e.memset(St[:, :], 0.0), writes=["St"])
                    op3(0, "dve", lambda e: e.memset(Dn[:, :], 0.0), writes=["Dn"])
                    for n in chunks:
                        b = n % 2
                        bl = n % 3
                        op3(0, "sp", lambda e, n=n, bl=bl: e.dma_start(
                            r16[bl][:, :], sc["r"].ap()[n]),
                            reads=[("scr", "r")], writes=[("r16", bl)])
                        op3(0, "sp", lambda e, n=n, bl=bl: e.dma_start(
                            k16[bl][:, :], sc["kf"].ap()[n]),
                            reads=[("scr", "kf")], writes=[("k16", bl)])
                        op3(0, "actdma", lambda e, n=n, bl=bl: e.dma_start(
                            ktok[bl][:, :], sc["k"].ap()[n], transpose=True),
                            reads=[("scr", "k")], writes=[("ktok", bl)])
                        op3(0, "sp", lambda e, n=n, bl=bl: e.dma_start(
                            vtok[bl][:, :], sc["v"].ap()[n * C:(n + 1) * C, :]),
                            reads=[("scr", "v")], writes=[("vtok", bl)])
                        op3(0, "sp", lambda e, n=n, bl=bl: e.dma_start(
                            ptok[bl][:, :], sc["p"].ap()[n * C:(n + 1) * C, :]),
                            reads=[("scr", "p")], writes=[("ptok", bl)])
                        for j in range(8):
                            op3(1, "pe", lambda e, j=j, bl=bl, tri_c=tri_c: e.matmul(
                                pA[:, j * 128:(j + 1) * 128],
                                ptok[bl][:, j * 128:(j + 1) * 128], tri_c[:, :],
                                start=(j % 4 == 0), stop=(j % 4 == 3)),
                                reads=[("ptok", bl), ("small", tri_ck)], writes=["pA"])
                        for j in range(8):
                            op3(1, "pe", lambda e, j=j, bl=bl: e.matmul(
                                pY[:, j * 4:(j + 1) * 4],
                                ptok[bl][:, j * 128:(j + 1) * 128],
                                smalls["aux_cols"][:, :],
                                start=(j == 0), stop=(j == 7)),
                                reads=[("ptok", bl), ("small", "aux_cols")],
                                writes=["pY"])
                        for half in range(2):
                            op3(1, "pe", lambda e, half=half, bl=bl, tri_tok=tri_tok: e.matmul(
                                pB[:, half * 512:half * 512 + 512], tri_tok[:, :],
                                ptok[bl][:, half * 512:half * 512 + 512],
                                start=True, stop=True),
                                reads=[("ptok", bl), ("small", tri_tk)], writes=["pB"])
                        op3(2, "dve", lambda e, b=b: e.tensor_copy(
                            auxsb[b][:, :], pY[:, 0:32]),
                            reads=["pY"], writes=[("aux", b)])
                        op3(2, "act", lambda e, mid_col=mid_col, b=b: e.activation(
                            emid[b][:, :], aux3(b, mid_col), AF.Exp),
                            reads=[("aux", b)], writes=[("emid", b)])
                        op3(2, "act", lambda e, b=b: e.activation(
                            eW8[b][:, :], aux3(b, 1), AF.Exp),
                            reads=[("aux", b)], writes=[("eW8", b)])
                        op3(2, "pool", lambda e, n=n, All=All: e.tensor_copy(
                            All[:, n * 8:(n + 1) * 8], Dn[:, :]),
                            reads=["Dn"], writes=[akey])
                        op3(2, "pool", lambda e, b=b: e.tensor_tensor(
                            Dn[:, :], Dn[:, :], aux3(b, 1), OP.add),
                            reads=["Dn", ("aux", b), akey], writes=["Dn"])
                        op3(3, "act", lambda e, b=b: e.activation(
                            E1c[b][:, :], pA[:, :], AF.Exp),
                            reads=["pA"], writes=[("E1c", b)])
                        op3(3, "act", lambda e, b=b: e.activation(
                            E1u[b][:, :], pA[:, :], AF.Exp, scale=ccol("neg1")),
                            reads=["pA", cread("neg1")], writes=[("E1u", b)])
                        op3(4, "dve", lambda e, b=b, bl=bl: e.tensor_tensor(
                            Kt[b][:, :], k16[bl][:, :], E1u[b][:, :], OP.mult),
                            reads=[("k16", bl), ("E1u", b)], writes=[("Kt", b)])
                        for par in range(2):
                            op3(5, "act", lambda e, par=par, b=b: e.activation(
                                KtP[b * 2 + par][:, :], Kt[b][:, :], AF.Identity,
                                scale=smalls["pmask"][:, par:par + 1]),
                                reads=[("Kt", b), ("small", "pmask")],
                                writes=[("KtP", b * 2 + par)])
                        op3(3, "act", lambda e, b=b: e.activation(
                            E3t[b][:, :], pB[:, :], AF.Exp),
                            reads=["pB"], writes=[("E3t", b)])
                        op3(4, "dve", lambda e, b=b, bl=bl: e.tensor_tensor(
                            Rt[b][:, :], r16[bl][:, :], E1c[b][:, :], OP.mult),
                            reads=[("r16", bl), ("E1c", b)], writes=[("Rt", b)])
                        op3(4, "dve", lambda e, b=b: e.tensor_tensor(
                            E1u[b][:, :].rearrange("p (a b) -> p a b", b=128),
                            E1c[b][:, :].rearrange("p (a b) -> p a b", b=128),
                            emid[b][:, :].unsqueeze(2).broadcast_to([128, 8, 128]),
                            OP.mult),
                            reads=[("E1c", b), ("emid", b)], writes=[("E1u", b)])
                        op3(4, "dve", lambda e, b=b, bl=bl: e.tensor_tensor(
                            R2[b][:, :], r16[bl][:, :], E1u[b][:, :], OP.mult),
                            reads=[("r16", bl), ("E1u", b)], writes=[("R2", b)])
                        for par in range(2):
                            op3(6, "dve", lambda e, par=par, b=b: e.tensor_scalar(
                                R2P[b * 2 + par][:, :], R2[b][:, :],
                                smalls["pmask"][:, par:par + 1], None, OP.mult),
                                reads=[("R2", b), ("small", "pmask")],
                                writes=[("R2P", b * 2 + par)])
                        op3(4, "dve", lambda e, b=b, bl=bl: e.tensor_tensor(
                            Kh[b][:, :], ktok[bl][:, :], E3t[b][:, :], OP.mult),
                            reads=[("ktok", bl), ("E3t", b)], writes=[("Kh", b)])
                        op3(7, "dve", lambda e, b=b: e.tensor_tensor(
                            KhP[b * 2][:, :], Kh[b][:, :],
                            colmask[:, 0:1024], OP.mult),
                            reads=[("Kh", b), ("small", "colmask")],
                            writes=[("KhP", b * 2)])
                        op3(7, "pool", lambda e, b=b: e.tensor_tensor(
                            KhP[b * 2 + 1][:, :], Kh[b][:, :],
                            colmask[:, 1024:2048], OP.mult),
                            reads=[("Kh", b), ("small", "colmask")],
                            writes=[("KhP", b * 2 + 1)])
                        op3(4, "act", lambda e: e.activation(
                            S16[:, :], St[:, :], AF.Copy),
                            reads=["St"], writes=["S16"])
                        for wave in range(2):
                            for hw_ in range(2):
                                for hh4 in range(4):
                                    hh = hw_ * 4 + hh4
                                    h = wave * 8 + hh
                                    par = h % 2
                                    jb = h // 2
                                    op3(5, "pe", lambda e, hh4=hh4, par=par, jb=jb, b=b: e.matmul(
                                        pP[:, hh4 * 128:(hh4 + 1) * 128],
                                        KtP[b * 2 + par][:, jb * 128:(jb + 1) * 128],
                                        Rt[b][:, jb * 128:(jb + 1) * 128],
                                        start=(hh4 == 0), stop=(hh4 == 3)),
                                        reads=[("KtP", b * 2 + par), ("Rt", b)],
                                        writes=["pP"])
                                op3(6, "dve", lambda e, wave=wave, hw_=hw_, mask=mask, b=b: e.tensor_tensor(
                                    Pt[b][:, wave * 1024 + hw_ * 512:
                                          wave * 1024 + hw_ * 512 + 512]
                                    .rearrange("p (a b) -> p a b", b=128),
                                    pP[:, :].rearrange("p (a b) -> p a b", b=128),
                                    mask[:, :].unsqueeze(1).broadcast_to([128, 4, 128]),
                                    OP.mult),
                                    reads=["pP", ("small", mkey)],
                                    writes=[("Pt", b, wave)])
                        for h in range(H):
                            wave, hh = h // 8, h % 8
                            jb = h // 2
                            op3(6, "pe", lambda e, h=h, wave=wave, hh=hh, b=b, bl=bl: e.matmul(
                                pY[:, h * 64:(h + 1) * 64],
                                Pt[b][:, wave * 1024 + hh * 128: wave * 1024 + (hh + 1) * 128],
                                vtok[bl][:, h * 64:(h + 1) * 64],
                                start=(h == 0 or h == 8), stop=False),
                                reads=[("Pt", b, wave), ("vtok", bl)], writes=["pY"])
                            op3(6, "pe", lambda e, h=h, jb=jb, b=b: e.matmul(
                                pY[:, h * 64:(h + 1) * 64],
                                R2P[b * 2 + h % 2][:, jb * 128:(jb + 1) * 128],
                                S16[:, jb * 64:(jb + 1) * 64],
                                start=False, stop=(h == 7 or h == 15)),
                                reads=[("R2P", b * 2 + h % 2), "S16"], writes=["pY"])
                        if fwd:
                            op3(6, "act", lambda e, n=n: e.activation(
                                ybuf[:, n * D:(n + 1) * D], pY[:, :], AF.Copy),
                                reads=["pY"], writes=[("y", n)])
                        else:
                            op3(6, "dve", lambda e, n=n: e.tensor_tensor(
                                ybuf[:, n * D:(n + 1) * D],
                                ybuf[:, n * D:(n + 1) * D], pY[:, :], OP.add),
                                reads=["pY", ("y", n)], writes=[("y", n)])
                        for h in range(H):
                            par = h % 2
                            jb = h // 2
                            op3(7, "pe", lambda e, h=h, b=b, bl=bl, par=par, jb=jb: e.matmul(
                                pSt[:, jb * 64:(jb + 1) * 64],
                                KhP[b * 2 + par][:, jb * 128:(jb + 1) * 128],
                                vtok[bl][:, h * 64:(h + 1) * 64],
                                start=(h == 0), stop=(h == 15)),
                                reads=[("KhP", b * 2 + par), ("vtok", bl)],
                                writes=["pSt"])
                        op3(8, "dve", lambda e, b=b: e.tensor_tensor(
                            Stmp[:, 0:256].rearrange("p (a b) -> p a b", b=64),
                            St[:, 0:256].rearrange("p (a b) -> p a b", b=64),
                            eW8[b][:, 0:4].unsqueeze(2).broadcast_to([128, 4, 64]),
                            OP.mult),
                            reads=["St", ("eW8", b), "S16"], writes=["Stmp"])
                        op3(8, "pool", lambda e, b=b: e.tensor_tensor(
                            Stmp[:, 256:512].rearrange("p (a b) -> p a b", b=64),
                            St[:, 256:512].rearrange("p (a b) -> p a b", b=64),
                            eW8[b][:, 4:8].unsqueeze(2).broadcast_to([128, 4, 64]),
                            OP.mult),
                            reads=["St", ("eW8", b), "S16"], writes=["Stmp"])
                        op3(8, "dve", lambda e: e.tensor_tensor(
                            St[:, :], Stmp[:, :], pSt[:, :], OP.add),
                            reads=["Stmp", "pSt"], writes=["St"])
                    rows0 = 0 if fwd else 1024
                    op3(9, "sp", lambda e, rows0=rows0: e.dma_start(
                        sc["ccin"].ap()[rows0:rows0 + 1024, :]
                        .rearrange("(j p) v -> p j v", p=128),
                        St[:, :].rearrange("p (j v) -> p j v", v=64)),
                        reads=["St"], writes=["ccin"])
                    if not fwd:
                        op3(9, "sp", lambda e: e.dma_start(
                            sc["ccin"].ap()[2048:2064, :]
                            .rearrange("t v -> (t v)")
                            .rearrange("(jj p) -> p jj", p=128),
                            Dn[:, :], allow_slow_non_contiguous=True),
                            reads=["Dn"], writes=["ccin"])

                sweep(True)
                sch.dma_drain()
                with nc.Block() as blk:
                    sch.emit(blk)
                sweep(False)
                sch.dma_drain()
                with nc.Block() as blk:
                    sch.emit(blk)
                swp.close()

                if 4 in _PHASES:
                    Sin = sbt(p3, "p3_Sin", [128, 512], F32)
                    Tin = sbt(p3, "p3_Tin", [128, 512], F32)
                    Sg = [sbt(p3, f"p3_Sg{i}", [128, 512], F32) for i in range(2)]
                    Wl = sbt(p3, "p3_Wl", [128, 8 * 8], F32)
                    runv = sbt(p3, "p3_run", [128, 8], F32)
                    eDcC = sbt(p3, "p3_eDcC", [128, 8], F32)
                    eDc = [sbt(p3, f"p3_eDc{i}", [128, 8], F32) for i in range(2)]
                    eDc2 = [sbt(p3, f"p3_eDc2{i}", [128, 8], F32) for i in range(2)]
                    Sn16 = [sbt(p3, f"p3_Sn16{i}", [128, 512], BF16) for i in range(2)]
                    Tn16 = [sbt(p3, f"p3_Tn16{i}", [128, 512], BF16) for i in range(2)]
                    sigt = [sbt(p3, f"p5_sig{i}", [128, D], F32) for i in range(2)]
                    gt = [sbt(p3, f"p5_g{i}", [128, D], F32) for i in range(2)]
                    lnw = sbt(p3, "p5_lnw", [128, D], F32)
                    lnb = sbt(p3, "p5_lnb", [128, D], F32)
                    tYs = [sbt(p3, f"p5_tY{i}", [128, D], F32) for i in range(2)]
                    tSqs = [sbt(p3, f"p5_tSq{i}", [128, D], F32) for i in range(2)]
                    st16s = [sbt(p3, f"p5_st{i}", [128, 64], F32) for i in range(2)]
                    zt = [sbt(p3, f"p5_z{i}", [128, D], F32) for i in range(2)]
                    lwg = [sbt(p3, f"p5_lwg{i}", [128, D], F32) for i in range(2)]
                    lbg = [sbt(p3, f"p5_lbg{i}", [128, D], F32) for i in range(2)]
                    if 5 in _PHASES:
                        sch.op("actdma", lambda e: e.dma_start(
                            lnw[:, :], inp["ln_w_rep"].ap()), writes=["lnw"])
                        sch.op("actdma", lambda e: e.dma_start(
                            lnb[:, :], inp["ln_b_rep"].ap()), writes=["lnb"])

                    sch.op("pool", lambda e: e.collective_compute(
                        "AllGather", OP.bypass,
                        replica_groups=[list(range(N_CORES))],
                        ins=[sc["ccin"].ap().opt()],
                        outs=[sc["ccout"].ap().opt()]),
                        reads=["ccin"], writes=["ccout"], force_inc=True)
                    for cp in range(N_CORES):
                        sch.op("sp", lambda e, cp=cp: e.dma_start(
                            Wl[:, cp * 8:(cp + 1) * 8],
                            sc["ccout"].ap()[cp * 2064 + 2048: cp * 2064 + 2064, :]
                            .rearrange("t v -> (t v)")
                            .rearrange("(jj p) -> p jj", p=128),
                            allow_slow_non_contiguous=True),
                            reads=["ccout"], writes=["Wl"])

                    def combine(out_t, maskname, order):
                        okey = ("cmb", maskname)
                        sch.op("dve", lambda e, out_t=out_t: e.memset(out_t[:, :], 0.0),
                               writes=[okey])
                        sch.op("dve", lambda e: e.memset(runv[:, :], 0.0),
                               writes=["runv"])
                        base = 0 if maskname == "cmask_f" else 1024
                        for cp in order:
                            bb = cp % 2
                            sch.op("sp", lambda e, cp=cp, bb=bb, base=base: e.dma_start(
                                Sg[bb][:, :].rearrange("p (j v) -> p j v", v=64),
                                sc["ccout"].ap()[cp * 2064 + base: cp * 2064 + base + 1024, :]
                                .rearrange("(j p) v -> p j v", p=128)),
                                reads=["ccout"], writes=[("Sg", bb)])
                            sch.op("act", lambda e: e.activation(
                                eDcC[:, :], runv[:, :], AF.Exp),
                                reads=["runv"], writes=["eDcC"])
                            sch.op("dve", lambda e, cp=cp, maskname=maskname: e.tensor_scalar(
                                eDcC[:, :], eDcC[:, :],
                                smalls[maskname][:, cp:cp + 1], None, OP.mult),
                                reads=["eDcC", ("small", maskname)], writes=["eDcC"])
                            sch.op("dve", lambda e, bb=bb: e.tensor_tensor(
                                Stmp[:, :].rearrange("p (a b) -> p a b", b=64),
                                Sg[bb][:, :].rearrange("p (a b) -> p a b", b=64),
                                eDcC[:, :].unsqueeze(2).broadcast_to([128, 8, 64]),
                                OP.mult), reads=[("Sg", bb), "eDcC"], writes=["Stmp"])
                            sch.op("dve", lambda e, out_t=out_t, okey=okey: e.tensor_tensor(
                                out_t[:, :], out_t[:, :], Stmp[:, :], OP.add),
                                reads=["Stmp", okey], writes=[okey])
                            sch.op("dve", lambda e, cp=cp, maskname=maskname: e.scalar_tensor_tensor(
                                runv[:, :], Wl[:, cp * 8:(cp + 1) * 8],
                                smalls[maskname][:, cp:cp + 1], runv[:, :],
                                OP.mult, OP.add),
                                reads=["Wl", ("small", maskname), "runv"],
                                writes=["runv"])

                    if _P4_LEVEL >= 2:
                        combine(Sin, "cmask_f", list(range(N_CORES - 1, -1, -1)))
                        combine(Tin, "cmask_b", list(range(N_CORES)))
                    sch.dma_drain()
                    with nc.Block() as blk:
                        sch.emit(blk)

                    L4 = _P4_LEVEL - 2

                    def op4(lvl, *a, **k):
                        if L4 >= lvl:
                            return sch.op(*a, **k)
                    for n in (range(_P3_CHUNKS) if _P4_LEVEL >= 3 else []):
                        b = n % 2
                        bl = n % 3
                        op4(1, "sp", lambda e, n=n, bl=bl: e.dma_start(
                            r16[bl][:, :], sc["r"].ap()[n]),
                            reads=[("scr", "r")], writes=[("r16", bl)])
                        op4(1, "sp", lambda e, n=n, bl=bl: e.dma_start(
                            ptok[bl][:, :], sc["p"].ap()[n * C:(n + 1) * C, :]),
                            reads=[("scr", "p")], writes=[("ptok", bl)])
                        for j in range(8):
                            op4(2, "pe", lambda e, j=j, bl=bl: e.matmul(
                                pA[:, j * 128:(j + 1) * 128],
                                ptok[bl][:, j * 128:(j + 1) * 128],
                                smalls["tri_ci"][:, :],
                                start=(j % 4 == 0), stop=(j % 4 == 3)),
                                reads=[("ptok", bl), ("small", "tri_ci")],
                                writes=["pA"])
                        for j in range(8):
                            op4(2, "pe", lambda e, j=j, bl=bl: e.matmul(
                                pB[:, j * 128:(j + 1) * 128],
                                ptok[bl][:, j * 128:(j + 1) * 128],
                                smalls["tri_cs"][:, :],
                                start=(j % 4 == 0), stop=(j % 4 == 3)),
                                reads=[("ptok", bl), ("small", "tri_cs")],
                                writes=["pB"])
                        op4(2, "act", lambda e, b=b: e.activation(
                            E1u[b][:, :], pA[:, :], AF.Exp),
                            reads=["pA"], writes=[("E1u", b)])
                        op4(2, "act", lambda e, b=b: e.activation(
                            E3t[b][:, :], pB[:, :], AF.Exp),
                            reads=["pB"], writes=[("E3t", b)])
                        op4(3, "dve", lambda e, b=b, bl=bl: e.tensor_tensor(
                            R2[b][:, :], r16[bl][:, :], E1u[b][:, :], OP.mult),
                            reads=[("r16", bl), ("E1u", b)], writes=[("R2", b)])
                        op4(3, "dve", lambda e, b=b, bl=bl: e.tensor_tensor(
                            Rt[b][:, :], r16[bl][:, :], E3t[b][:, :], OP.mult),
                            reads=[("r16", bl), ("E3t", b)], writes=[("Rt", b)])
                        for par in range(2):
                            op4(3, "dve", lambda e, par=par, b=b: e.tensor_scalar(
                                R2P[b * 2 + par][:, :], R2[b][:, :],
                                smalls["pmask"][:, par:par + 1], None, OP.mult),
                                reads=[("R2", b), ("small", "pmask")],
                                writes=[("R2P", b * 2 + par)])
                            op4(3, "act", lambda e, par=par, b=b: e.activation(
                                KtP[b * 2 + par][:, :], Rt[b][:, :], AF.Identity,
                                scale=smalls["pmask"][:, par:par + 1]),
                                reads=[("Rt", b), ("small", "pmask")],
                                writes=[("KtP", b * 2 + par)])
                        op4(4, "act", lambda e, n=n, b=b: e.activation(
                            eDc[b][:, :], DnAll[:, n * 8:(n + 1) * 8], AF.Exp),
                            reads=["DnAll"], writes=[("eDc", b)])
                        op4(4, "act", lambda e, n=n, b=b: e.activation(
                            eDc2[b][:, :], BsAll[:, n * 8:(n + 1) * 8], AF.Exp),
                            reads=["BsAll"], writes=[("eDc2", b)])
                        op4(4, "dve", lambda e, b=b: e.tensor_tensor(
                            Sn16[b][:, :].rearrange("p (a b) -> p a b", b=64),
                            Sin[:, :].rearrange("p (a b) -> p a b", b=64),
                            eDc[b][:, :].unsqueeze(2).broadcast_to([128, 8, 64]),
                            OP.mult),
                            reads=[("cmb", "cmask_f"), ("eDc", b)],
                            writes=[("Sn16", b)])
                        op4(4, "dve", lambda e, b=b: e.tensor_tensor(
                            Tn16[b][:, :].rearrange("p (a b) -> p a b", b=64),
                            Tin[:, :].rearrange("p (a b) -> p a b", b=64),
                            eDc2[b][:, :].unsqueeze(2).broadcast_to([128, 8, 64]),
                            OP.mult),
                            reads=[("cmb", "cmask_b"), ("eDc2", b)],
                            writes=[("Tn16", b)])
                        for h in range(H):
                            jb = h // 2
                            op4(5, "pe", lambda e, h=h, jb=jb, b=b: e.matmul(
                                pY[:, h * 64:(h + 1) * 64],
                                R2P[b * 2 + h % 2][:, jb * 128:(jb + 1) * 128],
                                Sn16[b][:, jb * 64:(jb + 1) * 64],
                                start=(h == 0 or h == 8), stop=False),
                                reads=[("R2P", b * 2 + h % 2), ("Sn16", b)],
                                writes=["pY"])
                            op4(5, "pe", lambda e, h=h, jb=jb, b=b: e.matmul(
                                pY[:, h * 64:(h + 1) * 64],
                                KtP[b * 2 + h % 2][:, jb * 128:(jb + 1) * 128],
                                Tn16[b][:, jb * 64:(jb + 1) * 64],
                                start=False, stop=(h == 7 or h == 15)),
                                reads=[("KtP", b * 2 + h % 2), ("Tn16", b)],
                                writes=["pY"])
                        op4(6, "act", lambda e, b=b: e.activation(
                            E1u[b][:, :], pY[:, :], AF.Identity, bias=ccol("zero")),
                            reads=["pY", cread("zero")], writes=[("E1u", b)])
                        op4(6, "dve", lambda e, n=n, b=b: e.tensor_tensor(
                            ybuf[:, n * D:(n + 1) * D],
                            ybuf[:, n * D:(n + 1) * D], E1u[b][:, :], OP.add),
                            reads=[("E1u", b), ("y", n)], writes=[("y", n)])
                        if 5 not in _PHASES or L4 < 6:
                            continue
                        tY, tSq, st16 = tYs[b], tSqs[b], st16s[b]
                        kY, kSq, kSt = ("tY", b), ("tSq", b), ("st", b)
                        sch.op("sp", lambda e, n=n, b=b: e.dma_start(
                            sigt[b][:, :], sc["sig"].ap()[n * C:(n + 1) * C, :]),
                            reads=[("scr", "sig")], writes=[("sigt", b)])
                        sch.op("sp", lambda e, n=n, b=b: e.dma_start(
                            gt[b][:, :], sc["g"].ap()[n * C:(n + 1) * C, :]),
                            reads=[("scr", "g")], writes=[("gt", b)])
                        sch.op("pool", lambda e, b=b: e.tensor_tensor(
                            lwg[b][:, :], lnw[:, :], gt[b][:, :], OP.mult),
                            reads=["lnw", ("gt", b)], writes=[("lwg", b)])
                        sch.op("pool", lambda e, b=b: e.tensor_tensor(
                            lbg[b][:, :], lnb[:, :], gt[b][:, :], OP.mult),
                            reads=["lnb", ("gt", b)], writes=[("lbg", b)])
                        sch.op("dve", lambda e, n=n, b=b, tY=tY: e.tensor_tensor(
                            tY[:, :], ybuf[:, n * D:(n + 1) * D], sigt[b][:, :],
                            OP.mult),
                            reads=[("y", n), ("sigt", b)], writes=[kY])
                        sch.op("act", lambda e, tY=tY, tSq=tSq: e.activation(
                            tSq[:, :], tY[:, :], AF.Square), reads=[kY],
                            writes=[kSq])
                        sch.op("dve", lambda e, tY=tY, st16=st16: e.reduce_sum(
                            st16[:, 0:16], tY[:, :].rearrange("p (g k) -> p g k", k=64),
                            axis=AX), reads=[kY], writes=[kSt])
                        sch.op("dve", lambda e, tSq=tSq, st16=st16: e.reduce_sum(
                            st16[:, 16:32],
                            tSq[:, :].rearrange("p (g k) -> p g k", k=64),
                            axis=AX), reads=[kSq], writes=[kSt])
                        sch.op("dve", lambda e, st16=st16: e.tensor_scalar(
                            st16[:, 32:48], st16[:, 0:16], ccol("i64"), None, OP.mult),
                            reads=[kSt, cread("i64")], writes=[kSt])
                        sch.op("dve", lambda e, st16=st16: e.tensor_tensor(
                            st16[:, 48:64], st16[:, 32:48], st16[:, 32:48], OP.mult),
                            reads=[kSt], writes=[kSt])
                        sch.op("dve", lambda e, st16=st16: e.scalar_tensor_tensor(
                            st16[:, 48:64], st16[:, 16:32], ccol("i64"),
                            st16[:, 48:64], OP.mult, OP.subtract),
                            reads=[kSt, cread("i64")], writes=[kSt])
                        sch.op("act", lambda e, st16=st16: e.activation(
                            st16[:, 48:64], st16[:, 48:64], AF.Ln,
                            bias=ccol("eps_gn")),
                            reads=[kSt, cread("eps_gn")], writes=[kSt])
                        sch.op("act", lambda e, st16=st16: e.activation(
                            st16[:, 48:64], st16[:, 48:64], AF.Exp,
                            scale=ccol("neg_half")),
                            reads=[kSt, cread("neg_half")], writes=[kSt])
                        sch.op("dve", lambda e, tY=tY, st16=st16: e.tensor_tensor(
                            tY[:, :].rearrange("p (g k) -> p g k", k=64),
                            tY[:, :].rearrange("p (g k) -> p g k", k=64),
                            st16[:, 32:48].unsqueeze(2).broadcast_to([128, 16, 64]),
                            OP.subtract), reads=[kSt, kY], writes=[kY])
                        sch.op("dve", lambda e, tY=tY, st16=st16: e.tensor_tensor(
                            tY[:, :].rearrange("p (g k) -> p g k", k=64),
                            tY[:, :].rearrange("p (g k) -> p g k", k=64),
                            st16[:, 48:64].unsqueeze(2).broadcast_to([128, 16, 64]),
                            OP.mult), reads=[kSt, kY], writes=[kY])
                        sch.op("dve", lambda e, b=b, tY=tY: e.tensor_tensor(
                            tY[:, :], tY[:, :], lwg[b][:, :], OP.mult),
                            reads=[kY, ("lwg", b)], writes=[kY])
                        sch.op("dve", lambda e, b=b, tY=tY: e.tensor_tensor(
                            zt[b][:, :], tY[:, :], lbg[b][:, :], OP.add),
                            reads=[kY, ("lbg", b)], writes=[("zt", b)])
                        sch.op("pooldma", lambda e, n=n, b=b: e.dma_start(
                            sc["zb"].ap()[n * C:(n + 1) * C, :], zt[b][:, :]),
                            reads=[("zt", b)], writes=[("scr", "zb")])
                    sch.dma_drain()
                    with nc.Block() as blk:
                        sch.emit(blk)
                p3.close()

            if "sc_y" in _DEBUG_SCRATCH and ybuf is not None:
                for n in range(NCH):
                    sch.op("sp", lambda e, n=n: e.dma_start(
                        sc["y"].ap()[n * C:(n + 1) * C, :],
                        ybuf[:, n * D:(n + 1) * D]),
                        reads=[("y", n)], force_inc=True)
                sch.dma_drain()
                with nc.Block() as blk:
                    sch.emit(blk)

            if 5 in _PHASES:
                with contextlib.ExitStack() as phs:
                    zf = sbt(phs, "p5_zf", [128, 8 * TC], BF16)
                    wo = sbt(phs, "p5_wo", [128, 8 * D], BF16)
                    xtt = [sbt(phs, f"p5_xt{i}", [128, D], F32) for i in range(2)]
                    x2t = [sbt(phs, f"p5_x2{i}", [128, D], F32) for i in range(2)]
                    sqWs = [sbt(phs, f"p5_sqW{i}", [128, D], F32) for i in range(2)]
                    lnsts = [sbt(phs, f"p5_lnst{i}", [128, 4], F32) for i in range(2)]
                    ppw = [phs.enter_context(
                        nc.psum_tensor(f"pp5_{i}", [128, 512], F32)) for i in range(4)]
                    sch.op("pooldma", lambda e: e.dma_start(wo[:, :], inp["Wo"].ap()),
                           writes=["wo"])
                    for jb in range(8):
                        sch.op("actdma", lambda e, jb=jb: e.dma_start(
                            zf[:, jb * TC:(jb + 1) * TC],
                            sc["zb"].ap()[:, jb * 128:(jb + 1) * 128], transpose=True),
                            reads=[("scr", "zb")], writes=[("zf", jb)])
                    for n in range(NCH):
                        b = n % 2
                        sqW, lnst = sqWs[b], lnsts[b]
                        kSq, kLn = ("sqW", b), ("lnst", b)
                        sch.op("sp", lambda e, n=n, b=b: e.dma_start(
                            xtt[b][:, :], inp["xt"].ap()[n * C:(n + 1) * C, :]),
                            writes=[("xtt", b)])
                        for half in range(2):
                            pb = ppw[2 * b + half]
                            for ic in range(8):
                                sch.op("pe", lambda e, n=n, half=half, ic=ic, pb=pb: e.matmul(
                                    pb[:, :],
                                    zf[:, ic * TC + n * 128: ic * TC + (n + 1) * 128],
                                    wo[:, ic * D + half * 512: ic * D + half * 512 + 512],
                                    start=(ic == 0), stop=(ic == 7)),
                                    reads=[("zf", ic), "wo"],
                                    writes=[("ppw", 2 * b + half)])
                            sch.op("dve", lambda e, half=half, b=b, pb=pb: e.tensor_tensor(
                                x2t[b][:, half * 512:half * 512 + 512], pb[:, :],
                                xtt[b][:, half * 512:half * 512 + 512], OP.add),
                                reads=[("ppw", 2 * b + half), ("xtt", b)],
                                writes=[("x2t", b)])
                        sch.op("dve", lambda e, b=b, lnst=lnst: e.reduce_sum(
                            lnst[:, 0:1], x2t[b][:, :], axis=AX),
                            reads=[("x2t", b)], writes=[kLn])
                        sch.op("dve", lambda e, lnst=lnst: e.tensor_scalar(
                            lnst[:, 0:1], lnst[:, 0:1], ccol("i1024"), None, OP.mult),
                            reads=[kLn, cread("i1024")], writes=[kLn])
                        sch.op("dve", lambda e, b=b, lnst=lnst: e.tensor_scalar(
                            x2t[b][:, :], x2t[b][:, :], lnst[:, 0:1], None,
                            OP.subtract),
                            reads=[("x2t", b), kLn], writes=[("x2t", b)])
                        sch.op("act", lambda e, b=b, sqW=sqW: e.activation(
                            sqW[:, :], x2t[b][:, :], AF.Square),
                            reads=[("x2t", b)], writes=[kSq])
                        sch.op("dve", lambda e, sqW=sqW, lnst=lnst: e.reduce_sum(
                            lnst[:, 1:2], sqW[:, :], axis=AX),
                            reads=[kSq], writes=[kLn])
                        sch.op("dve", lambda e, lnst=lnst: e.tensor_scalar(
                            lnst[:, 1:2], lnst[:, 1:2], ccol("i1024"), None, OP.mult),
                            reads=[kLn, cread("i1024")], writes=[kLn])
                        sch.op("act", lambda e, lnst=lnst: e.activation(
                            lnst[:, 1:2], lnst[:, 1:2], AF.Ln, bias=ccol("eps_ln")),
                            reads=[kLn, cread("eps_ln")], writes=[kLn])
                        sch.op("act", lambda e, lnst=lnst: e.activation(
                            lnst[:, 1:2], lnst[:, 1:2], AF.Exp,
                            scale=ccol("neg_half")),
                            reads=[kLn, cread("neg_half")], writes=[kLn])
                        sch.op("dve", lambda e, b=b, lnst=lnst: e.tensor_scalar(
                            x2t[b][:, :], x2t[b][:, :], lnst[:, 1:2], None, OP.mult),
                            reads=[("x2t", b), kLn], writes=[("x2t", b)])
                        sch.op("actdma", lambda e, n=n, b=b: e.dma_start(
                            sc["x2"].ap()[n * C:(n + 1) * C, :], x2t[b][:, :]),
                            reads=[("x2t", b)], writes=[("scr", "x2")])
                        sch.op("pooldma", lambda e, n=n, b=b: e.dma_start(
                            sc["x2b"].ap()[n * C:(n + 1) * C, :], x2t[b][:, :]),
                            reads=[("x2t", b)], writes=[("scr", "x2b")])
                    sch.dma_drain()
                    with nc.Block() as blk:
                        sch.emit(blk)

        if 6 in _PHASES:
            with contextlib.ExitStack() as phs:
                TT6 = 1024
                x2f = sbt(phs, "p6_x2f", [128, 8 * TT6], BF16)
                hsb = sbt(phs, "p6_h", [128, 32 * TT6], BF16)
                usb = sbt(phs, "p6_u", [128, 32 * TT6], BF16)
                wA = [sbt(phs, f"p6_wA{i}", [128, DH], BF16) for i in range(2)]
                wB = [sbt(phs, f"p6_wB{i}", [128, D], BF16) for i in range(3)]
                w1s = [sbt(phs, f"p6_w1s{i}", [128, D], BF16) for i in range(3)]
                w3blk = [sbt(phs, f"p6_w3b{i}", [128, 1024], BF16) for i in range(3)]
                s1s = sbt(phs, "p6_s1", [128, 32], F32)
                s2s = sbt(phs, "p6_s2", [128, 32], F32)
                sh1s = sbt(phs, "p6_sh1", [128, 32], F32)
                sh2s = sbt(phs, "p6_sh2", [128, 32], F32)
                s3r = sbt(phs, "p6_s3r", [128, D], F32)
                silu_ts = [sbt(phs, f"p6_silu{i}", [128, 512], F32) for i in range(2)]
                x2l = [sbt(phs, "p6_x2l0", [128, D], F32)]
                otW = sbt(phs, "p6_otW", [128, 2 * D], F32)
                sq6 = sbt(phs, "p6_sq", [128, D], F32)
                lnst6 = sbt(phs, "p6_lnst", [128, 4], F32)
                pf = [phs.enter_context(
                    nc.psum_tensor(f"pp6_{i}", [128, 512], F32)) for i in range(8)]
                for nm, t_ in [("s1c", s1s), ("s2c", s2s), ("sh1c", sh1s),
                               ("sh2c", sh2s)]:
                    sch.op("sp", lambda e, nm=nm, t_=t_: e.dma_start(
                        t_[:, :], inp[nm].ap()), writes=[("p6s", nm)])
                sch.op("sp", lambda e: e.dma_start(s3r[:, :], inp["s3_rep"].ap()),
                       writes=["s3r"])

                for tt6 in range(TC // TT6):
                    for jb in range(8):
                        sch.op("actdma", lambda e, tt6=tt6, jb=jb: e.dma_start(
                            x2f[:, jb * TT6:(jb + 1) * TT6],
                            sc["x2b"].ap()[tt6 * TT6:(tt6 + 1) * TT6,
                                           jb * 128:(jb + 1) * 128], transpose=True),
                            reads=[("scr", "x2b")], writes=[("x2f", jb)])
                    for oc in range(32):
                        bw = oc % 3
                        sch.op("pooldma", lambda e, oc=oc, bw=bw: e.dma_start(
                            w1s[bw][:, :], inp["w1b"].ap()[oc]),
                            writes=[("w1s", bw)])
                        sch.op("pooldma", lambda e, oc=oc, bw=bw: e.dma_start(
                            wB[bw][:, :], inp["w2b"].ap()[oc]),
                            writes=[("wB", bw)])
                        po = (oc % 2) * 4
                        for ic in range(8):
                            for half in range(2):
                                sch.op("pe", lambda e, half=half, ic=ic, bw=bw, po=po: e.matmul(
                                    pf[po + half][:, :],
                                    w1s[bw][:, ic * 128:(ic + 1) * 128],
                                    x2f[:, ic * TT6 + half * 512: ic * TT6 + half * 512 + 512],
                                    start=(ic == 0), stop=(ic == 7)),
                                    reads=[("w1s", bw), ("x2f", ic)],
                                    writes=[("pf", po + half)])
                            for half in range(2):
                                sch.op("pe", lambda e, half=half, ic=ic, bw=bw, po=po: e.matmul(
                                    pf[po + 2 + half][:, :],
                                    wB[bw][:, ic * 128:(ic + 1) * 128],
                                    x2f[:, ic * TT6 + half * 512: ic * TT6 + half * 512 + 512],
                                    start=(ic == 0), stop=(ic == 7)),
                                    reads=[("wB", bw), ("x2f", ic)],
                                    writes=[("pf", po + 2 + half)])
                        for half in range(2):
                            st_ = silu_ts[(oc + half) % 2]
                            sch.op("act", lambda e, oc=oc, half=half, po=po, st_=st_: e.activation(
                                st_[:, :], pf[po + half][:, :], AF.Silu,
                                scale=s1s[:, oc:oc + 1]),
                                reads=[("pf", po + half), ("p6s", "s1c")],
                                writes=[("silu_t", (oc + half) % 2)])
                            sch.op("dve", lambda e, oc=oc, half=half, po=po, st_=st_: e.scalar_tensor_tensor(
                                hsb[:, oc * TT6 + half * 512: oc * TT6 + half * 512 + 512],
                                pf[po + 2 + half][:, :], s2s[:, oc:oc + 1],
                                st_[:, :],
                                OP.mult, OP.mult),
                                reads=[("pf", po + 2 + half), ("p6s", "s2c"),
                                       ("silu_t", (oc + half) % 2)],
                                writes=[("hsb", oc)])
                    for stage, (wkey, src, dst, dkey, scl, sclk) in enumerate([
                            ("h1b", "hsb", usb, "usb", sh1s, "sh1c"),
                            ("h2b", "usb", hsb, "hsb", sh2s, "sh2c")]):
                        srcsb = hsb if src == "hsb" else usb
                        for oc in range(32):
                            bw = oc % 2
                            sch.op("pooldma", lambda e, oc=oc, bw=bw, wkey=wkey: e.dma_start(
                                wA[bw][:, :], inp[wkey].ap()[oc]),
                                writes=[("wA", bw)])
                            boff = 4 - 4 * stage
                            for ic in range(32):
                                for half in range(2):
                                    pi = boff + (oc % 2) * 2 + half
                                    sch.op("pe", lambda e, half=half, ic=ic, pi=pi, bw=bw, srcsb=srcsb: e.matmul(
                                        pf[pi][:, :], wA[bw][:, ic * 128:(ic + 1) * 128],
                                        srcsb[:, ic * TT6 + half * 512: ic * TT6 + half * 512 + 512],
                                        start=(ic == 0), stop=(ic == 31)),
                                        reads=[("wA", bw), (src, ic)],
                                        writes=[("pf", pi)])
                            for half in range(2):
                                pi = boff + (oc % 2) * 2 + half
                                sch.op("act", lambda e, oc=oc, half=half, pi=pi, dst=dst, scl=scl: e.activation(
                                    dst[:, oc * TT6 + half * 512: oc * TT6 + half * 512 + 512],
                                    pf[pi][:, :], AF.Gelu, scale=scl[:, oc:oc + 1]),
                                    reads=[("pf", pi), ("p6s", sclk)],
                                    writes=[(dkey, oc)])
                    for wave in range(2):
                        for ic in range(32):
                            wb3 = ic % 3
                            sch.op("pooldma", lambda e, ic=ic, wb3=wb3: e.dma_start(
                                w3blk[wb3][:, :],
                                inp["w3t"].ap()[:, ic * D:(ic + 1) * D]),
                                writes=[("w3b", wb3)])
                            for q in range(4):
                                tc8 = wave * 4 + q
                                for half in range(2):
                                    sch.op("pe", lambda e, tc8=tc8, ic=ic, q=q, half=half, wb3=wb3: e.matmul(
                                        pf[q * 2 + half][:, :],
                                        hsb[:, ic * TT6 + tc8 * 128: ic * TT6 + (tc8 + 1) * 128],
                                        w3blk[wb3][:, half * 512:half * 512 + 512],
                                        start=(ic == 0), stop=(ic == 31)),
                                        reads=[("hsb", ic), ("w3b", wb3)],
                                        writes=[("pf", q * 2 + half)])
                        for q in range(4):
                            tc8 = wave * 4 + q
                            n = tt6 * (TT6 // 128) + tc8
                            b = n % 2
                            ow = otW[:, (q % 2) * D:(q % 2 + 1) * D]
                            for half in range(2):
                                sch.op("dve", lambda e, q=q, half=half: e.tensor_tensor(
                                    otW[:, (q % 2) * D + half * 512:
                                        (q % 2) * D + half * 512 + 512],
                                    pf[q * 2 + half][:, :],
                                    s3r[:, half * 512:half * 512 + 512], OP.mult),
                                    reads=[("pf", q * 2 + half), "s3r"],
                                    writes=[("otW", q % 2)])
                            sch.op("sp", lambda e, n=n: e.dma_start(
                                x2l[0][:, :], sc["x2"].ap()[n * C:(n + 1) * C, :]),
                                reads=[("scr", "x2")], writes=["x2l"])
                            sch.op("dve", lambda e, ow=ow: e.tensor_tensor(
                                ow, ow, x2l[0][:, :], OP.add),
                                reads=[("otW", q % 2), "x2l"], writes=[("otW", q % 2)])
                            sch.op("dve", lambda e, ow=ow: e.reduce_sum(
                                lnst6[:, 0:1], ow, axis=AX),
                                reads=[("otW", q % 2)], writes=["lnst6"])
                            sch.op("dve", lambda e: e.tensor_scalar(
                                lnst6[:, 0:1], lnst6[:, 0:1], ccol("i1024"), None,
                                OP.mult),
                                reads=["lnst6", cread("i1024")], writes=["lnst6"])
                            sch.op("dve", lambda e, ow=ow: e.tensor_scalar(
                                ow, ow, lnst6[:, 0:1], None, OP.subtract),
                                reads=[("otW", q % 2), "lnst6"], writes=[("otW", q % 2)])
                            sch.op("act", lambda e, ow=ow: e.activation(
                                sq6[:, :], ow, AF.Square),
                                reads=[("otW", q % 2)], writes=["sq6"])
                            sch.op("dve", lambda e: e.reduce_sum(
                                lnst6[:, 1:2], sq6[:, :], axis=AX),
                                reads=["sq6"], writes=["lnst6"])
                            sch.op("dve", lambda e: e.tensor_scalar(
                                lnst6[:, 1:2], lnst6[:, 1:2], ccol("i1024"), None,
                                OP.mult),
                                reads=["lnst6", cread("i1024")], writes=["lnst6"])
                            sch.op("act", lambda e: e.activation(
                                lnst6[:, 1:2], lnst6[:, 1:2], AF.Sqrt,
                                bias=ccol("eps_ln")),
                                reads=["lnst6", cread("eps_ln")], writes=["lnst6"])
                            sch.op("dve", lambda e: e.reciprocal(
                                lnst6[:, 1:2], lnst6[:, 1:2]),
                                reads=["lnst6"], writes=["lnst6"])
                            sch.op("dve", lambda e, ow=ow: e.tensor_scalar(
                                ow, ow, lnst6[:, 1:2], None, OP.mult),
                                reads=[("otW", q % 2), "lnst6"], writes=[("otW", q % 2)])
                            sch.op("sp", lambda e, n=n, ow=ow: e.dma_start(
                                out.ap()[n * C:(n + 1) * C, :], ow),
                                reads=[("otW", q % 2)], force_inc=True)
                sch.dma_drain()
                with nc.Block() as blk:
                    sch.emit(blk)

        with nc.Block() as blk:
            def final_waits(q):
                n_dma = sch.dma_count[q]
                vals = []
                for idx in range(_DMA_SUBSEMS):
                    cnt = (n_dma - idx + _DMA_SUBSEMS - 1) // _DMA_SUBSEMS
                    if cnt > 0:
                        vals.append((idx, 16 * cnt))
                return vals

            sp_waits = final_waits("sp")
            pd_waits = final_waits("pooldma")
            ad_waits = final_waits("actdma")

            @blk.sync
            def _(e):
                for idx, v in sp_waits:
                    e.wait_ge(dma_sems["sp"][idx], v)

            @blk.gpsimd
            def _(e):
                for idx, v in pd_waits:
                    e.wait_ge(dma_sems["pooldma"][idx], v)

            @blk.scalar
            def _(e):
                for idx, v in ad_waits:
                    e.wait_ge(dma_sems["actdma"][idx], v)

    return nc


def kernel(**inputs):
    in_maps = prep_inputs(inputs)
    nc = build_nc()
    res = run_bass_kernel_spmd(nc, in_maps, core_ids=list(range(N_CORES)))
    outp = np.concatenate(
        [np.asarray(r["out"], np.float32) for r in res.results], axis=0)
    return outp.reshape(1, T, D)


# version 33 (final)
# speedup vs baseline: 1.0325x; 1.0325x over previous
import contextlib

import ml_dtypes
import numpy as np

import concourse.bass as bass
import concourse.mybir as mybir
from concourse.alu_op_type import AluOpType as OP
from concourse.bass_utils import run_bass_kernel_spmd

F32 = mybir.dt.float32
BF16 = mybir.dt.bfloat16
AF = mybir.ActivationFunctionType
AX = mybir.AxisListType.X

N_CORES = 8
T, D, H = 16384, 1024, 16
TC = T // N_CORES
C = 128
NCH = TC // C
DH = 4096
MID = C // 2 - 1

_DEBUG_SCRATCH = set()
_PHASES = (1, 2, 3, 4, 5, 6)
_P3_CHUNKS = NCH
_P3_BWD = True
_P3_LEVEL = 9
_P4_LEVEL = 8



_ENGINE_OF = {
    "pe": "tensor", "act": "scalar", "dve": "vector", "pool": "gpsimd",
    "sp": "sync", "actdma": "scalar", "pooldma": "gpsimd",
}
_DMA_QUEUES = {"sp", "actdma", "pooldma"}


_DMA_SUBSEMS = 6


class _Op:
    __slots__ = ("q", "fn", "deps", "needs_inc", "inc_val", "sem_idx", "gate",
                 "pre_waits")

    def __init__(self, q, fn):
        self.q = q
        self.fn = fn
        self.deps = []
        self.needs_inc = q in _DMA_QUEUES
        self.inc_val = None
        self.sem_idx = 0
        self.gate = None
        self.pre_waits = ()


class Sched:

    def __init__(self, nc, sems, dma_sems):
        self.nc = nc
        self.sems = sems
        self.dma_sems = dma_sems
        self.inc_count = {q: 0 for q in _ENGINE_OF}
        self.dma_count = {q: 0 for q in _DMA_QUEUES}
        self.ops = []
        self.last_writer = {}
        self.readers = {}

    def dma_drain(self):
        for q in _DMA_QUEUES:
            waits = []
            n_dma = self.dma_count[q]
            for idx in range(_DMA_SUBSEMS):
                cnt = (n_dma - idx + _DMA_SUBSEMS - 1) // _DMA_SUBSEMS
                if cnt > 0:
                    waits.append((self.dma_sems[q][idx], 16 * cnt, (q, idx)))
            if waits:
                o = _Op(q, None)
                o.needs_inc = False
                o.pre_waits = waits
                self.ops.append(o)

    def op(self, q, fn, reads=(), writes=(), force_inc=False):
        o = _Op(q, fn)
        if q in _DMA_QUEUES:
            i = self.dma_count[q]
            self.dma_count[q] = i + 1
            o.sem_idx = i % _DMA_SUBSEMS
            o.inc_val = 16 * (i // _DMA_SUBSEMS + 1)
            if i >= _DMA_SUBSEMS:
                o.gate = o.inc_val - 16
        for k in reads:
            w = self.last_writer.get(k)
            if w is not None:
                o.deps.append(w)
            self.readers.setdefault(k, []).append(o)
        for k in writes:
            w = self.last_writer.get(k)
            if w is not None:
                o.deps.append(w)
            o.deps.extend(self.readers.get(k, ()))
            self.last_writer[k] = o
            self.readers[k] = []
        self.ops.append(o)
        return o

    def emit(self, block):
        ops = self.ops
        self.ops = []
        for o in ops:
            o.deps = [d for d in o.deps
                      if d is not o and not (d.q == "pe" and o.q == "pe")]
            for d in o.deps:
                d.needs_inc = True
        for o in ops:
            if o.needs_inc and o.inc_val is None:
                self.inc_count[o.q] += 1
                o.inc_val = self.inc_count[o.q]
        per_engine = {}
        for o in ops:
            per_engine.setdefault(_ENGINE_OF[o.q], []).append(o)

        def dep_sem(d):
            if d.q in _DMA_QUEUES:
                return self.dma_sems[d.q][d.sem_idx], (d.q, d.sem_idx)
            return self.sems[d.q], (d.q, 0)

        def mk_section(eng_ops):
            def section(eng):
                seen = {}
                for o in eng_ops:
                    for sem, v, key in o.pre_waits:
                        if seen.get(key, 0) < v:
                            seen[key] = v
                            eng.wait_ge(sem, v)
                    if o.fn is None:
                        continue
                    if o.gate is not None:
                        sem = self.dma_sems[o.q][o.sem_idx]
                        key = (o.q, o.sem_idx)
                        if seen.get(key, 0) < o.gate:
                            seen[key] = o.gate
                            eng.wait_ge(sem, o.gate)
                    for d in o.deps:
                        if d.inc_val is None:
                            continue
                        sem, key = dep_sem(d)
                        if seen.get(key, 0) >= d.inc_val:
                            continue
                        seen[key] = d.inc_val
                        eng.wait_ge(sem, d.inc_val)
                    inst = o.fn(eng)
                    if o.needs_inc:
                        if o.q in _DMA_QUEUES:
                            inst.then_inc(self.dma_sems[o.q][o.sem_idx], 16)
                        else:
                            inst.then_inc(self.sems[o.q], 1)
            return section

        for name, eng_ops in per_engine.items():
            getattr(block, name)(mk_section(eng_ops))



def _chunk_cols(v):
    return np.ascontiguousarray(np.asarray(v, np.float32).reshape(8, 128).T)


def _swiz(w):
    w = np.asarray(w, np.float32)
    ni = w.shape[0] // 128
    return np.ascontiguousarray(
        w.reshape(ni, 128, w.shape[1]).transpose(1, 0, 2).reshape(128, -1))


def _mw2d(w):
    out = np.zeros((64, 3 * D), np.float32)
    for f in range(5):
        blk, par = f // 2, f % 2
        out[par * 32:(par + 1) * 32, blk * D:(blk + 1) * D] = w[f]
    return np.ascontiguousarray(out)


def _blkw(wT, nblk_o):
    inn, outn = wT.shape
    ni = inn // 128
    a = wT.reshape(ni, 128, nblk_o, 128).transpose(2, 1, 0, 3)
    return np.ascontiguousarray(a.reshape(nblk_o, 128, inn))


def prep_inputs(ins):
    x = np.asarray(ins["L0_attn_x"], np.float32)[0]
    conv_w = np.asarray(ins["L0_attn_conv_w"], np.float32)
    j = np.arange(C)[:, None]
    i = np.arange(C)[None, :]
    tri_ci = -(j <= i).astype(np.float32)
    tri_cs = -(j > i).astype(np.float32)
    tri_cc = ((j <= MID).astype(np.float32) - (j <= i))
    tri_cb = ((j > MID).astype(np.float32) - (j > i))
    aux = np.zeros((C, 4), np.float32)
    aux[:, 0] = -(np.arange(C) <= MID).astype(np.float32)
    aux[:, 1] = -1.0
    aux[:, 2] = -(np.arange(C) > MID).astype(np.float32)
    mask_f = (j <= i).astype(np.float32)
    mask_b = (j > i).astype(np.float32)
    pmask = np.zeros((128, 2), np.float32)
    pmask[0:64, 0] = 1.0
    pmask[64:128, 1] = 1.0
    colmask = np.zeros((128, 2048), np.float32)
    cols = np.arange(1024)
    colmask[:, 0:1024] = ((cols % 128) < 64).astype(np.float32)[None, :]
    colmask[:, 1024:2048] = ((cols % 128) >= 64).astype(np.float32)[None, :]

    def rep(v):
        return np.ascontiguousarray(
            np.broadcast_to(np.asarray(v, np.float32)[None, :], (128, D)))

    w1T = np.asarray(ins["L0_ffn_w1_w"], np.float32).T
    w2T = np.asarray(ins["L0_ffn_w2_w"], np.float32).T
    h1T = np.asarray(ins["L0_ffn_h1_w"], np.float32).T
    h2T = np.asarray(ins["L0_ffn_h2_w"], np.float32).T
    w3T = np.asarray(ins["L0_ffn_w3_w"], np.float32).T

    def scale_cols(s, nblk):
        return np.ascontiguousarray(np.asarray(s, np.float32).reshape(nblk, 128).T)

    def _bf(a):
        return np.ascontiguousarray(np.asarray(a).astype(ml_dtypes.bfloat16))

    shared = {
        "c0": _chunk_cols(conv_w[:, 0, 0]),
        "c1": _chunk_cols(conv_w[:, 0, 1]),
        "c2": _chunk_cols(conv_w[:, 0, 2]),
        "maa_x": _chunk_cols(ins["L0_attn_maa_x"]),
        "maa_w": _chunk_cols(ins["L0_attn_maa_w"]),
        "maa_k": _chunk_cols(ins["L0_attn_maa_k"]),
        "maa_v": _chunk_cols(ins["L0_attn_maa_v"]),
        "maa_r": _chunk_cols(ins["L0_attn_maa_r"]),
        "maa_g": _chunk_cols(ins["L0_attn_maa_g"]),
        "maa_w1": _bf(_swiz(np.asarray(ins["L0_attn_maa_w1"], np.float32))),
        "maa_w2": _bf(_mw2d(np.asarray(ins["L0_attn_maa_w2"], np.float32))),
        "dec_w1": _bf(_swiz(np.asarray(ins["L0_attn_decay_w1"], np.float32))),
        "dec_w2": _bf(np.asarray(ins["L0_attn_decay_w2"], np.float32)),
        "td_rep": rep(ins["L0_attn_time_decay"]),
        "Wr": _bf(_swiz(ins["L0_attn_Wr"])), "Wk": _bf(_swiz(ins["L0_attn_Wk"])),
        "Wv": _bf(_swiz(ins["L0_attn_Wv"])), "Wg": _bf(_swiz(ins["L0_attn_Wg"])),
        "Wo": _bf(_swiz(ins["L0_attn_Wo"])), "gate_w": _bf(_swiz(ins["L0_attn_gate_w"])),
        "gate_b_rep": rep(ins["L0_attn_gate_b"]),
        "ln_w_rep": rep(ins["L0_attn_ln_w"]),
        "ln_b_rep": rep(ins["L0_attn_ln_b"]),
        "tri_ci": tri_ci, "tri_cs": tri_cs, "tri_cc": tri_cc, "tri_cb": tri_cb,
        "pmask": pmask, "colmask": _bf(colmask),
        "aux_cols": np.ascontiguousarray(aux),
        "mask_f": mask_f, "mask_b": mask_b,
        "w1b": _bf(_blkw(w1T, 32)), "w2b": _bf(_blkw(w2T, 32)),
        "h1b": _bf(_blkw(h1T, 32)), "h2b": _bf(_blkw(h2T, 32)),
        "w3t": _bf(_swiz(w3T)),
        "s1c": scale_cols(ins["L0_ffn_w1_s"], 32),
        "s2c": scale_cols(ins["L0_ffn_w2_s"], 32),
        "sh1c": scale_cols(ins["L0_ffn_h1_s"], 32),
        "sh2c": scale_cols(ins["L0_ffn_h2_s"], 32),
        "s3_rep": rep(ins["L0_ffn_w3_s"]),
    }

    in_maps = []
    for c in range(N_CORES):
        s, e = c * TC, (c + 1) * TC
        xh = np.zeros((TC + 2, D), np.float32)
        xh[1:-1] = x[s:e]
        if s > 0:
            xh[0] = x[s - 1]
        if e < T:
            xh[-1] = x[e]
        xf = np.ascontiguousarray(
            xh.T.reshape(8, 128, TC + 2).transpose(1, 0, 2))
        cmf = np.zeros((128, 8), np.float32)
        cmb = np.zeros((128, 8), np.float32)
        cmf[:, :c] = 1.0
        cmb[:, c + 1:] = 1.0
        m = dict(shared)
        m["xf"] = xf
        m["xt"] = np.ascontiguousarray(x[s:e])
        m["cmask_f"] = cmf
        m["cmask_b"] = cmb
        in_maps.append(m)
    return in_maps



_BF16_INPUTS = {
    "maa_w1", "maa_w2", "dec_w1", "dec_w2",
    "Wr", "Wk", "Wv", "Wg", "Wo", "gate_w",
    "w1b", "w2b", "h1b", "h2b", "w3t", "colmask",
}


def build_nc():
    nc = bass.Bass("TRN2", target_bir_lowering=False)

    EI = dict(kind="ExternalInput")
    inp = {}
    for name, shape in [
        ("xf", [128, 8, TC + 2]), ("xt", [TC, D]),
        ("c0", [128, 8]), ("c1", [128, 8]), ("c2", [128, 8]),
        ("maa_x", [128, 8]), ("maa_w", [128, 8]), ("maa_k", [128, 8]),
        ("maa_v", [128, 8]), ("maa_r", [128, 8]), ("maa_g", [128, 8]),
        ("maa_w1", [128, 8 * 160]), ("maa_w2", [64, 3 * D]),
        ("dec_w1", [128, 8 * 64]), ("dec_w2", [64, D]), ("td_rep", [128, D]),
        ("Wr", [128, 8 * D]), ("Wk", [128, 8 * D]), ("Wv", [128, 8 * D]),
        ("Wg", [128, 8 * D]), ("Wo", [128, 8 * D]), ("gate_w", [128, 8 * D]),
        ("gate_b_rep", [128, D]), ("ln_w_rep", [128, D]), ("ln_b_rep", [128, D]),
        ("tri_ci", [C, C]), ("tri_cs", [C, C]), ("tri_cc", [C, C]),
        ("tri_cb", [C, C]), ("aux_cols", [C, 4]),
        ("mask_f", [C, C]), ("mask_b", [C, C]),
        ("pmask", [128, 2]), ("colmask", [128, 2048]),
        ("cmask_f", [128, 8]), ("cmask_b", [128, 8]),
        ("w1b", [32, 128, D]), ("w2b", [32, 128, D]),
        ("h1b", [32, 128, DH]), ("h2b", [32, 128, DH]), ("w3t", [128, 32 * D]),
        ("s1c", [128, 32]), ("s2c", [128, 32]), ("sh1c", [128, 32]),
        ("sh2c", [128, 32]), ("s3_rep", [128, D]),
    ]:
        dt = BF16 if name in _BF16_INPUTS else F32
        inp[name] = nc.dram_tensor(name, shape, dt, **EI)
    out = nc.dram_tensor("out", [TC, D], F32, kind="ExternalOutput")

    def scratch(name, shape, dt):
        kind = "ExternalOutput" if name in _DEBUG_SCRATCH else "Internal"
        return nc.dram_tensor(name, shape, dt, kind=kind)

    sc = {
        "xw": scratch("sc_xw", [D, TC], BF16),
        "xk": scratch("sc_xk", [D, TC], BF16),
        "xv": scratch("sc_xv", [D, TC], BF16),
        "xr": scratch("sc_xr", [D, TC], BF16),
        "xg": scratch("sc_xg", [D, TC], BF16),
        "dx": scratch("sc_dx", [D, TC], BF16),
        "r": scratch("sc_r", [NCH, 128, 8 * C], BF16),
        "kf": scratch("sc_kf", [NCH, 128, 8 * C], BF16),
        "k": scratch("sc_k", [NCH, D, C], BF16),
        "v": scratch("sc_v", [TC, D], BF16),
        "p": scratch("sc_p", [TC, D], F32),
        "g": scratch("sc_g", [TC, D], F32),
        "sig": scratch("sc_sig", [TC, D], F32),
        "x2": scratch("sc_x2", [TC, D], F32),
        "x2b": scratch("sc_x2b", [TC, D], BF16),
        "zb": scratch("sc_zb", [TC, D], BF16),
        "y": scratch("sc_y", [TC, D], F32),
        "ccin": scratch("cc_in", [2064, 64], F32),
        "ccout": scratch("cc_out", [2064 * N_CORES, 64], F32),
    }

    with contextlib.ExitStack() as stack:
        sems = {}
        dma_sems = {}
        for q in _ENGINE_OF:
            if q in _DMA_QUEUES:
                dma_sems[q] = [stack.enter_context(nc.semaphore(f"sem_{q}{i}"))
                               for i in range(_DMA_SUBSEMS)]
            else:
                sems[q] = stack.enter_context(nc.semaphore("sem_" + q))
        sch = Sched(nc, sems, dma_sems)

        def sbt(st, name, shape, dt):
            return st.enter_context(nc.sbuf_tensor(name, shape, dt))

        consts = sbt(stack, "consts", [128, 8], F32)
        CONST_COLS = {"zero": 0, "one": 1, "neg1": 2, "i1024": 3,
                      "eps_gn": 4, "eps_ln": 5, "i64": 6, "neg_half": 7}
        CONST_VALS = {"zero": 0.0, "one": 1.0, "neg1": -1.0, "i1024": 1.0 / 1024,
                      "eps_gn": 1e-5 * 64.0, "eps_ln": 1e-5, "i64": 1.0 / 64,
                      "neg_half": -0.5}

        def ccol(name):
            return consts[:, CONST_COLS[name]:CONST_COLS[name] + 1]

        def cread(name):
            return ("consts", CONST_COLS[name])

        smalls = {}
        for name, shape in [
            ("c0", [128, 8]), ("c1", [128, 8]), ("c2", [128, 8]),
            ("maa_x", [128, 8]), ("maa_w", [128, 8]), ("maa_k", [128, 8]),
            ("maa_v", [128, 8]), ("maa_r", [128, 8]), ("maa_g", [128, 8]),
            ("tri_ci", [C, C]), ("tri_cs", [C, C]), ("tri_cc", [C, C]),
            ("tri_cb", [C, C]), ("aux_cols", [C, 4]),
            ("mask_f", [C, C]), ("mask_b", [C, C]),
            ("pmask", [128, 2]),
            ("cmask_f", [128, 8]), ("cmask_b", [128, 8]),
        ]:
            smalls[name] = sbt(stack, "s_" + name, shape, F32)

        for name, col in CONST_COLS.items():
            sch.op("pool", lambda e, c=col, v=CONST_VALS[name]: e.memset(
                consts[:, c:c + 1], v), writes=[("consts", col)])
        for name in smalls:
            sch.op("sp", lambda e, n=name: e.dma_start(smalls[n][:, :], inp[n].ap()),
                   writes=[("small", name)])
        sch.op("dve", lambda e: e.tensor_scalar(
            smalls["c1"][:, :], smalls["c1"][:, :], 1.0, None, OP.subtract),
            reads=[("small", "c1")], writes=[("small", "c1")])
        sch.dma_drain()
        with nc.Block() as blk:
            sch.emit(blk)

        if 1 in _PHASES:
            with contextlib.ExitStack() as phs:
                xin = sbt(phs, "p1_xin", [128, 8, TC + 2], F32)
                dxb = sbt(phs, "p1_dx", [128, 8 * TC], BF16)
                xxx = sbt(phs, "p1_xxx", [128, 8 * TC], BF16)
                tmpas = [sbt(phs, "p1_tmpa0", [128, TC], F32)] * 2
                tmpbs = [sbt(phs, "p1_tmpb0", [128, TC], F32)] * 2
                mw1 = sbt(phs, "p1_mw1", [128, 8 * 160], BF16)
                mw2 = sbt(phs, "p1_mw2", [64, 3 * D], BF16)
                t5a = sbt(phs, "p1_t5a", [64, TC], BF16)
                t5b = sbt(phs, "p1_t5b", [64, TC], BF16)
                t5c = sbt(phs, "p1_t5c", [32, TC], BF16)
                xog = [sbt(phs, f"p1_xo{i}", [128, 2048], BF16) for i in range(5)]
                mtmp = [sbt(phs, f"p1_mt{i}", [128, 512], F32) for i in range(2)]
                p_t5 = [phs.enter_context(
                    nc.psum_tensor(f"pp_t5{i}", [64, 512], F32)) for i in range(2)]
                p_t5x = phs.enter_context(nc.psum_tensor("pp_t5x", [32, 512], F32))
                p_m = [phs.enter_context(
                    nc.psum_tensor(f"pp_m{i}", [128, 512], F32)) for i in range(5)]

                sch.op("pooldma", lambda e: e.dma_start(mw1[:, :], inp["maa_w1"].ap()),
                       writes=["mw1"])
                sch.op("pooldma", lambda e: e.dma_start(mw2[:, :], inp["maa_w2"].ap()),
                       writes=["mw2"])
                for j in range(8):
                    sch.op("sp", lambda e, j=j: e.dma_start(
                        xin[:, j, :], inp["xf"].ap()[:, j, :]),
                        writes=[("xin", j)])

                for j in range(8):
                    jb2 = 0
                    tmpa, tmpb = tmpas[jb2], tmpbs[jb2]
                    ka, kb = ("tmpa", jb2), ("tmpb", jb2)
                    x_l = xin[:, j, 0:TC]
                    x_c = xin[:, j, 1:TC + 1]
                    x_r = xin[:, j, 2:TC + 2]
                    sch.op("dve", lambda e, x_l=x_l, j=j, tmpa=tmpa: e.tensor_scalar(
                        tmpa[:, :], x_l, smalls["c0"][:, j:j + 1], None, OP.mult),
                        reads=[("xin", j), ("small", "c0")], writes=[ka])
                    sch.op("dve", lambda e, x_c=x_c, j=j, tmpa=tmpa, tmpb=tmpb: e.scalar_tensor_tensor(
                        tmpb[:, :], x_c, smalls["c1"][:, j:j + 1], tmpa[:, :],
                        OP.mult, OP.add),
                        reads=[("xin", j), ("small", "c1"), ka], writes=[kb])
                    sch.op("dve", lambda e, x_r=x_r, j=j, tmpb=tmpb: e.scalar_tensor_tensor(
                        dxb[:, j * TC:(j + 1) * TC], x_r, smalls["c2"][:, j:j + 1],
                        tmpb[:, :], OP.mult, OP.add),
                        reads=[("xin", j), ("small", "c2"), kb],
                        writes=[("dxb", j)])
                    sch.op("dve", lambda e, x_c=x_c, j=j: e.scalar_tensor_tensor(
                        xxx[:, j * TC:(j + 1) * TC], dxb[:, j * TC:(j + 1) * TC],
                        smalls["maa_x"][:, j:j + 1], x_c, OP.mult, OP.add),
                        reads=[("dxb", j), ("small", "maa_x"), ("xin", j)],
                        writes=[("xxx", j)])
                    sch.op("sp", lambda e, j=j: e.dma_start(
                        sc["dx"].ap()[128 * j:128 * (j + 1), :],
                        dxb[:, j * TC:(j + 1) * TC]),
                        reads=[("dxb", j)], writes=[("scr", "dx")])

                for tt in range(4):
                    for ic in range(8):
                        for g in range(2):
                            sch.op("pe", lambda e, ic=ic, tt=tt, g=g: e.matmul(
                                p_t5[g][:, :],
                                mw1[:, ic * 160 + g * 64:ic * 160 + g * 64 + 64],
                                xxx[:, ic * TC + tt * 512: ic * TC + (tt + 1) * 512],
                                start=(ic == 0), stop=(ic == 7)),
                                reads=["mw1", ("xxx", ic)], writes=[("p_t5", g)])
                        sch.op("pe", lambda e, ic=ic, tt=tt: e.matmul(
                            p_t5x[:, :], mw1[:, ic * 160 + 128:ic * 160 + 160],
                            xxx[:, ic * TC + tt * 512: ic * TC + (tt + 1) * 512],
                            start=(ic == 0), stop=(ic == 7)),
                            reads=["mw1", ("xxx", ic)], writes=["p_t5x"])
                    sch.op("act", lambda e, tt=tt: e.activation(
                        t5a[:, tt * 512:(tt + 1) * 512], p_t5[0][:, :], AF.Tanh),
                        reads=[("p_t5", 0)], writes=["t5a"])
                    sch.op("act", lambda e, tt=tt: e.activation(
                        t5b[:, tt * 512:(tt + 1) * 512], p_t5[1][:, :], AF.Tanh),
                        reads=[("p_t5", 1)], writes=["t5b"])
                    sch.op("act", lambda e, tt=tt: e.activation(
                        t5c[:, tt * 512:(tt + 1) * 512], p_t5x[:, :], AF.Tanh),
                        reads=["p_t5x"], writes=["t5c"])

                t5block = [(t5a, 0, "t5a"), (t5a, 32, "t5a"), (t5b, 0, "t5b"),
                           (t5b, 32, "t5b"), (t5c, 0, "t5c")]
                names = ["xw", "xk", "xv", "xr", "xg"]
                maas = ["maa_w", "maa_k", "maa_v", "maa_r", "maa_g"]
                for oc in range(8):
                    for f in range(5):
                        tile, base, tkey = t5block[f]
                        blk5 = f // 2
                        xot = xog[f]
                        for tt in range(4):
                            sch.op("pe", lambda e, f=f, oc=oc, tt=tt, tile=tile, base=base, blk5=blk5: e.matmul(
                                p_m[f][:, :],
                                mw2[base:base + 32,
                                    blk5 * D + oc * 128: blk5 * D + (oc + 1) * 128],
                                tile[base:base + 32, tt * 512:(tt + 1) * 512],
                                start=True, stop=True),
                                reads=["mw2", tkey], writes=[("p_m", f)])
                            mt = mtmp[(f + tt) % 2]
                            sch.op("dve", lambda e, f=f, oc=oc, tt=tt, mt=mt: e.scalar_tensor_tensor(
                                mt[:, :], p_m[f][:, :],
                                smalls[maas[f]][:, oc:oc + 1],
                                dxb[:, oc * TC + tt * 512: oc * TC + (tt + 1) * 512],
                                OP.add, OP.mult),
                                reads=[("p_m", f), ("small", maas[f]),
                                       ("dxb", oc)],
                                writes=[("mtmp", (f + tt) % 2)])
                            sch.op("pool", lambda e, oc=oc, tt=tt, mt=mt, xot=xot: e.tensor_tensor(
                                xot[:, tt * 512:(tt + 1) * 512], mt[:, :],
                                xin[:, oc, 1 + tt * 512: 1 + (tt + 1) * 512],
                                OP.add),
                                reads=[("mtmp", (f + tt) % 2), ("xin", oc)],
                                writes=[("xog", f)])
                        sch.op("sp" if f % 2 == 0 else "actdma",
                               lambda e, f=f, oc=oc, xot=xot: e.dma_start(
                                   sc[names[f]].ap()[oc * 128:(oc + 1) * 128, :],
                                   xot[:, :]),
                               reads=[("xog", f)],
                               writes=[("scr", names[f])])
                sch.dma_drain()
                with nc.Block() as blk:
                    sch.emit(blk)

        if 2 in _PHASES:
            with contextlib.ExitStack() as phs:
                wt = [sbt(phs, f"p2_wt{i}", [128, 8 * D], BF16) for i in range(2)]
                xs = [sbt(phs, f"p2_xs{i}", [128, 8 * TC], BF16) for i in range(2)]
                ot = [sbt(phs, f"p2_ot{i}", [128, 1024], BF16) for i in range(4)]
                of = [sbt(phs, f"p2_of{i}", [128, 1024], F32) for i in range(2)]
                hdec = sbt(phs, "p2_hdec", [64, TC], BF16)
                dw1 = sbt(phs, "p2_dw1", [128, 8 * 64], BF16)
                dw2 = sbt(phs, "p2_dw2", [64, D], BF16)
                gb = sbt(phs, "p2_gb", [128, D], F32)
                td = sbt(phs, "p2_td", [128, D], F32)
                pp = [phs.enter_context(
                    nc.psum_tensor(f"pp2_{i}", [128, 512], F32)) for i in range(4)]

                sch.op("sp", lambda e: e.dma_start(gb[:, :], inp["gate_b_rep"].ap()),
                       writes=["gb"])
                sch.op("sp", lambda e: e.dma_start(td[:, :], inp["td_rep"].ap()),
                       writes=["td"])
                sch.op("pooldma", lambda e: e.dma_start(dw1[:, :], inp["dec_w1"].ap()),
                       writes=["dw1"])
                sch.op("pooldma", lambda e: e.dma_start(dw2[:, :], inp["dec_w2"].ap()),
                       writes=["dw2"])

                def load_xs(key, xb):
                    for j in range(8):
                        sch.op("pooldma", lambda e, j=j, key=key, xb=xb: e.dma_start(
                            xs[xb][:, j * TC:(j + 1) * TC],
                            sc[key].ap()[128 * j:128 * (j + 1), :]),
                            reads=[("scr", key)], writes=[("xs", xb, j)])

                def load_w(wname, buf):
                    sch.op("pooldma", lambda e, wname=wname, buf=buf: e.dma_start(
                        wt[buf][:, :], inp[wname].ap()), writes=[("wt", buf)])

                for wname, skey, okey in [("Wr", "xr", "r"), ("Wk", "xk", "k")]:
                    buf = 0 if wname == "Wr" else 1
                    xb = buf
                    load_w(wname, buf)
                    load_xs(skey, xb)
                    for oc in range(8):
                        for tt in range(4):
                            bi = (oc * 4 + tt) % 4
                            pb = pp[bi]
                            for ic in range(8):
                                sch.op("pe", lambda e, oc=oc, tt=tt, ic=ic, pb=pb, buf=buf, xb=xb: e.matmul(
                                    pb[:, :],
                                    wt[buf][:, ic * D + oc * 128: ic * D + (oc + 1) * 128],
                                    xs[xb][:, ic * TC + tt * 512: ic * TC + (tt + 1) * 512],
                                    start=(ic == 0), stop=(ic == 7)),
                                    reads=[("wt", buf), ("xs", xb, ic)],
                                    writes=[("pp", bi)])
                            obuf = ot[bi]
                            sch.op("act", lambda e, pb=pb, obuf=obuf: e.activation(
                                obuf[:, 0:512], pb[:, :], AF.Copy),
                                reads=[("pp", bi)],
                                writes=[("ot", bi)])
                            if okey == "r":
                                sch.op("sp", lambda e, tt=tt, oc=oc, obuf=obuf: e.dma_start(
                                    sc["r"].ap()[4 * tt:4 * tt + 4, :, oc * C:(oc + 1) * C]
                                    .rearrange("n p c -> p n c"),
                                    obuf[:, 0:512].rearrange("p (n c) -> p n c", c=C)),
                                    reads=[("ot", bi)],
                                    writes=[("scr", "r")])
                            else:
                                sch.op("sp", lambda e, tt=tt, oc=oc, obuf=obuf: e.dma_start(
                                    sc["kf"].ap()[4 * tt:4 * tt + 4, :, oc * C:(oc + 1) * C]
                                    .rearrange("n p c -> p n c"),
                                    obuf[:, 0:512].rearrange("p (n c) -> p n c", c=C)),
                                    reads=[("ot", bi)],
                                    writes=[("scr", "kf")])
                                sch.op("actdma", lambda e, tt=tt, oc=oc, obuf=obuf: e.dma_start(
                                    sc["k"].ap()[4 * tt:4 * tt + 4, oc * 128:(oc + 1) * 128, :]
                                    .rearrange("n p c -> p n c"),
                                    obuf[:, 0:512].rearrange("p (n c) -> p n c", c=C)),
                                    reads=[("ot", bi)],
                                    writes=[("scr", "k")])

                for wi, (wname, skey, okey) in enumerate([
                        ("Wv", "xv", "v"), ("Wg", "xg", "g"),
                        ("gate_w", "dx", "sig")]):
                    buf = wi % 2
                    xb = wi % 2
                    load_w(wname, buf)
                    load_xs(skey, xb)
                    for tc in range(NCH):
                        pbase = (tc % 2) * 2
                        for half in range(2):
                            pb = pp[pbase + half]
                            for ic in range(8):
                                sch.op("pe", lambda e, tc=tc, half=half, ic=ic, pb=pb, buf=buf, xb=xb: e.matmul(
                                    pb[:, :],
                                    xs[xb][:, ic * TC + tc * 128: ic * TC + (tc + 1) * 128],
                                    wt[buf][:, ic * D + half * 512: ic * D + half * 512 + 512],
                                    start=(ic == 0), stop=(ic == 7)),
                                    reads=[("wt", buf), ("xs", xb, ic)],
                                    writes=[("pp", pbase + half)])
                        if okey == "v":
                            obuf = ot[tc % 2]
                            for half in range(2):
                                sch.op("act", lambda e, half=half, obuf=obuf, pbase=pbase: e.activation(
                                    obuf[:, half * 512:half * 512 + 512],
                                    pp[pbase + half][:, :], AF.Copy),
                                    reads=[("pp", pbase + half)], writes=[("ot", tc % 2)])
                            sch.op("sp", lambda e, tc=tc, obuf=obuf: e.dma_start(
                                sc["v"].ap()[tc * 128:(tc + 1) * 128, :], obuf[:, :]),
                                reads=[("ot", tc % 2)], writes=[("scr", "v")])
                        elif okey == "g":
                            obuf = of[tc % 2]
                            for half in range(2):
                                sch.op("act", lambda e, half=half, obuf=obuf, pbase=pbase: e.activation(
                                    obuf[:, half * 512:half * 512 + 512],
                                    pp[pbase + half][:, :], AF.Silu),
                                    reads=[("pp", pbase + half)], writes=[("of", tc % 2)])
                            sch.op("sp", lambda e, tc=tc, obuf=obuf: e.dma_start(
                                sc["g"].ap()[tc * 128:(tc + 1) * 128, :], obuf[:, :]),
                                reads=[("of", tc % 2)], writes=[("scr", "g")])
                        else:
                            obuf = of[tc % 2]
                            for half in range(2):
                                sch.op("dve", lambda e, half=half, obuf=obuf, pbase=pbase: e.tensor_tensor(
                                    obuf[:, half * 512:half * 512 + 512],
                                    pp[pbase + half][:, :],
                                    gb[:, half * 512:half * 512 + 512], OP.add),
                                    reads=[("pp", pbase + half), "gb"],
                                    writes=[("of", tc % 2)])
                            sch.op("act", lambda e, obuf=obuf: e.activation(
                                obuf[:, :], obuf[:, :], AF.Sigmoid),
                                reads=[("of", tc % 2)], writes=[("of", tc % 2)])
                            sch.op("sp", lambda e, tc=tc, obuf=obuf: e.dma_start(
                                sc["sig"].ap()[tc * 128:(tc + 1) * 128, :], obuf[:, :]),
                                reads=[("of", tc % 2)], writes=[("scr", "sig")])

                load_xs("xw", 1)
                for tt in range(4):
                    pb = pp[0]
                    for ic in range(8):
                        sch.op("pe", lambda e, tt=tt, ic=ic, pb=pb: e.matmul(
                            pb[0:64, :], dw1[:, ic * 64:(ic + 1) * 64],
                            xs[1][:, ic * TC + tt * 512: ic * TC + (tt + 1) * 512],
                            start=(ic == 0), stop=(ic == 7)),
                            reads=["dw1", ("xs", 1, ic)], writes=[("pp", 0)])
                    sch.op("act", lambda e, tt=tt, pb=pb: e.activation(
                        hdec[:, tt * 512:(tt + 1) * 512], pb[0:64, :], AF.Tanh),
                        reads=[("pp", 0)], writes=["hdec"])
                for tc in range(NCH):
                    pbase = (tc % 2) * 2
                    for half in range(2):
                        pb = pp[pbase + half]
                        sch.op("pe", lambda e, tc=tc, half=half, pb=pb: e.matmul(
                            pb[:, :], hdec[:, tc * 128:(tc + 1) * 128],
                            dw2[:, half * 512:half * 512 + 512],
                            start=True, stop=True),
                            reads=["hdec", "dw2"], writes=[("pp", pbase + half)])
                    obuf = of[tc % 2]
                    for half in range(2):
                        sch.op("dve", lambda e, half=half, obuf=obuf, pbase=pbase: e.tensor_tensor(
                            obuf[:, half * 512:half * 512 + 512], pp[pbase + half][:, :],
                            td[:, half * 512:half * 512 + 512], OP.add),
                            reads=[("pp", pbase + half), "td"], writes=[("of", tc % 2)])
                    sch.op("act", lambda e, obuf=obuf: e.activation(
                        obuf[:, :], obuf[:, :], AF.Exp),
                        reads=[("of", tc % 2)], writes=[("of", tc % 2)])
                    sch.op("sp", lambda e, tc=tc, obuf=obuf: e.dma_start(
                        sc["p"].ap()[tc * 128:(tc + 1) * 128, :], obuf[:, :]),
                        reads=[("of", tc % 2)], writes=[("scr", "p")])
                sch.dma_drain()
                with nc.Block() as blk:
                    sch.emit(blk)

        with contextlib.ExitStack() as mid:
            ybuf = None
            if any(p in _PHASES for p in (3, 4, 5)):
                ybuf = sbt(mid, "ybuf", [128, NCH * D], F32)

            if 3 in _PHASES:
                p3 = contextlib.ExitStack()
                mid.enter_context(p3)
                swp = contextlib.ExitStack()
                r16 = [sbt(p3, f"p3_r{i}", [128, D], BF16) for i in range(3)]
                ptok = [sbt(p3, f"p3_pt{i}", [128, D], F32) for i in range(3)]
                Stmp = sbt(p3, "p3_Stmp", [128, 512], F32)
                DnAll = sbt(p3, "p3_DnAll", [128, 8 * NCH], F32)
                BsAll = sbt(p3, "p3_BsAll", [128, 8 * NCH], F32)
                E1u = [sbt(p3, f"p3_E1u{i}", [128, D], BF16) for i in range(2)]
                E3t = [sbt(p3, f"p3_E3t{i}", [128, D], BF16) for i in range(2)]
                Rt = [sbt(p3, f"p3_Rt{i}", [128, D], BF16) for i in range(2)]
                R2 = [sbt(p3, f"p3_R2{i}", [128, D], BF16) for i in range(2)]
                KtP = [sbt(p3, f"p3_KtP{i}", [128, D], BF16) for i in range(4)]
                R2P = [sbt(p3, f"p3_R2P{i}", [128, D], BF16) for i in range(4)]
                k16 = [sbt(swp, f"p3_k{i}", [128, D], BF16) for i in range(3)]
                ktok = [sbt(swp, f"p3_kt{i}", [128, D], BF16) for i in range(3)]
                vtok = [sbt(swp, f"p3_vt{i}", [128, D], BF16) for i in range(3)]
                St = sbt(swp, "p3_S", [128, 512], F32)
                S16 = sbt(swp, "p3_S16", [128, 512], BF16)
                Dn = sbt(swp, "p3_Dn", [128, 8], F32)
                auxsb = [sbt(swp, f"p3_aux{i}", [128, 32], F32) for i in range(2)]
                eW8 = [sbt(swp, f"p3_eW8{i}", [128, 8], F32) for i in range(2)]
                emid = [sbt(swp, f"p3_emid{i}", [128, 8], BF16) for i in range(2)]
                E1c = [sbt(swp, f"p3_E1c{i}", [128, D], BF16) for i in range(2)]
                Kt = [sbt(swp, f"p3_Kt{i}", [128, D], BF16) for i in range(2)]
                Kh = [sbt(swp, f"p3_Kh{i}", [128, D], BF16) for i in range(2)]
                colmask = sbt(swp, "p3_colmask", [128, 2048], BF16)
                sch.op("sp", lambda e: e.dma_start(
                    colmask[:, :], inp["colmask"].ap()),
                    writes=[("small", "colmask")])
                KhP = [sbt(swp, f"p3_KhP{i}", [128, D], BF16) for i in range(4)]
                Pt = [sbt(swp, f"p3_P{i}", [128, 2 * 1024], BF16) for i in range(2)]
                pA = p3.enter_context(nc.psum_tensor("pp3_A", [128, 1024], F32))
                pB = p3.enter_context(nc.psum_tensor("pp3_B", [128, 1024], F32))
                pP = p3.enter_context(nc.psum_tensor("pp3_P", [128, 512], F32))
                pY = p3.enter_context(nc.psum_tensor("pp3_Y", [128, 1024], F32))
                pSt = p3.enter_context(nc.psum_tensor("pp3_St", [128, 512], F32))

                def aux3(b, colidx):
                    return auxsb[b][:, :].rearrange("p (j c) -> p j c", c=4)[:, :, colidx]

                def sweep(fwd):
                    tri_c = smalls["tri_cc"] if fwd else smalls["tri_cb"]
                    tri_ck = "tri_cc" if fwd else "tri_cb"
                    tri_tok = smalls["tri_cs"] if fwd else smalls["tri_ci"]
                    tri_tk = "tri_cs" if fwd else "tri_ci"
                    mask = smalls["mask_f"] if fwd else smalls["mask_b"]
                    mkey = "mask_f" if fwd else "mask_b"
                    mid_col = 0 if fwd else 2
                    All = DnAll if fwd else BsAll
                    akey = "DnAll" if fwd else "BsAll"
                    chunks = (range(_P3_CHUNKS) if fwd
                              else range(_P3_CHUNKS - 1, -1, -1))
                    L = _P3_LEVEL

                    def op3(lvl, *a, **k):
                        if L >= lvl:
                            return sch.op(*a, **k)
                    op3(0, "dve", lambda e: e.memset(St[:, :], 0.0), writes=["St"])
                    op3(0, "dve", lambda e: e.memset(Dn[:, :], 0.0), writes=["Dn"])
                    for n in chunks:
                        b = n % 2
                        bl = n % 3
                        op3(0, "sp", lambda e, n=n, bl=bl: e.dma_start(
                            r16[bl][:, :], sc["r"].ap()[n]),
                            reads=[("scr", "r")], writes=[("r16", bl)])
                        op3(0, "sp", lambda e, n=n, bl=bl: e.dma_start(
                            k16[bl][:, :], sc["kf"].ap()[n]),
                            reads=[("scr", "kf")], writes=[("k16", bl)])
                        op3(0, "actdma", lambda e, n=n, bl=bl: e.dma_start(
                            ktok[bl][:, :], sc["k"].ap()[n], transpose=True),
                            reads=[("scr", "k")], writes=[("ktok", bl)])
                        op3(0, "sp", lambda e, n=n, bl=bl: e.dma_start(
                            vtok[bl][:, :], sc["v"].ap()[n * C:(n + 1) * C, :]),
                            reads=[("scr", "v")], writes=[("vtok", bl)])
                        op3(0, "sp", lambda e, n=n, bl=bl: e.dma_start(
                            ptok[bl][:, :], sc["p"].ap()[n * C:(n + 1) * C, :]),
                            reads=[("scr", "p")], writes=[("ptok", bl)])
                        for j in range(8):
                            op3(1, "pe", lambda e, j=j, bl=bl, tri_c=tri_c: e.matmul(
                                pA[:, j * 128:(j + 1) * 128],
                                ptok[bl][:, j * 128:(j + 1) * 128], tri_c[:, :],
                                start=(j % 4 == 0), stop=(j % 4 == 3)),
                                reads=[("ptok", bl), ("small", tri_ck)], writes=["pA"])
                        for j in range(8):
                            op3(1, "pe", lambda e, j=j, bl=bl: e.matmul(
                                pY[:, j * 4:(j + 1) * 4],
                                ptok[bl][:, j * 128:(j + 1) * 128],
                                smalls["aux_cols"][:, :],
                                start=(j == 0), stop=(j == 7)),
                                reads=[("ptok", bl), ("small", "aux_cols")],
                                writes=["pY"])
                        for half in range(2):
                            op3(1, "pe", lambda e, half=half, bl=bl, tri_tok=tri_tok: e.matmul(
                                pB[:, half * 512:half * 512 + 512], tri_tok[:, :],
                                ptok[bl][:, half * 512:half * 512 + 512],
                                start=True, stop=True),
                                reads=[("ptok", bl), ("small", tri_tk)], writes=["pB"])
                        op3(2, "dve", lambda e, b=b: e.tensor_copy(
                            auxsb[b][:, :], pY[:, 0:32]),
                            reads=["pY"], writes=[("aux", b)])
                        op3(2, "act", lambda e, mid_col=mid_col, b=b: e.activation(
                            emid[b][:, :], aux3(b, mid_col), AF.Exp),
                            reads=[("aux", b)], writes=[("emid", b)])
                        op3(2, "act", lambda e, b=b: e.activation(
                            eW8[b][:, :], aux3(b, 1), AF.Exp),
                            reads=[("aux", b)], writes=[("eW8", b)])
                        op3(2, "pool", lambda e, n=n, All=All: e.tensor_copy(
                            All[:, n * 8:(n + 1) * 8], Dn[:, :]),
                            reads=["Dn"], writes=[akey])
                        op3(2, "pool", lambda e, b=b: e.tensor_tensor(
                            Dn[:, :], Dn[:, :], aux3(b, 1), OP.add),
                            reads=["Dn", ("aux", b), akey], writes=["Dn"])
                        op3(3, "act", lambda e, b=b: e.activation(
                            E1c[b][:, :], pA[:, :], AF.Exp),
                            reads=["pA"], writes=[("E1c", b)])
                        op3(3, "act", lambda e, b=b: e.activation(
                            E1u[b][:, :], pA[:, :], AF.Exp, scale=ccol("neg1")),
                            reads=["pA", cread("neg1")], writes=[("E1u", b)])
                        op3(4, "dve", lambda e, b=b, bl=bl: e.tensor_tensor(
                            Kt[b][:, :], k16[bl][:, :], E1u[b][:, :], OP.mult),
                            reads=[("k16", bl), ("E1u", b)], writes=[("Kt", b)])
                        for par in range(2):
                            op3(5, "act", lambda e, par=par, b=b: e.activation(
                                KtP[b * 2 + par][:, :], Kt[b][:, :], AF.Identity,
                                scale=smalls["pmask"][:, par:par + 1]),
                                reads=[("Kt", b), ("small", "pmask")],
                                writes=[("KtP", b * 2 + par)])
                        op3(3, "act", lambda e, b=b: e.activation(
                            E3t[b][:, :], pB[:, :], AF.Exp),
                            reads=["pB"], writes=[("E3t", b)])
                        op3(4, "dve", lambda e, b=b, bl=bl: e.tensor_tensor(
                            Rt[b][:, :], r16[bl][:, :], E1c[b][:, :], OP.mult),
                            reads=[("r16", bl), ("E1c", b)], writes=[("Rt", b)])
                        op3(4, "dve", lambda e, b=b: e.tensor_tensor(
                            E1u[b][:, :].rearrange("p (a b) -> p a b", b=128),
                            E1c[b][:, :].rearrange("p (a b) -> p a b", b=128),
                            emid[b][:, :].unsqueeze(2).broadcast_to([128, 8, 128]),
                            OP.mult),
                            reads=[("E1c", b), ("emid", b)], writes=[("E1u", b)])
                        op3(4, "dve", lambda e, b=b, bl=bl: e.tensor_tensor(
                            R2[b][:, :], r16[bl][:, :], E1u[b][:, :], OP.mult),
                            reads=[("r16", bl), ("E1u", b)], writes=[("R2", b)])
                        for par in range(2):
                            op3(6, "dve", lambda e, par=par, b=b: e.tensor_scalar(
                                R2P[b * 2 + par][:, :], R2[b][:, :],
                                smalls["pmask"][:, par:par + 1], None, OP.mult),
                                reads=[("R2", b), ("small", "pmask")],
                                writes=[("R2P", b * 2 + par)])
                        op3(4, "dve", lambda e, b=b, bl=bl: e.tensor_tensor(
                            Kh[b][:, :], ktok[bl][:, :], E3t[b][:, :], OP.mult),
                            reads=[("ktok", bl), ("E3t", b)], writes=[("Kh", b)])
                        op3(7, "dve", lambda e, b=b: e.tensor_tensor(
                            KhP[b * 2][:, :], Kh[b][:, :],
                            colmask[:, 0:1024], OP.mult),
                            reads=[("Kh", b), ("small", "colmask")],
                            writes=[("KhP", b * 2)])
                        op3(7, "pool", lambda e, b=b: e.tensor_tensor(
                            KhP[b * 2 + 1][:, :], Kh[b][:, :],
                            colmask[:, 1024:2048], OP.mult),
                            reads=[("Kh", b), ("small", "colmask")],
                            writes=[("KhP", b * 2 + 1)])
                        op3(4, "act", lambda e: e.activation(
                            S16[:, :], St[:, :], AF.Copy),
                            reads=["St"], writes=["S16"])
                        for wave in range(2):
                            for hw_ in range(2):
                                for hh4 in range(4):
                                    hh = hw_ * 4 + hh4
                                    h = wave * 8 + hh
                                    par = h % 2
                                    jb = h // 2
                                    op3(5, "pe", lambda e, hh4=hh4, par=par, jb=jb, b=b: e.matmul(
                                        pP[:, hh4 * 128:(hh4 + 1) * 128],
                                        KtP[b * 2 + par][:, jb * 128:(jb + 1) * 128],
                                        Rt[b][:, jb * 128:(jb + 1) * 128],
                                        start=(hh4 == 0), stop=(hh4 == 3)),
                                        reads=[("KtP", b * 2 + par), ("Rt", b)],
                                        writes=["pP"])
                                op3(6, "dve", lambda e, wave=wave, hw_=hw_, mask=mask, b=b: e.tensor_tensor(
                                    Pt[b][:, wave * 1024 + hw_ * 512:
                                          wave * 1024 + hw_ * 512 + 512]
                                    .rearrange("p (a b) -> p a b", b=128),
                                    pP[:, :].rearrange("p (a b) -> p a b", b=128),
                                    mask[:, :].unsqueeze(1).broadcast_to([128, 4, 128]),
                                    OP.mult),
                                    reads=["pP", ("small", mkey)],
                                    writes=[("Pt", b, wave)])
                        for h in range(H):
                            wave, hh = h // 8, h % 8
                            jb = h // 2
                            op3(6, "pe", lambda e, h=h, wave=wave, hh=hh, b=b, bl=bl: e.matmul(
                                pY[:, h * 64:(h + 1) * 64],
                                Pt[b][:, wave * 1024 + hh * 128: wave * 1024 + (hh + 1) * 128],
                                vtok[bl][:, h * 64:(h + 1) * 64],
                                start=(h == 0 or h == 8), stop=False),
                                reads=[("Pt", b, wave), ("vtok", bl)], writes=["pY"])
                            op3(6, "pe", lambda e, h=h, jb=jb, b=b: e.matmul(
                                pY[:, h * 64:(h + 1) * 64],
                                R2P[b * 2 + h % 2][:, jb * 128:(jb + 1) * 128],
                                S16[:, jb * 64:(jb + 1) * 64],
                                start=False, stop=(h == 7 or h == 15)),
                                reads=[("R2P", b * 2 + h % 2), "S16"], writes=["pY"])
                        if fwd:
                            op3(6, "act", lambda e, n=n: e.activation(
                                ybuf[:, n * D:(n + 1) * D], pY[:, :], AF.Copy),
                                reads=["pY"], writes=[("y", n)])
                        else:
                            op3(6, "dve", lambda e, n=n: e.tensor_tensor(
                                ybuf[:, n * D:(n + 1) * D],
                                ybuf[:, n * D:(n + 1) * D], pY[:, :], OP.add),
                                reads=["pY", ("y", n)], writes=[("y", n)])
                        for h in range(H):
                            par = h % 2
                            jb = h // 2
                            op3(7, "pe", lambda e, h=h, b=b, bl=bl, par=par, jb=jb: e.matmul(
                                pSt[:, jb * 64:(jb + 1) * 64],
                                KhP[b * 2 + par][:, jb * 128:(jb + 1) * 128],
                                vtok[bl][:, h * 64:(h + 1) * 64],
                                start=(h == 0), stop=(h == 15)),
                                reads=[("KhP", b * 2 + par), ("vtok", bl)],
                                writes=["pSt"])
                        op3(8, "dve", lambda e, b=b: e.tensor_tensor(
                            Stmp[:, 0:256].rearrange("p (a b) -> p a b", b=64),
                            St[:, 0:256].rearrange("p (a b) -> p a b", b=64),
                            eW8[b][:, 0:4].unsqueeze(2).broadcast_to([128, 4, 64]),
                            OP.mult),
                            reads=["St", ("eW8", b), "S16"], writes=["Stmp"])
                        op3(8, "pool", lambda e, b=b: e.tensor_tensor(
                            Stmp[:, 256:512].rearrange("p (a b) -> p a b", b=64),
                            St[:, 256:512].rearrange("p (a b) -> p a b", b=64),
                            eW8[b][:, 4:8].unsqueeze(2).broadcast_to([128, 4, 64]),
                            OP.mult),
                            reads=["St", ("eW8", b), "S16"], writes=["Stmp"])
                        op3(8, "dve", lambda e: e.tensor_tensor(
                            St[:, :], Stmp[:, :], pSt[:, :], OP.add),
                            reads=["Stmp", "pSt"], writes=["St"])
                    rows0 = 0 if fwd else 1024
                    op3(9, "sp", lambda e, rows0=rows0: e.dma_start(
                        sc["ccin"].ap()[rows0:rows0 + 1024, :]
                        .rearrange("(j p) v -> p j v", p=128),
                        St[:, :].rearrange("p (j v) -> p j v", v=64)),
                        reads=["St"], writes=["ccin"])
                    if not fwd:
                        op3(9, "sp", lambda e: e.dma_start(
                            sc["ccin"].ap()[2048:2064, :]
                            .rearrange("t v -> (t v)")
                            .rearrange("(jj p) -> p jj", p=128),
                            Dn[:, :], allow_slow_non_contiguous=True),
                            reads=["Dn"], writes=["ccin"])

                sweep(True)
                sch.dma_drain()
                with nc.Block() as blk:
                    sch.emit(blk)
                sweep(False)
                sch.dma_drain()
                with nc.Block() as blk:
                    sch.emit(blk)
                swp.close()

                if 4 in _PHASES:
                    Sin = sbt(p3, "p3_Sin", [128, 512], F32)
                    Tin = sbt(p3, "p3_Tin", [128, 512], F32)
                    Sg = [sbt(p3, f"p3_Sg{i}", [128, 512], F32) for i in range(2)]
                    Wl = sbt(p3, "p3_Wl", [128, 8 * 8], F32)
                    runv = sbt(p3, "p3_run", [128, 8], F32)
                    eDcC = sbt(p3, "p3_eDcC", [128, 8], F32)
                    eDc = [sbt(p3, f"p3_eDc{i}", [128, 8], F32) for i in range(2)]
                    eDc2 = [sbt(p3, f"p3_eDc2{i}", [128, 8], F32) for i in range(2)]
                    Sn16 = [sbt(p3, f"p3_Sn16{i}", [128, 512], BF16) for i in range(2)]
                    Tn16 = [sbt(p3, f"p3_Tn16{i}", [128, 512], BF16) for i in range(2)]
                    sigt = [sbt(p3, f"p5_sig{i}", [128, D], F32) for i in range(2)]
                    gt = [sbt(p3, f"p5_g{i}", [128, D], F32) for i in range(2)]
                    lnw = sbt(p3, "p5_lnw", [128, D], F32)
                    lnb = sbt(p3, "p5_lnb", [128, D], F32)
                    tYs = [sbt(p3, f"p5_tY{i}", [128, D], F32) for i in range(2)]
                    tSqs = [sbt(p3, f"p5_tSq{i}", [128, D], F32) for i in range(2)]
                    st16s = [sbt(p3, f"p5_st{i}", [128, 64], F32) for i in range(2)]
                    zt = [sbt(p3, f"p5_z{i}", [128, D], F32) for i in range(2)]
                    lwg = [sbt(p3, f"p5_lwg{i}", [128, D], F32) for i in range(2)]
                    lbg = [sbt(p3, f"p5_lbg{i}", [128, D], F32) for i in range(2)]
                    if 5 in _PHASES:
                        sch.op("actdma", lambda e: e.dma_start(
                            lnw[:, :], inp["ln_w_rep"].ap()), writes=["lnw"])
                        sch.op("actdma", lambda e: e.dma_start(
                            lnb[:, :], inp["ln_b_rep"].ap()), writes=["lnb"])

                    sch.op("pool", lambda e: e.collective_compute(
                        "AllGather", OP.bypass,
                        replica_groups=[list(range(N_CORES))],
                        ins=[sc["ccin"].ap().opt()],
                        outs=[sc["ccout"].ap().opt()]),
                        reads=["ccin"], writes=["ccout"], force_inc=True)
                    for cp in range(N_CORES):
                        sch.op("sp", lambda e, cp=cp: e.dma_start(
                            Wl[:, cp * 8:(cp + 1) * 8],
                            sc["ccout"].ap()[cp * 2064 + 2048: cp * 2064 + 2064, :]
                            .rearrange("t v -> (t v)")
                            .rearrange("(jj p) -> p jj", p=128),
                            allow_slow_non_contiguous=True),
                            reads=["ccout"], writes=["Wl"])

                    def combine(out_t, maskname, order):
                        okey = ("cmb", maskname)
                        sch.op("dve", lambda e, out_t=out_t: e.memset(out_t[:, :], 0.0),
                               writes=[okey])
                        sch.op("dve", lambda e: e.memset(runv[:, :], 0.0),
                               writes=["runv"])
                        base = 0 if maskname == "cmask_f" else 1024
                        for cp in order:
                            bb = cp % 2
                            sch.op("sp", lambda e, cp=cp, bb=bb, base=base: e.dma_start(
                                Sg[bb][:, :].rearrange("p (j v) -> p j v", v=64),
                                sc["ccout"].ap()[cp * 2064 + base: cp * 2064 + base + 1024, :]
                                .rearrange("(j p) v -> p j v", p=128)),
                                reads=["ccout"], writes=[("Sg", bb)])
                            sch.op("act", lambda e: e.activation(
                                eDcC[:, :], runv[:, :], AF.Exp),
                                reads=["runv"], writes=["eDcC"])
                            sch.op("dve", lambda e, cp=cp, maskname=maskname: e.tensor_scalar(
                                eDcC[:, :], eDcC[:, :],
                                smalls[maskname][:, cp:cp + 1], None, OP.mult),
                                reads=["eDcC", ("small", maskname)], writes=["eDcC"])
                            sch.op("dve", lambda e, bb=bb: e.tensor_tensor(
                                Stmp[:, :].rearrange("p (a b) -> p a b", b=64),
                                Sg[bb][:, :].rearrange("p (a b) -> p a b", b=64),
                                eDcC[:, :].unsqueeze(2).broadcast_to([128, 8, 64]),
                                OP.mult), reads=[("Sg", bb), "eDcC"], writes=["Stmp"])
                            sch.op("dve", lambda e, out_t=out_t, okey=okey: e.tensor_tensor(
                                out_t[:, :], out_t[:, :], Stmp[:, :], OP.add),
                                reads=["Stmp", okey], writes=[okey])
                            sch.op("dve", lambda e, cp=cp, maskname=maskname: e.scalar_tensor_tensor(
                                runv[:, :], Wl[:, cp * 8:(cp + 1) * 8],
                                smalls[maskname][:, cp:cp + 1], runv[:, :],
                                OP.mult, OP.add),
                                reads=["Wl", ("small", maskname), "runv"],
                                writes=["runv"])

                    if _P4_LEVEL >= 2:
                        combine(Sin, "cmask_f", list(range(N_CORES - 1, -1, -1)))
                        combine(Tin, "cmask_b", list(range(N_CORES)))
                    sch.dma_drain()
                    with nc.Block() as blk:
                        sch.emit(blk)

                    L4 = _P4_LEVEL - 2

                    def op4(lvl, *a, **k):
                        if L4 >= lvl:
                            return sch.op(*a, **k)
                    for n in (range(_P3_CHUNKS) if _P4_LEVEL >= 3 else []):
                        b = n % 2
                        bl = n % 3
                        op4(1, "sp", lambda e, n=n, bl=bl: e.dma_start(
                            r16[bl][:, :], sc["r"].ap()[n]),
                            reads=[("scr", "r")], writes=[("r16", bl)])
                        op4(1, "sp", lambda e, n=n, bl=bl: e.dma_start(
                            ptok[bl][:, :], sc["p"].ap()[n * C:(n + 1) * C, :]),
                            reads=[("scr", "p")], writes=[("ptok", bl)])
                        for j in range(8):
                            op4(2, "pe", lambda e, j=j, bl=bl: e.matmul(
                                pA[:, j * 128:(j + 1) * 128],
                                ptok[bl][:, j * 128:(j + 1) * 128],
                                smalls["tri_ci"][:, :],
                                start=(j % 4 == 0), stop=(j % 4 == 3)),
                                reads=[("ptok", bl), ("small", "tri_ci")],
                                writes=["pA"])
                        for j in range(8):
                            op4(2, "pe", lambda e, j=j, bl=bl: e.matmul(
                                pB[:, j * 128:(j + 1) * 128],
                                ptok[bl][:, j * 128:(j + 1) * 128],
                                smalls["tri_cs"][:, :],
                                start=(j % 4 == 0), stop=(j % 4 == 3)),
                                reads=[("ptok", bl), ("small", "tri_cs")],
                                writes=["pB"])
                        op4(2, "act", lambda e, b=b: e.activation(
                            E1u[b][:, :], pA[:, :], AF.Exp),
                            reads=["pA"], writes=[("E1u", b)])
                        op4(2, "act", lambda e, b=b: e.activation(
                            E3t[b][:, :], pB[:, :], AF.Exp),
                            reads=["pB"], writes=[("E3t", b)])
                        op4(3, "dve", lambda e, b=b, bl=bl: e.tensor_tensor(
                            R2[b][:, :], r16[bl][:, :], E1u[b][:, :], OP.mult),
                            reads=[("r16", bl), ("E1u", b)], writes=[("R2", b)])
                        op4(3, "dve", lambda e, b=b, bl=bl: e.tensor_tensor(
                            Rt[b][:, :], r16[bl][:, :], E3t[b][:, :], OP.mult),
                            reads=[("r16", bl), ("E3t", b)], writes=[("Rt", b)])
                        for par in range(2):
                            op4(3, "dve", lambda e, par=par, b=b: e.tensor_scalar(
                                R2P[b * 2 + par][:, :], R2[b][:, :],
                                smalls["pmask"][:, par:par + 1], None, OP.mult),
                                reads=[("R2", b), ("small", "pmask")],
                                writes=[("R2P", b * 2 + par)])
                            op4(3, "act", lambda e, par=par, b=b: e.activation(
                                KtP[b * 2 + par][:, :], Rt[b][:, :], AF.Identity,
                                scale=smalls["pmask"][:, par:par + 1]),
                                reads=[("Rt", b), ("small", "pmask")],
                                writes=[("KtP", b * 2 + par)])
                        op4(4, "act", lambda e, n=n, b=b: e.activation(
                            eDc[b][:, :], DnAll[:, n * 8:(n + 1) * 8], AF.Exp),
                            reads=["DnAll"], writes=[("eDc", b)])
                        op4(4, "act", lambda e, n=n, b=b: e.activation(
                            eDc2[b][:, :], BsAll[:, n * 8:(n + 1) * 8], AF.Exp),
                            reads=["BsAll"], writes=[("eDc2", b)])
                        op4(4, "dve", lambda e, b=b: e.tensor_tensor(
                            Sn16[b][:, :].rearrange("p (a b) -> p a b", b=64),
                            Sin[:, :].rearrange("p (a b) -> p a b", b=64),
                            eDc[b][:, :].unsqueeze(2).broadcast_to([128, 8, 64]),
                            OP.mult),
                            reads=[("cmb", "cmask_f"), ("eDc", b)],
                            writes=[("Sn16", b)])
                        op4(4, "dve", lambda e, b=b: e.tensor_tensor(
                            Tn16[b][:, :].rearrange("p (a b) -> p a b", b=64),
                            Tin[:, :].rearrange("p (a b) -> p a b", b=64),
                            eDc2[b][:, :].unsqueeze(2).broadcast_to([128, 8, 64]),
                            OP.mult),
                            reads=[("cmb", "cmask_b"), ("eDc2", b)],
                            writes=[("Tn16", b)])
                        for h in range(H):
                            jb = h // 2
                            op4(5, "pe", lambda e, h=h, jb=jb, b=b: e.matmul(
                                pY[:, h * 64:(h + 1) * 64],
                                R2P[b * 2 + h % 2][:, jb * 128:(jb + 1) * 128],
                                Sn16[b][:, jb * 64:(jb + 1) * 64],
                                start=(h == 0 or h == 8), stop=False),
                                reads=[("R2P", b * 2 + h % 2), ("Sn16", b)],
                                writes=["pY"])
                            op4(5, "pe", lambda e, h=h, jb=jb, b=b: e.matmul(
                                pY[:, h * 64:(h + 1) * 64],
                                KtP[b * 2 + h % 2][:, jb * 128:(jb + 1) * 128],
                                Tn16[b][:, jb * 64:(jb + 1) * 64],
                                start=False, stop=(h == 7 or h == 15)),
                                reads=[("KtP", b * 2 + h % 2), ("Tn16", b)],
                                writes=["pY"])
                        op4(6, "act", lambda e, b=b: e.activation(
                            E1u[b][:, :], pY[:, :], AF.Identity, bias=ccol("zero")),
                            reads=["pY", cread("zero")], writes=[("E1u", b)])
                        op4(6, "dve", lambda e, n=n, b=b: e.tensor_tensor(
                            ybuf[:, n * D:(n + 1) * D],
                            ybuf[:, n * D:(n + 1) * D], E1u[b][:, :], OP.add),
                            reads=[("E1u", b), ("y", n)], writes=[("y", n)])
                        if 5 not in _PHASES or L4 < 6:
                            continue
                        tY, tSq, st16 = tYs[b], tSqs[b], st16s[b]
                        kY, kSq, kSt = ("tY", b), ("tSq", b), ("st", b)
                        sch.op("sp", lambda e, n=n, b=b: e.dma_start(
                            sigt[b][:, :], sc["sig"].ap()[n * C:(n + 1) * C, :]),
                            reads=[("scr", "sig")], writes=[("sigt", b)])
                        sch.op("sp", lambda e, n=n, b=b: e.dma_start(
                            gt[b][:, :], sc["g"].ap()[n * C:(n + 1) * C, :]),
                            reads=[("scr", "g")], writes=[("gt", b)])
                        sch.op("pool", lambda e, b=b: e.tensor_tensor(
                            lwg[b][:, :], lnw[:, :], gt[b][:, :], OP.mult),
                            reads=["lnw", ("gt", b)], writes=[("lwg", b)])
                        sch.op("pool", lambda e, b=b: e.tensor_tensor(
                            lbg[b][:, :], lnb[:, :], gt[b][:, :], OP.mult),
                            reads=["lnb", ("gt", b)], writes=[("lbg", b)])
                        sch.op("dve", lambda e, n=n, b=b, tY=tY: e.tensor_tensor(
                            tY[:, :], ybuf[:, n * D:(n + 1) * D], sigt[b][:, :],
                            OP.mult),
                            reads=[("y", n), ("sigt", b)], writes=[kY])
                        sch.op("act", lambda e, tY=tY, tSq=tSq: e.activation(
                            tSq[:, :], tY[:, :], AF.Square), reads=[kY],
                            writes=[kSq])
                        sch.op("dve", lambda e, tY=tY, st16=st16: e.reduce_sum(
                            st16[:, 0:16], tY[:, :].rearrange("p (g k) -> p g k", k=64),
                            axis=AX), reads=[kY], writes=[kSt])
                        sch.op("dve", lambda e, tSq=tSq, st16=st16: e.reduce_sum(
                            st16[:, 16:32],
                            tSq[:, :].rearrange("p (g k) -> p g k", k=64),
                            axis=AX), reads=[kSq], writes=[kSt])
                        sch.op("dve", lambda e, st16=st16: e.tensor_scalar(
                            st16[:, 32:48], st16[:, 0:16], ccol("i64"), None, OP.mult),
                            reads=[kSt, cread("i64")], writes=[kSt])
                        sch.op("dve", lambda e, st16=st16: e.tensor_tensor(
                            st16[:, 48:64], st16[:, 32:48], st16[:, 32:48], OP.mult),
                            reads=[kSt], writes=[kSt])
                        sch.op("dve", lambda e, st16=st16: e.scalar_tensor_tensor(
                            st16[:, 48:64], st16[:, 16:32], ccol("i64"),
                            st16[:, 48:64], OP.mult, OP.subtract),
                            reads=[kSt, cread("i64")], writes=[kSt])
                        sch.op("act", lambda e, st16=st16: e.activation(
                            st16[:, 48:64], st16[:, 48:64], AF.Ln,
                            bias=ccol("eps_gn")),
                            reads=[kSt, cread("eps_gn")], writes=[kSt])
                        sch.op("act", lambda e, st16=st16: e.activation(
                            st16[:, 48:64], st16[:, 48:64], AF.Exp,
                            scale=ccol("neg_half")),
                            reads=[kSt, cread("neg_half")], writes=[kSt])
                        sch.op("dve", lambda e, tY=tY, st16=st16: e.tensor_tensor(
                            tY[:, :].rearrange("p (g k) -> p g k", k=64),
                            tY[:, :].rearrange("p (g k) -> p g k", k=64),
                            st16[:, 32:48].unsqueeze(2).broadcast_to([128, 16, 64]),
                            OP.subtract), reads=[kSt, kY], writes=[kY])
                        sch.op("dve", lambda e, tY=tY, st16=st16: e.tensor_tensor(
                            tY[:, :].rearrange("p (g k) -> p g k", k=64),
                            tY[:, :].rearrange("p (g k) -> p g k", k=64),
                            st16[:, 48:64].unsqueeze(2).broadcast_to([128, 16, 64]),
                            OP.mult), reads=[kSt, kY], writes=[kY])
                        sch.op("dve", lambda e, b=b, tY=tY: e.tensor_tensor(
                            tY[:, :], tY[:, :], lwg[b][:, :], OP.mult),
                            reads=[kY, ("lwg", b)], writes=[kY])
                        sch.op("dve", lambda e, b=b, tY=tY: e.tensor_tensor(
                            zt[b][:, :], tY[:, :], lbg[b][:, :], OP.add),
                            reads=[kY, ("lbg", b)], writes=[("zt", b)])
                        sch.op("pooldma", lambda e, n=n, b=b: e.dma_start(
                            sc["zb"].ap()[n * C:(n + 1) * C, :], zt[b][:, :]),
                            reads=[("zt", b)], writes=[("scr", "zb")])
                    sch.dma_drain()
                    with nc.Block() as blk:
                        sch.emit(blk)
                p3.close()

            if "sc_y" in _DEBUG_SCRATCH and ybuf is not None:
                for n in range(NCH):
                    sch.op("sp", lambda e, n=n: e.dma_start(
                        sc["y"].ap()[n * C:(n + 1) * C, :],
                        ybuf[:, n * D:(n + 1) * D]),
                        reads=[("y", n)], force_inc=True)
                sch.dma_drain()
                with nc.Block() as blk:
                    sch.emit(blk)

            if 5 in _PHASES:
                with contextlib.ExitStack() as phs:
                    zf = sbt(phs, "p5_zf", [128, 8 * TC], BF16)
                    wo = sbt(phs, "p5_wo", [128, 8 * D], BF16)
                    xtt = [sbt(phs, f"p5_xt{i}", [128, D], F32) for i in range(2)]
                    x2t = [sbt(phs, f"p5_x2{i}", [128, D], F32) for i in range(2)]
                    sqWs = [sbt(phs, f"p5_sqW{i}", [128, D], F32) for i in range(2)]
                    lnsts = [sbt(phs, f"p5_lnst{i}", [128, 4], F32) for i in range(2)]
                    ppw = [phs.enter_context(
                        nc.psum_tensor(f"pp5_{i}", [128, 512], F32)) for i in range(4)]
                    sch.op("pooldma", lambda e: e.dma_start(wo[:, :], inp["Wo"].ap()),
                           writes=["wo"])
                    for jb in range(8):
                        sch.op("actdma", lambda e, jb=jb: e.dma_start(
                            zf[:, jb * TC:(jb + 1) * TC],
                            sc["zb"].ap()[:, jb * 128:(jb + 1) * 128], transpose=True),
                            reads=[("scr", "zb")], writes=[("zf", jb)])
                    for n in range(NCH):
                        b = n % 2
                        sqW, lnst = sqWs[b], lnsts[b]
                        kSq, kLn = ("sqW", b), ("lnst", b)
                        sch.op("sp", lambda e, n=n, b=b: e.dma_start(
                            xtt[b][:, :], inp["xt"].ap()[n * C:(n + 1) * C, :]),
                            writes=[("xtt", b)])
                        for half in range(2):
                            pb = ppw[2 * b + half]
                            for ic in range(8):
                                sch.op("pe", lambda e, n=n, half=half, ic=ic, pb=pb: e.matmul(
                                    pb[:, :],
                                    zf[:, ic * TC + n * 128: ic * TC + (n + 1) * 128],
                                    wo[:, ic * D + half * 512: ic * D + half * 512 + 512],
                                    start=(ic == 0), stop=(ic == 7)),
                                    reads=[("zf", ic), "wo"],
                                    writes=[("ppw", 2 * b + half)])
                            sch.op("dve", lambda e, half=half, b=b, pb=pb: e.tensor_tensor(
                                x2t[b][:, half * 512:half * 512 + 512], pb[:, :],
                                xtt[b][:, half * 512:half * 512 + 512], OP.add),
                                reads=[("ppw", 2 * b + half), ("xtt", b)],
                                writes=[("x2t", b)])
                        sch.op("dve", lambda e, b=b, lnst=lnst: e.reduce_sum(
                            lnst[:, 0:1], x2t[b][:, :], axis=AX),
                            reads=[("x2t", b)], writes=[kLn])
                        sch.op("dve", lambda e, lnst=lnst: e.tensor_scalar(
                            lnst[:, 0:1], lnst[:, 0:1], ccol("i1024"), None, OP.mult),
                            reads=[kLn, cread("i1024")], writes=[kLn])
                        sch.op("dve", lambda e, b=b, lnst=lnst: e.tensor_scalar(
                            x2t[b][:, :], x2t[b][:, :], lnst[:, 0:1], None,
                            OP.subtract),
                            reads=[("x2t", b), kLn], writes=[("x2t", b)])
                        sch.op("act", lambda e, b=b, sqW=sqW: e.activation(
                            sqW[:, :], x2t[b][:, :], AF.Square),
                            reads=[("x2t", b)], writes=[kSq])
                        sch.op("dve", lambda e, sqW=sqW, lnst=lnst: e.reduce_sum(
                            lnst[:, 1:2], sqW[:, :], axis=AX),
                            reads=[kSq], writes=[kLn])
                        sch.op("dve", lambda e, lnst=lnst: e.tensor_scalar(
                            lnst[:, 1:2], lnst[:, 1:2], ccol("i1024"), None, OP.mult),
                            reads=[kLn, cread("i1024")], writes=[kLn])
                        sch.op("act", lambda e, lnst=lnst: e.activation(
                            lnst[:, 1:2], lnst[:, 1:2], AF.Ln, bias=ccol("eps_ln")),
                            reads=[kLn, cread("eps_ln")], writes=[kLn])
                        sch.op("act", lambda e, lnst=lnst: e.activation(
                            lnst[:, 1:2], lnst[:, 1:2], AF.Exp,
                            scale=ccol("neg_half")),
                            reads=[kLn, cread("neg_half")], writes=[kLn])
                        sch.op("dve", lambda e, b=b, lnst=lnst: e.tensor_scalar(
                            x2t[b][:, :], x2t[b][:, :], lnst[:, 1:2], None, OP.mult),
                            reads=[("x2t", b), kLn], writes=[("x2t", b)])
                        sch.op("actdma", lambda e, n=n, b=b: e.dma_start(
                            sc["x2"].ap()[n * C:(n + 1) * C, :], x2t[b][:, :]),
                            reads=[("x2t", b)], writes=[("scr", "x2")])
                        sch.op("pooldma", lambda e, n=n, b=b: e.dma_start(
                            sc["x2b"].ap()[n * C:(n + 1) * C, :], x2t[b][:, :]),
                            reads=[("x2t", b)], writes=[("scr", "x2b")])
                    sch.dma_drain()
                    with nc.Block() as blk:
                        sch.emit(blk)

        if 6 in _PHASES:
            with contextlib.ExitStack() as phs:
                TT6 = 1024
                x2f = sbt(phs, "p6_x2f", [128, 8 * TT6], BF16)
                hsb = sbt(phs, "p6_h", [128, 32 * TT6], BF16)
                usb = sbt(phs, "p6_u", [128, 32 * TT6], BF16)
                wA = [sbt(phs, f"p6_wA{i}", [128, DH], BF16) for i in range(2)]
                wB = [sbt(phs, f"p6_wB{i}", [128, D], BF16) for i in range(3)]
                w1s = [sbt(phs, f"p6_w1s{i}", [128, D], BF16) for i in range(3)]
                w3blk = [sbt(phs, f"p6_w3b{i}", [128, 1024], BF16) for i in range(3)]
                s1s = sbt(phs, "p6_s1", [128, 32], F32)
                s2s = sbt(phs, "p6_s2", [128, 32], F32)
                sh1s = sbt(phs, "p6_sh1", [128, 32], F32)
                sh2s = sbt(phs, "p6_sh2", [128, 32], F32)
                s3r = sbt(phs, "p6_s3r", [128, D], F32)
                silu_ts = [sbt(phs, f"p6_silu{i}", [128, 512], F32) for i in range(2)]
                x2l = [sbt(phs, "p6_x2l0", [128, D], F32)]
                otW = sbt(phs, "p6_otW", [128, 2 * D], F32)
                sq6 = sbt(phs, "p6_sq", [128, D], F32)
                lnst6 = sbt(phs, "p6_lnst", [128, 4], F32)
                pf = [phs.enter_context(
                    nc.psum_tensor(f"pp6_{i}", [128, 512], F32)) for i in range(8)]
                for nm, t_ in [("s1c", s1s), ("s2c", s2s), ("sh1c", sh1s),
                               ("sh2c", sh2s)]:
                    sch.op("sp", lambda e, nm=nm, t_=t_: e.dma_start(
                        t_[:, :], inp[nm].ap()), writes=[("p6s", nm)])
                sch.op("sp", lambda e: e.dma_start(s3r[:, :], inp["s3_rep"].ap()),
                       writes=["s3r"])

                for tt6 in range(TC // TT6):
                    for jb in range(8):
                        sch.op("actdma", lambda e, tt6=tt6, jb=jb: e.dma_start(
                            x2f[:, jb * TT6:(jb + 1) * TT6],
                            sc["x2b"].ap()[tt6 * TT6:(tt6 + 1) * TT6,
                                           jb * 128:(jb + 1) * 128], transpose=True),
                            reads=[("scr", "x2b")], writes=[("x2f", jb)])
                    for oc in range(32):
                        bw = oc % 3
                        sch.op("pooldma", lambda e, oc=oc, bw=bw: e.dma_start(
                            w1s[bw][:, :], inp["w1b"].ap()[oc]),
                            writes=[("w1s", bw)])
                        sch.op("pooldma", lambda e, oc=oc, bw=bw: e.dma_start(
                            wB[bw][:, :], inp["w2b"].ap()[oc]),
                            writes=[("wB", bw)])
                        po = (oc % 2) * 4
                        for ic in range(8):
                            for half in range(2):
                                sch.op("pe", lambda e, half=half, ic=ic, bw=bw, po=po: e.matmul(
                                    pf[po + half][:, :],
                                    w1s[bw][:, ic * 128:(ic + 1) * 128],
                                    x2f[:, ic * TT6 + half * 512: ic * TT6 + half * 512 + 512],
                                    start=(ic == 0), stop=(ic == 7)),
                                    reads=[("w1s", bw), ("x2f", ic)],
                                    writes=[("pf", po + half)])
                            for half in range(2):
                                sch.op("pe", lambda e, half=half, ic=ic, bw=bw, po=po: e.matmul(
                                    pf[po + 2 + half][:, :],
                                    wB[bw][:, ic * 128:(ic + 1) * 128],
                                    x2f[:, ic * TT6 + half * 512: ic * TT6 + half * 512 + 512],
                                    start=(ic == 0), stop=(ic == 7)),
                                    reads=[("wB", bw), ("x2f", ic)],
                                    writes=[("pf", po + 2 + half)])
                        for half in range(2):
                            st_ = silu_ts[(oc + half) % 2]
                            sch.op("act", lambda e, oc=oc, half=half, po=po, st_=st_: e.activation(
                                st_[:, :], pf[po + half][:, :], AF.Silu,
                                scale=s1s[:, oc:oc + 1]),
                                reads=[("pf", po + half), ("p6s", "s1c")],
                                writes=[("silu_t", (oc + half) % 2)])
                            sch.op("dve", lambda e, oc=oc, half=half, po=po, st_=st_: e.scalar_tensor_tensor(
                                hsb[:, oc * TT6 + half * 512: oc * TT6 + half * 512 + 512],
                                pf[po + 2 + half][:, :], s2s[:, oc:oc + 1],
                                st_[:, :],
                                OP.mult, OP.mult),
                                reads=[("pf", po + 2 + half), ("p6s", "s2c"),
                                       ("silu_t", (oc + half) % 2)],
                                writes=[("hsb", oc)])
                    for stage, (wkey, src, dst, dkey, scl, sclk) in enumerate([
                            ("h1b", "hsb", usb, "usb", sh1s, "sh1c"),
                            ("h2b", "usb", hsb, "hsb", sh2s, "sh2c")]):
                        srcsb = hsb if src == "hsb" else usb
                        for oc in range(32):
                            bw = oc % 2
                            sch.op("pooldma", lambda e, oc=oc, bw=bw, wkey=wkey: e.dma_start(
                                wA[bw][:, :], inp[wkey].ap()[oc]),
                                writes=[("wA", bw)])
                            boff = 4 - 4 * stage
                            for ic in range(32):
                                for half in range(2):
                                    pi = boff + (oc % 2) * 2 + half
                                    sch.op("pe", lambda e, half=half, ic=ic, pi=pi, bw=bw, srcsb=srcsb: e.matmul(
                                        pf[pi][:, :], wA[bw][:, ic * 128:(ic + 1) * 128],
                                        srcsb[:, ic * TT6 + half * 512: ic * TT6 + half * 512 + 512],
                                        start=(ic == 0), stop=(ic == 31)),
                                        reads=[("wA", bw), (src, ic)],
                                        writes=[("pf", pi)])
                            for half in range(2):
                                pi = boff + (oc % 2) * 2 + half
                                sch.op("act", lambda e, oc=oc, half=half, pi=pi, dst=dst, scl=scl: e.activation(
                                    dst[:, oc * TT6 + half * 512: oc * TT6 + half * 512 + 512],
                                    pf[pi][:, :], AF.Gelu, scale=scl[:, oc:oc + 1]),
                                    reads=[("pf", pi), ("p6s", sclk)],
                                    writes=[(dkey, oc)])
                    for wave in range(2):
                        for ic in range(32):
                            wb3 = ic % 3
                            sch.op("pooldma", lambda e, ic=ic, wb3=wb3: e.dma_start(
                                w3blk[wb3][:, :],
                                inp["w3t"].ap()[:, ic * D:(ic + 1) * D]),
                                writes=[("w3b", wb3)])
                            for q in range(4):
                                tc8 = wave * 4 + q
                                for half in range(2):
                                    sch.op("pe", lambda e, tc8=tc8, ic=ic, q=q, half=half, wb3=wb3: e.matmul(
                                        pf[q * 2 + half][:, :],
                                        hsb[:, ic * TT6 + tc8 * 128: ic * TT6 + (tc8 + 1) * 128],
                                        w3blk[wb3][:, half * 512:half * 512 + 512],
                                        start=(ic == 0), stop=(ic == 31)),
                                        reads=[("hsb", ic), ("w3b", wb3)],
                                        writes=[("pf", q * 2 + half)])
                        for q in range(4):
                            tc8 = wave * 4 + q
                            n = tt6 * (TT6 // 128) + tc8
                            b = n % 2
                            ow = otW[:, (q % 2) * D:(q % 2 + 1) * D]
                            for half in range(2):
                                sch.op("dve", lambda e, q=q, half=half: e.tensor_tensor(
                                    otW[:, (q % 2) * D + half * 512:
                                        (q % 2) * D + half * 512 + 512],
                                    pf[q * 2 + half][:, :],
                                    s3r[:, half * 512:half * 512 + 512], OP.mult),
                                    reads=[("pf", q * 2 + half), "s3r"],
                                    writes=[("otW", q % 2)])
                            sch.op("sp", lambda e, n=n: e.dma_start(
                                x2l[0][:, :], sc["x2"].ap()[n * C:(n + 1) * C, :]),
                                reads=[("scr", "x2")], writes=["x2l"])
                            sch.op("dve", lambda e, ow=ow: e.tensor_tensor(
                                ow, ow, x2l[0][:, :], OP.add),
                                reads=[("otW", q % 2), "x2l"], writes=[("otW", q % 2)])
                            sch.op("dve", lambda e, ow=ow: e.reduce_sum(
                                lnst6[:, 0:1], ow, axis=AX),
                                reads=[("otW", q % 2)], writes=["lnst6"])
                            sch.op("dve", lambda e: e.tensor_scalar(
                                lnst6[:, 0:1], lnst6[:, 0:1], ccol("i1024"), None,
                                OP.mult),
                                reads=["lnst6", cread("i1024")], writes=["lnst6"])
                            sch.op("dve", lambda e, ow=ow: e.tensor_scalar(
                                ow, ow, lnst6[:, 0:1], None, OP.subtract),
                                reads=[("otW", q % 2), "lnst6"], writes=[("otW", q % 2)])
                            sch.op("act", lambda e, ow=ow: e.activation(
                                sq6[:, :], ow, AF.Square),
                                reads=[("otW", q % 2)], writes=["sq6"])
                            sch.op("dve", lambda e: e.reduce_sum(
                                lnst6[:, 1:2], sq6[:, :], axis=AX),
                                reads=["sq6"], writes=["lnst6"])
                            sch.op("dve", lambda e: e.tensor_scalar(
                                lnst6[:, 1:2], lnst6[:, 1:2], ccol("i1024"), None,
                                OP.mult),
                                reads=["lnst6", cread("i1024")], writes=["lnst6"])
                            sch.op("act", lambda e: e.activation(
                                lnst6[:, 1:2], lnst6[:, 1:2], AF.Sqrt,
                                bias=ccol("eps_ln")),
                                reads=["lnst6", cread("eps_ln")], writes=["lnst6"])
                            sch.op("dve", lambda e: e.reciprocal(
                                lnst6[:, 1:2], lnst6[:, 1:2]),
                                reads=["lnst6"], writes=["lnst6"])
                            sch.op("dve", lambda e, ow=ow: e.tensor_scalar(
                                ow, ow, lnst6[:, 1:2], None, OP.mult),
                                reads=[("otW", q % 2), "lnst6"], writes=[("otW", q % 2)])
                            sch.op("sp", lambda e, n=n, ow=ow: e.dma_start(
                                out.ap()[n * C:(n + 1) * C, :], ow),
                                reads=[("otW", q % 2)], force_inc=True)
                sch.dma_drain()
                with nc.Block() as blk:
                    sch.emit(blk)

        with nc.Block() as blk:
            def final_waits(q):
                n_dma = sch.dma_count[q]
                vals = []
                for idx in range(_DMA_SUBSEMS):
                    cnt = (n_dma - idx + _DMA_SUBSEMS - 1) // _DMA_SUBSEMS
                    if cnt > 0:
                        vals.append((idx, 16 * cnt))
                return vals

            sp_waits = final_waits("sp")
            pd_waits = final_waits("pooldma")
            ad_waits = final_waits("actdma")

            @blk.sync
            def _(e):
                for idx, v in sp_waits:
                    e.wait_ge(dma_sems["sp"][idx], v)

            @blk.gpsimd
            def _(e):
                for idx, v in pd_waits:
                    e.wait_ge(dma_sems["pooldma"][idx], v)

            @blk.scalar
            def _(e):
                for idx, v in ad_waits:
                    e.wait_ge(dma_sems["actdma"][idx], v)

    return nc


def kernel(**inputs):
    in_maps = prep_inputs(inputs)
    nc = build_nc()
    res = run_bass_kernel_spmd(nc, in_maps, core_ids=list(range(N_CORES)))
    outp = np.concatenate(
        [np.asarray(r["out"], np.float32) for r in res.results], axis=0)
    return outp.reshape(1, T, D)


# version 34
# speedup vs baseline: 1.0487x; 1.0158x over previous
import contextlib

import ml_dtypes
import numpy as np

import concourse.bass as bass
import concourse.mybir as mybir
from concourse.alu_op_type import AluOpType as OP
from concourse.bass_utils import run_bass_kernel_spmd

F32 = mybir.dt.float32
BF16 = mybir.dt.bfloat16
AF = mybir.ActivationFunctionType
AX = mybir.AxisListType.X

N_CORES = 8
T, D, H = 16384, 1024, 16
TC = T // N_CORES
C = 128
NCH = TC // C
DH = 4096
MID = C // 2 - 1

_DEBUG_SCRATCH = set()
_PHASES = (1, 2, 3, 4, 5, 6)
_P3_CHUNKS = NCH
_P3_BWD = True
_P3_LEVEL = 9
_P4_LEVEL = 8



_ENGINE_OF = {
    "pe": "tensor", "act": "scalar", "dve": "vector", "pool": "gpsimd",
    "sp": "sync", "actdma": "scalar", "pooldma": "gpsimd",
}
_DMA_QUEUES = {"sp", "actdma", "pooldma"}


_DMA_SUBSEMS = 6


class _Op:
    __slots__ = ("q", "fn", "deps", "needs_inc", "inc_val", "sem_idx", "gate",
                 "pre_waits")

    def __init__(self, q, fn):
        self.q = q
        self.fn = fn
        self.deps = []
        self.needs_inc = q in _DMA_QUEUES
        self.inc_val = None
        self.sem_idx = 0
        self.gate = None
        self.pre_waits = ()


class Sched:

    def __init__(self, nc, sems, dma_sems):
        self.nc = nc
        self.sems = sems
        self.dma_sems = dma_sems
        self.inc_count = {q: 0 for q in _ENGINE_OF}
        self.dma_count = {q: 0 for q in _DMA_QUEUES}
        self.ops = []
        self.last_writer = {}
        self.readers = {}

    def dma_drain(self):
        for q in _DMA_QUEUES:
            waits = []
            n_dma = self.dma_count[q]
            for idx in range(_DMA_SUBSEMS):
                cnt = (n_dma - idx + _DMA_SUBSEMS - 1) // _DMA_SUBSEMS
                if cnt > 0:
                    waits.append((self.dma_sems[q][idx], 16 * cnt, (q, idx)))
            if waits:
                o = _Op(q, None)
                o.needs_inc = False
                o.pre_waits = waits
                self.ops.append(o)

    def op(self, q, fn, reads=(), writes=(), force_inc=False):
        o = _Op(q, fn)
        if q in _DMA_QUEUES:
            i = self.dma_count[q]
            self.dma_count[q] = i + 1
            o.sem_idx = i % _DMA_SUBSEMS
            o.inc_val = 16 * (i // _DMA_SUBSEMS + 1)
            if i >= _DMA_SUBSEMS:
                o.gate = o.inc_val - 16
        for k in reads:
            w = self.last_writer.get(k)
            if w is not None:
                o.deps.append(w)
            self.readers.setdefault(k, []).append(o)
        for k in writes:
            w = self.last_writer.get(k)
            if w is not None:
                o.deps.append(w)
            o.deps.extend(self.readers.get(k, ()))
            self.last_writer[k] = o
            self.readers[k] = []
        self.ops.append(o)
        return o

    def emit(self, block):
        ops = self.ops
        self.ops = []
        for o in ops:
            o.deps = [d for d in o.deps
                      if d is not o and not (d.q == "pe" and o.q == "pe")]
            for d in o.deps:
                d.needs_inc = True
        for o in ops:
            if o.needs_inc and o.inc_val is None:
                self.inc_count[o.q] += 1
                o.inc_val = self.inc_count[o.q]
        per_engine = {}
        for o in ops:
            per_engine.setdefault(_ENGINE_OF[o.q], []).append(o)

        def dep_sem(d):
            if d.q in _DMA_QUEUES:
                return self.dma_sems[d.q][d.sem_idx], (d.q, d.sem_idx)
            return self.sems[d.q], (d.q, 0)

        def mk_section(eng_ops):
            def section(eng):
                seen = {}
                for o in eng_ops:
                    for sem, v, key in o.pre_waits:
                        if seen.get(key, 0) < v:
                            seen[key] = v
                            eng.wait_ge(sem, v)
                    if o.fn is None:
                        continue
                    if o.gate is not None:
                        sem = self.dma_sems[o.q][o.sem_idx]
                        key = (o.q, o.sem_idx)
                        if seen.get(key, 0) < o.gate:
                            seen[key] = o.gate
                            eng.wait_ge(sem, o.gate)
                    for d in o.deps:
                        if d.inc_val is None:
                            continue
                        sem, key = dep_sem(d)
                        if seen.get(key, 0) >= d.inc_val:
                            continue
                        seen[key] = d.inc_val
                        eng.wait_ge(sem, d.inc_val)
                    inst = o.fn(eng)
                    if o.needs_inc:
                        if o.q in _DMA_QUEUES:
                            inst.then_inc(self.dma_sems[o.q][o.sem_idx], 16)
                        else:
                            inst.then_inc(self.sems[o.q], 1)
            return section

        for name, eng_ops in per_engine.items():
            getattr(block, name)(mk_section(eng_ops))



def _chunk_cols(v):
    return np.ascontiguousarray(np.asarray(v, np.float32).reshape(8, 128).T)


def _swiz(w):
    w = np.asarray(w, np.float32)
    ni = w.shape[0] // 128
    return np.ascontiguousarray(
        w.reshape(ni, 128, w.shape[1]).transpose(1, 0, 2).reshape(128, -1))


def _mw2d(w):
    out = np.zeros((64, 3 * D), np.float32)
    for f in range(5):
        blk, par = f // 2, f % 2
        out[par * 32:(par + 1) * 32, blk * D:(blk + 1) * D] = w[f]
    return np.ascontiguousarray(out)


def _blkw(wT, nblk_o):
    inn, outn = wT.shape
    ni = inn // 128
    a = wT.reshape(ni, 128, nblk_o, 128).transpose(2, 1, 0, 3)
    return np.ascontiguousarray(a.reshape(nblk_o, 128, inn))


def prep_inputs(ins):
    x = np.asarray(ins["L0_attn_x"], np.float32)[0]
    conv_w = np.asarray(ins["L0_attn_conv_w"], np.float32)
    j = np.arange(C)[:, None]
    i = np.arange(C)[None, :]
    tri_ci = -(j <= i).astype(np.float32)
    tri_cs = -(j > i).astype(np.float32)
    tri_cc = ((j <= MID).astype(np.float32) - (j <= i))
    tri_cb = ((j > MID).astype(np.float32) - (j > i))
    aux = np.zeros((C, 4), np.float32)
    aux[:, 0] = -(np.arange(C) <= MID).astype(np.float32)
    aux[:, 1] = -1.0
    aux[:, 2] = -(np.arange(C) > MID).astype(np.float32)
    mask_f = (j <= i).astype(np.float32)
    mask_b = (j > i).astype(np.float32)
    pmask = np.zeros((128, 2), np.float32)
    pmask[0:64, 0] = 1.0
    pmask[64:128, 1] = 1.0
    colmask = np.zeros((128, 2048), np.float32)
    cols = np.arange(1024)
    colmask[:, 0:1024] = ((cols % 128) < 64).astype(np.float32)[None, :]
    colmask[:, 1024:2048] = ((cols % 128) >= 64).astype(np.float32)[None, :]

    def rep(v):
        return np.ascontiguousarray(
            np.broadcast_to(np.asarray(v, np.float32)[None, :], (128, D)))

    w1T = np.asarray(ins["L0_ffn_w1_w"], np.float32).T
    w2T = np.asarray(ins["L0_ffn_w2_w"], np.float32).T
    h1T = np.asarray(ins["L0_ffn_h1_w"], np.float32).T
    h2T = np.asarray(ins["L0_ffn_h2_w"], np.float32).T
    w3T = np.asarray(ins["L0_ffn_w3_w"], np.float32).T

    def scale_cols(s, nblk):
        return np.ascontiguousarray(np.asarray(s, np.float32).reshape(nblk, 128).T)

    def _bf(a):
        return np.ascontiguousarray(np.asarray(a).astype(ml_dtypes.bfloat16))

    shared = {
        "c0": _chunk_cols(conv_w[:, 0, 0]),
        "c1": _chunk_cols(conv_w[:, 0, 1]),
        "c2": _chunk_cols(conv_w[:, 0, 2]),
        "maa_x": _chunk_cols(ins["L0_attn_maa_x"]),
        "maa_w": _chunk_cols(ins["L0_attn_maa_w"]),
        "maa_k": _chunk_cols(ins["L0_attn_maa_k"]),
        "maa_v": _chunk_cols(ins["L0_attn_maa_v"]),
        "maa_r": _chunk_cols(ins["L0_attn_maa_r"]),
        "maa_g": _chunk_cols(ins["L0_attn_maa_g"]),
        "maa_w1": _bf(_swiz(np.asarray(ins["L0_attn_maa_w1"], np.float32))),
        "maa_w2": _bf(_mw2d(np.asarray(ins["L0_attn_maa_w2"], np.float32))),
        "dec_w1": _bf(_swiz(np.asarray(ins["L0_attn_decay_w1"], np.float32))),
        "dec_w2": _bf(np.asarray(ins["L0_attn_decay_w2"], np.float32)),
        "td_rep": rep(ins["L0_attn_time_decay"]),
        "Wr": _bf(_swiz(ins["L0_attn_Wr"])), "Wk": _bf(_swiz(ins["L0_attn_Wk"])),
        "Wv": _bf(_swiz(ins["L0_attn_Wv"])), "Wg": _bf(_swiz(ins["L0_attn_Wg"])),
        "Wo": _bf(_swiz(ins["L0_attn_Wo"])), "gate_w": _bf(_swiz(ins["L0_attn_gate_w"])),
        "gate_b_rep": rep(ins["L0_attn_gate_b"]),
        "ln_w_rep": rep(ins["L0_attn_ln_w"]),
        "ln_b_rep": rep(ins["L0_attn_ln_b"]),
        "tri_ci": tri_ci, "tri_cs": tri_cs, "tri_cc": tri_cc, "tri_cb": tri_cb,
        "pmask": pmask, "colmask": _bf(colmask),
        "aux_cols": np.ascontiguousarray(aux),
        "mask_f": mask_f, "mask_b": mask_b,
        "w1b": _bf(_blkw(w1T, 32)), "w2b": _bf(_blkw(w2T, 32)),
        "h1b": _bf(_blkw(h1T, 32)), "h2b": _bf(_blkw(h2T, 32)),
        "w3t": _bf(_swiz(w3T)),
        "s1c": scale_cols(ins["L0_ffn_w1_s"], 32),
        "s2c": scale_cols(ins["L0_ffn_w2_s"], 32),
        "sh1c": scale_cols(ins["L0_ffn_h1_s"], 32),
        "sh2c": scale_cols(ins["L0_ffn_h2_s"], 32),
        "s3_rep": rep(ins["L0_ffn_w3_s"]),
    }

    in_maps = []
    for c in range(N_CORES):
        s, e = c * TC, (c + 1) * TC
        xh = np.zeros((TC + 2, D), np.float32)
        xh[1:-1] = x[s:e]
        if s > 0:
            xh[0] = x[s - 1]
        if e < T:
            xh[-1] = x[e]
        xf = np.ascontiguousarray(
            xh.T.reshape(8, 128, TC + 2).transpose(1, 0, 2))
        cmf = np.zeros((128, 8), np.float32)
        cmb = np.zeros((128, 8), np.float32)
        cmf[:, :c] = 1.0
        cmb[:, c + 1:] = 1.0
        m = dict(shared)
        m["xf"] = xf
        m["xt"] = np.ascontiguousarray(x[s:e])
        m["cmask_f"] = cmf
        m["cmask_b"] = cmb
        in_maps.append(m)
    return in_maps



_BF16_INPUTS = {
    "maa_w1", "maa_w2", "dec_w1", "dec_w2",
    "Wr", "Wk", "Wv", "Wg", "Wo", "gate_w",
    "w1b", "w2b", "h1b", "h2b", "w3t", "colmask",
}


def build_nc():
    nc = bass.Bass("TRN2", target_bir_lowering=False)

    EI = dict(kind="ExternalInput")
    inp = {}
    for name, shape in [
        ("xf", [128, 8, TC + 2]), ("xt", [TC, D]),
        ("c0", [128, 8]), ("c1", [128, 8]), ("c2", [128, 8]),
        ("maa_x", [128, 8]), ("maa_w", [128, 8]), ("maa_k", [128, 8]),
        ("maa_v", [128, 8]), ("maa_r", [128, 8]), ("maa_g", [128, 8]),
        ("maa_w1", [128, 8 * 160]), ("maa_w2", [64, 3 * D]),
        ("dec_w1", [128, 8 * 64]), ("dec_w2", [64, D]), ("td_rep", [128, D]),
        ("Wr", [128, 8 * D]), ("Wk", [128, 8 * D]), ("Wv", [128, 8 * D]),
        ("Wg", [128, 8 * D]), ("Wo", [128, 8 * D]), ("gate_w", [128, 8 * D]),
        ("gate_b_rep", [128, D]), ("ln_w_rep", [128, D]), ("ln_b_rep", [128, D]),
        ("tri_ci", [C, C]), ("tri_cs", [C, C]), ("tri_cc", [C, C]),
        ("tri_cb", [C, C]), ("aux_cols", [C, 4]),
        ("mask_f", [C, C]), ("mask_b", [C, C]),
        ("pmask", [128, 2]), ("colmask", [128, 2048]),
        ("cmask_f", [128, 8]), ("cmask_b", [128, 8]),
        ("w1b", [32, 128, D]), ("w2b", [32, 128, D]),
        ("h1b", [32, 128, DH]), ("h2b", [32, 128, DH]), ("w3t", [128, 32 * D]),
        ("s1c", [128, 32]), ("s2c", [128, 32]), ("sh1c", [128, 32]),
        ("sh2c", [128, 32]), ("s3_rep", [128, D]),
    ]:
        dt = BF16 if name in _BF16_INPUTS else F32
        inp[name] = nc.dram_tensor(name, shape, dt, **EI)
    out = nc.dram_tensor("out", [TC, D], F32, kind="ExternalOutput")

    def scratch(name, shape, dt):
        kind = "ExternalOutput" if name in _DEBUG_SCRATCH else "Internal"
        return nc.dram_tensor(name, shape, dt, kind=kind)

    sc = {
        "xw": scratch("sc_xw", [D, TC], BF16),
        "xk": scratch("sc_xk", [D, TC], BF16),
        "xv": scratch("sc_xv", [D, TC], BF16),
        "xr": scratch("sc_xr", [D, TC], BF16),
        "xg": scratch("sc_xg", [D, TC], BF16),
        "dx": scratch("sc_dx", [D, TC], BF16),
        "r": scratch("sc_r", [NCH, 128, 8 * C], BF16),
        "kf": scratch("sc_kf", [NCH, 128, 8 * C], BF16),
        "k": scratch("sc_k", [NCH, D, C], BF16),
        "v": scratch("sc_v", [TC, D], BF16),
        "p": scratch("sc_p", [TC, D], F32),
        "g": scratch("sc_g", [TC, D], F32),
        "sig": scratch("sc_sig", [TC, D], F32),
        "x2": scratch("sc_x2", [TC, D], F32),
        "x2b": scratch("sc_x2b", [TC, D], BF16),
        "zb": scratch("sc_zb", [TC, D], BF16),
        "y": scratch("sc_y", [TC, D], F32),
        "ccin": scratch("cc_in", [2064, 64], F32),
        "ccout": scratch("cc_out", [2064 * N_CORES, 64], F32),
    }

    with contextlib.ExitStack() as stack:
        sems = {}
        dma_sems = {}
        for q in _ENGINE_OF:
            if q in _DMA_QUEUES:
                dma_sems[q] = [stack.enter_context(nc.semaphore(f"sem_{q}{i}"))
                               for i in range(_DMA_SUBSEMS)]
            else:
                sems[q] = stack.enter_context(nc.semaphore("sem_" + q))
        sch = Sched(nc, sems, dma_sems)

        def sbt(st, name, shape, dt):
            return st.enter_context(nc.sbuf_tensor(name, shape, dt))

        consts = sbt(stack, "consts", [128, 8], F32)
        CONST_COLS = {"zero": 0, "one": 1, "neg1": 2, "i1024": 3,
                      "eps_gn": 4, "eps_ln": 5, "i64": 6, "neg_half": 7}
        CONST_VALS = {"zero": 0.0, "one": 1.0, "neg1": -1.0, "i1024": 1.0 / 1024,
                      "eps_gn": 1e-5 * 64.0, "eps_ln": 1e-5, "i64": 1.0 / 64,
                      "neg_half": -0.5}

        def ccol(name):
            return consts[:, CONST_COLS[name]:CONST_COLS[name] + 1]

        def cread(name):
            return ("consts", CONST_COLS[name])

        smalls = {}
        for name, shape in [
            ("c0", [128, 8]), ("c1", [128, 8]), ("c2", [128, 8]),
            ("maa_x", [128, 8]), ("maa_w", [128, 8]), ("maa_k", [128, 8]),
            ("maa_v", [128, 8]), ("maa_r", [128, 8]), ("maa_g", [128, 8]),
            ("tri_ci", [C, C]), ("tri_cs", [C, C]), ("tri_cc", [C, C]),
            ("tri_cb", [C, C]), ("aux_cols", [C, 4]),
            ("mask_f", [C, C]), ("mask_b", [C, C]),
            ("pmask", [128, 2]),
            ("cmask_f", [128, 8]), ("cmask_b", [128, 8]),
        ]:
            smalls[name] = sbt(stack, "s_" + name, shape, F32)

        for name, col in CONST_COLS.items():
            sch.op("pool", lambda e, c=col, v=CONST_VALS[name]: e.memset(
                consts[:, c:c + 1], v), writes=[("consts", col)])
        for name in smalls:
            sch.op("sp", lambda e, n=name: e.dma_start(smalls[n][:, :], inp[n].ap()),
                   writes=[("small", name)])
        sch.op("dve", lambda e: e.tensor_scalar(
            smalls["c1"][:, :], smalls["c1"][:, :], 1.0, None, OP.subtract),
            reads=[("small", "c1")], writes=[("small", "c1")])
        sch.dma_drain()
        with nc.Block() as blk:
            sch.emit(blk)

        if 1 in _PHASES:
            with contextlib.ExitStack() as phs:
                xin = sbt(phs, "p1_xin", [128, 8, TC + 2], F32)
                dxb = sbt(phs, "p1_dx", [128, 8 * TC], BF16)
                xxx = sbt(phs, "p1_xxx", [128, 8 * TC], BF16)
                tmpas = [sbt(phs, "p1_tmpa0", [128, TC], F32)] * 2
                tmpbs = [sbt(phs, "p1_tmpb0", [128, TC], F32)] * 2
                mw1 = sbt(phs, "p1_mw1", [128, 8 * 160], BF16)
                mw2 = sbt(phs, "p1_mw2", [64, 3 * D], BF16)
                t5a = sbt(phs, "p1_t5a", [64, TC], BF16)
                t5b = sbt(phs, "p1_t5b", [64, TC], BF16)
                t5c = sbt(phs, "p1_t5c", [32, TC], BF16)
                xog = [sbt(phs, f"p1_xo{i}", [128, 2048], BF16) for i in range(5)]
                mtmp = [sbt(phs, f"p1_mt{i}", [128, 512], F32) for i in range(2)]
                p_t5 = [phs.enter_context(
                    nc.psum_tensor(f"pp_t5{i}", [64, 512], F32)) for i in range(2)]
                p_t5x = phs.enter_context(nc.psum_tensor("pp_t5x", [32, 512], F32))
                p_m = [phs.enter_context(
                    nc.psum_tensor(f"pp_m{i}", [128, 512], F32)) for i in range(5)]

                sch.op("pooldma", lambda e: e.dma_start(mw1[:, :], inp["maa_w1"].ap()),
                       writes=["mw1"])
                sch.op("pooldma", lambda e: e.dma_start(mw2[:, :], inp["maa_w2"].ap()),
                       writes=["mw2"])
                for j in range(8):
                    sch.op("sp", lambda e, j=j: e.dma_start(
                        xin[:, j, :], inp["xf"].ap()[:, j, :]),
                        writes=[("xin", j)])

                for j in range(8):
                    jb2 = 0
                    tmpa, tmpb = tmpas[jb2], tmpbs[jb2]
                    ka, kb = ("tmpa", jb2), ("tmpb", jb2)
                    x_l = xin[:, j, 0:TC]
                    x_c = xin[:, j, 1:TC + 1]
                    x_r = xin[:, j, 2:TC + 2]
                    sch.op("dve", lambda e, x_l=x_l, j=j, tmpa=tmpa: e.tensor_scalar(
                        tmpa[:, :], x_l, smalls["c0"][:, j:j + 1], None, OP.mult),
                        reads=[("xin", j), ("small", "c0")], writes=[ka])
                    sch.op("dve", lambda e, x_c=x_c, j=j, tmpa=tmpa, tmpb=tmpb: e.scalar_tensor_tensor(
                        tmpb[:, :], x_c, smalls["c1"][:, j:j + 1], tmpa[:, :],
                        OP.mult, OP.add),
                        reads=[("xin", j), ("small", "c1"), ka], writes=[kb])
                    sch.op("dve", lambda e, x_r=x_r, j=j, tmpb=tmpb: e.scalar_tensor_tensor(
                        dxb[:, j * TC:(j + 1) * TC], x_r, smalls["c2"][:, j:j + 1],
                        tmpb[:, :], OP.mult, OP.add),
                        reads=[("xin", j), ("small", "c2"), kb],
                        writes=[("dxb", j)])
                    sch.op("dve", lambda e, x_c=x_c, j=j: e.scalar_tensor_tensor(
                        xxx[:, j * TC:(j + 1) * TC], dxb[:, j * TC:(j + 1) * TC],
                        smalls["maa_x"][:, j:j + 1], x_c, OP.mult, OP.add),
                        reads=[("dxb", j), ("small", "maa_x"), ("xin", j)],
                        writes=[("xxx", j)])
                    sch.op("sp", lambda e, j=j: e.dma_start(
                        sc["dx"].ap()[128 * j:128 * (j + 1), :],
                        dxb[:, j * TC:(j + 1) * TC]),
                        reads=[("dxb", j)], writes=[("scr", "dx")])

                for tt in range(4):
                    for ic in range(8):
                        for g in range(2):
                            sch.op("pe", lambda e, ic=ic, tt=tt, g=g: e.matmul(
                                p_t5[g][:, :],
                                mw1[:, ic * 160 + g * 64:ic * 160 + g * 64 + 64],
                                xxx[:, ic * TC + tt * 512: ic * TC + (tt + 1) * 512],
                                start=(ic == 0), stop=(ic == 7)),
                                reads=["mw1", ("xxx", ic)], writes=[("p_t5", g)])
                        sch.op("pe", lambda e, ic=ic, tt=tt: e.matmul(
                            p_t5x[:, :], mw1[:, ic * 160 + 128:ic * 160 + 160],
                            xxx[:, ic * TC + tt * 512: ic * TC + (tt + 1) * 512],
                            start=(ic == 0), stop=(ic == 7)),
                            reads=["mw1", ("xxx", ic)], writes=["p_t5x"])
                    sch.op("act", lambda e, tt=tt: e.activation(
                        t5a[:, tt * 512:(tt + 1) * 512], p_t5[0][:, :], AF.Tanh),
                        reads=[("p_t5", 0)], writes=["t5a"])
                    sch.op("act", lambda e, tt=tt: e.activation(
                        t5b[:, tt * 512:(tt + 1) * 512], p_t5[1][:, :], AF.Tanh),
                        reads=[("p_t5", 1)], writes=["t5b"])
                    sch.op("act", lambda e, tt=tt: e.activation(
                        t5c[:, tt * 512:(tt + 1) * 512], p_t5x[:, :], AF.Tanh),
                        reads=["p_t5x"], writes=["t5c"])

                t5block = [(t5a, 0, "t5a"), (t5a, 32, "t5a"), (t5b, 0, "t5b"),
                           (t5b, 32, "t5b"), (t5c, 0, "t5c")]
                names = ["xw", "xk", "xv", "xr", "xg"]
                maas = ["maa_w", "maa_k", "maa_v", "maa_r", "maa_g"]
                for oc in range(8):
                    for f in range(5):
                        tile, base, tkey = t5block[f]
                        blk5 = f // 2
                        xot = xog[f]
                        for tt in range(4):
                            sch.op("pe", lambda e, f=f, oc=oc, tt=tt, tile=tile, base=base, blk5=blk5: e.matmul(
                                p_m[f][:, :],
                                mw2[base:base + 32,
                                    blk5 * D + oc * 128: blk5 * D + (oc + 1) * 128],
                                tile[base:base + 32, tt * 512:(tt + 1) * 512],
                                start=True, stop=True),
                                reads=["mw2", tkey], writes=[("p_m", f)])
                            mt = mtmp[(f + tt) % 2]
                            sch.op("dve", lambda e, f=f, oc=oc, tt=tt, mt=mt: e.scalar_tensor_tensor(
                                mt[:, :], p_m[f][:, :],
                                smalls[maas[f]][:, oc:oc + 1],
                                dxb[:, oc * TC + tt * 512: oc * TC + (tt + 1) * 512],
                                OP.add, OP.mult),
                                reads=[("p_m", f), ("small", maas[f]),
                                       ("dxb", oc)],
                                writes=[("mtmp", (f + tt) % 2)])
                            sch.op("pool", lambda e, oc=oc, tt=tt, mt=mt, xot=xot: e.tensor_tensor(
                                xot[:, tt * 512:(tt + 1) * 512], mt[:, :],
                                xin[:, oc, 1 + tt * 512: 1 + (tt + 1) * 512],
                                OP.add),
                                reads=[("mtmp", (f + tt) % 2), ("xin", oc)],
                                writes=[("xog", f)])
                        sch.op("sp" if f % 2 == 0 else "actdma",
                               lambda e, f=f, oc=oc, xot=xot: e.dma_start(
                                   sc[names[f]].ap()[oc * 128:(oc + 1) * 128, :],
                                   xot[:, :]),
                               reads=[("xog", f)],
                               writes=[("scr", names[f])])
                sch.dma_drain()
                with nc.Block() as blk:
                    sch.emit(blk)

        if 2 in _PHASES:
            with contextlib.ExitStack() as phs:
                wt = [sbt(phs, f"p2_wt{i}", [128, 8 * D], BF16) for i in range(2)]
                xs = [sbt(phs, f"p2_xs{i}", [128, 8 * TC], BF16) for i in range(2)]
                ot = [sbt(phs, f"p2_ot{i}", [128, 1024], BF16) for i in range(4)]
                of = [sbt(phs, f"p2_of{i}", [128, 1024], F32) for i in range(2)]
                hdec = sbt(phs, "p2_hdec", [64, TC], BF16)
                dw1 = sbt(phs, "p2_dw1", [128, 8 * 64], BF16)
                dw2 = sbt(phs, "p2_dw2", [64, D], BF16)
                gb = sbt(phs, "p2_gb", [128, D], F32)
                td = sbt(phs, "p2_td", [128, D], F32)
                pp = [phs.enter_context(
                    nc.psum_tensor(f"pp2_{i}", [128, 512], F32)) for i in range(4)]

                sch.op("sp", lambda e: e.dma_start(gb[:, :], inp["gate_b_rep"].ap()),
                       writes=["gb"])
                sch.op("sp", lambda e: e.dma_start(td[:, :], inp["td_rep"].ap()),
                       writes=["td"])
                sch.op("pooldma", lambda e: e.dma_start(dw1[:, :], inp["dec_w1"].ap()),
                       writes=["dw1"])
                sch.op("pooldma", lambda e: e.dma_start(dw2[:, :], inp["dec_w2"].ap()),
                       writes=["dw2"])

                def load_xs(key, xb):
                    for j in range(8):
                        sch.op("pooldma", lambda e, j=j, key=key, xb=xb: e.dma_start(
                            xs[xb][:, j * TC:(j + 1) * TC],
                            sc[key].ap()[128 * j:128 * (j + 1), :]),
                            reads=[("scr", key)], writes=[("xs", xb, j)])

                def load_w(wname, buf):
                    sch.op("pooldma", lambda e, wname=wname, buf=buf: e.dma_start(
                        wt[buf][:, :], inp[wname].ap()), writes=[("wt", buf)])

                for wname, skey, okey in [("Wr", "xr", "r"), ("Wk", "xk", "k")]:
                    buf = 0 if wname == "Wr" else 1
                    xb = buf
                    load_w(wname, buf)
                    load_xs(skey, xb)
                    for oc in range(8):
                        for tt in range(4):
                            bi = (oc * 4 + tt) % 4
                            pb = pp[bi]
                            for ic in range(8):
                                sch.op("pe", lambda e, oc=oc, tt=tt, ic=ic, pb=pb, buf=buf, xb=xb: e.matmul(
                                    pb[:, :],
                                    wt[buf][:, ic * D + oc * 128: ic * D + (oc + 1) * 128],
                                    xs[xb][:, ic * TC + tt * 512: ic * TC + (tt + 1) * 512],
                                    start=(ic == 0), stop=(ic == 7)),
                                    reads=[("wt", buf), ("xs", xb, ic)],
                                    writes=[("pp", bi)])
                            obuf = ot[bi]
                            sch.op("act", lambda e, pb=pb, obuf=obuf: e.activation(
                                obuf[:, 0:512], pb[:, :], AF.Copy),
                                reads=[("pp", bi)],
                                writes=[("ot", bi)])
                            if okey == "r":
                                sch.op("sp", lambda e, tt=tt, oc=oc, obuf=obuf: e.dma_start(
                                    sc["r"].ap()[4 * tt:4 * tt + 4, :, oc * C:(oc + 1) * C]
                                    .rearrange("n p c -> p n c"),
                                    obuf[:, 0:512].rearrange("p (n c) -> p n c", c=C)),
                                    reads=[("ot", bi)],
                                    writes=[("scr", "r")])
                            else:
                                sch.op("sp", lambda e, tt=tt, oc=oc, obuf=obuf: e.dma_start(
                                    sc["kf"].ap()[4 * tt:4 * tt + 4, :, oc * C:(oc + 1) * C]
                                    .rearrange("n p c -> p n c"),
                                    obuf[:, 0:512].rearrange("p (n c) -> p n c", c=C)),
                                    reads=[("ot", bi)],
                                    writes=[("scr", "kf")])
                                sch.op("actdma", lambda e, tt=tt, oc=oc, obuf=obuf: e.dma_start(
                                    sc["k"].ap()[4 * tt:4 * tt + 4, oc * 128:(oc + 1) * 128, :]
                                    .rearrange("n p c -> p n c"),
                                    obuf[:, 0:512].rearrange("p (n c) -> p n c", c=C)),
                                    reads=[("ot", bi)],
                                    writes=[("scr", "k")])

                for wi, (wname, skey, okey) in enumerate([
                        ("Wv", "xv", "v"), ("Wg", "xg", "g"),
                        ("gate_w", "dx", "sig")]):
                    buf = wi % 2
                    xb = wi % 2
                    load_w(wname, buf)
                    load_xs(skey, xb)
                    for tc in range(NCH):
                        pbase = (tc % 2) * 2
                        for half in range(2):
                            pb = pp[pbase + half]
                            for ic in range(8):
                                sch.op("pe", lambda e, tc=tc, half=half, ic=ic, pb=pb, buf=buf, xb=xb: e.matmul(
                                    pb[:, :],
                                    xs[xb][:, ic * TC + tc * 128: ic * TC + (tc + 1) * 128],
                                    wt[buf][:, ic * D + half * 512: ic * D + half * 512 + 512],
                                    start=(ic == 0), stop=(ic == 7)),
                                    reads=[("wt", buf), ("xs", xb, ic)],
                                    writes=[("pp", pbase + half)])
                        if okey == "v":
                            obuf = ot[tc % 2]
                            for half in range(2):
                                sch.op("act", lambda e, half=half, obuf=obuf, pbase=pbase: e.activation(
                                    obuf[:, half * 512:half * 512 + 512],
                                    pp[pbase + half][:, :], AF.Copy),
                                    reads=[("pp", pbase + half)], writes=[("ot", tc % 2)])
                            sch.op("sp", lambda e, tc=tc, obuf=obuf: e.dma_start(
                                sc["v"].ap()[tc * 128:(tc + 1) * 128, :], obuf[:, :]),
                                reads=[("ot", tc % 2)], writes=[("scr", "v")])
                        elif okey == "g":
                            obuf = of[tc % 2]
                            for half in range(2):
                                sch.op("act", lambda e, half=half, obuf=obuf, pbase=pbase: e.activation(
                                    obuf[:, half * 512:half * 512 + 512],
                                    pp[pbase + half][:, :], AF.Silu),
                                    reads=[("pp", pbase + half)], writes=[("of", tc % 2)])
                            sch.op("sp", lambda e, tc=tc, obuf=obuf: e.dma_start(
                                sc["g"].ap()[tc * 128:(tc + 1) * 128, :], obuf[:, :]),
                                reads=[("of", tc % 2)], writes=[("scr", "g")])
                        else:
                            obuf = of[tc % 2]
                            for half in range(2):
                                sch.op("dve", lambda e, half=half, obuf=obuf, pbase=pbase: e.tensor_tensor(
                                    obuf[:, half * 512:half * 512 + 512],
                                    pp[pbase + half][:, :],
                                    gb[:, half * 512:half * 512 + 512], OP.add),
                                    reads=[("pp", pbase + half), "gb"],
                                    writes=[("of", tc % 2)])
                            sch.op("act", lambda e, obuf=obuf: e.activation(
                                obuf[:, :], obuf[:, :], AF.Sigmoid),
                                reads=[("of", tc % 2)], writes=[("of", tc % 2)])
                            sch.op("sp", lambda e, tc=tc, obuf=obuf: e.dma_start(
                                sc["sig"].ap()[tc * 128:(tc + 1) * 128, :], obuf[:, :]),
                                reads=[("of", tc % 2)], writes=[("scr", "sig")])

                load_xs("xw", 1)
                for tt in range(4):
                    pb = pp[0]
                    for ic in range(8):
                        sch.op("pe", lambda e, tt=tt, ic=ic, pb=pb: e.matmul(
                            pb[0:64, :], dw1[:, ic * 64:(ic + 1) * 64],
                            xs[1][:, ic * TC + tt * 512: ic * TC + (tt + 1) * 512],
                            start=(ic == 0), stop=(ic == 7)),
                            reads=["dw1", ("xs", 1, ic)], writes=[("pp", 0)])
                    sch.op("act", lambda e, tt=tt, pb=pb: e.activation(
                        hdec[:, tt * 512:(tt + 1) * 512], pb[0:64, :], AF.Tanh),
                        reads=[("pp", 0)], writes=["hdec"])
                for tc in range(NCH):
                    pbase = (tc % 2) * 2
                    for half in range(2):
                        pb = pp[pbase + half]
                        sch.op("pe", lambda e, tc=tc, half=half, pb=pb: e.matmul(
                            pb[:, :], hdec[:, tc * 128:(tc + 1) * 128],
                            dw2[:, half * 512:half * 512 + 512],
                            start=True, stop=True),
                            reads=["hdec", "dw2"], writes=[("pp", pbase + half)])
                    obuf = of[tc % 2]
                    for half in range(2):
                        sch.op("dve", lambda e, half=half, obuf=obuf, pbase=pbase: e.tensor_tensor(
                            obuf[:, half * 512:half * 512 + 512], pp[pbase + half][:, :],
                            td[:, half * 512:half * 512 + 512], OP.add),
                            reads=[("pp", pbase + half), "td"], writes=[("of", tc % 2)])
                    sch.op("act", lambda e, obuf=obuf: e.activation(
                        obuf[:, :], obuf[:, :], AF.Exp),
                        reads=[("of", tc % 2)], writes=[("of", tc % 2)])
                    sch.op("sp", lambda e, tc=tc, obuf=obuf: e.dma_start(
                        sc["p"].ap()[tc * 128:(tc + 1) * 128, :], obuf[:, :]),
                        reads=[("of", tc % 2)], writes=[("scr", "p")])
                sch.dma_drain()
                with nc.Block() as blk:
                    sch.emit(blk)

        with contextlib.ExitStack() as mid:
            ybuf = None
            if any(p in _PHASES for p in (3, 4, 5)):
                ybuf = sbt(mid, "ybuf", [128, NCH * D], F32)

            if 3 in _PHASES:
                p3 = contextlib.ExitStack()
                mid.enter_context(p3)
                swp = contextlib.ExitStack()
                r16 = [sbt(p3, f"p3_r{i}", [128, D], BF16) for i in range(3)]
                ptok = [sbt(p3, f"p3_pt{i}", [128, D], F32) for i in range(3)]
                Stmp = sbt(p3, "p3_Stmp", [128, 512], F32)
                DnAll = sbt(p3, "p3_DnAll", [128, 8 * NCH], F32)
                BsAll = sbt(p3, "p3_BsAll", [128, 8 * NCH], F32)
                E1u = [sbt(p3, f"p3_E1u{i}", [128, D], BF16) for i in range(2)]
                E3t = [sbt(p3, f"p3_E3t{i}", [128, D], BF16) for i in range(2)]
                Rt = [sbt(p3, f"p3_Rt{i}", [128, D], BF16) for i in range(2)]
                R2 = [sbt(p3, f"p3_R2{i}", [128, D], BF16) for i in range(2)]
                KtP = [sbt(p3, f"p3_KtP{i}", [128, D], BF16) for i in range(4)]
                R2P = [sbt(p3, f"p3_R2P{i}", [128, D], BF16) for i in range(4)]
                k16 = [sbt(swp, f"p3_k{i}", [128, D], BF16) for i in range(3)]
                ktok = [sbt(swp, f"p3_kt{i}", [128, D], BF16) for i in range(3)]
                vtok = [sbt(swp, f"p3_vt{i}", [128, D], BF16) for i in range(3)]
                St = sbt(swp, "p3_S", [128, 512], F32)
                S16 = sbt(swp, "p3_S16", [128, 512], BF16)
                Dn = sbt(swp, "p3_Dn", [128, 8], F32)
                auxsb = [sbt(swp, f"p3_aux{i}", [128, 32], F32) for i in range(2)]
                eW8 = [sbt(swp, f"p3_eW8{i}", [128, 8], F32) for i in range(2)]
                emid = [sbt(swp, f"p3_emid{i}", [128, 8], BF16) for i in range(2)]
                E1c = [sbt(swp, f"p3_E1c{i}", [128, D], BF16) for i in range(2)]
                Kt = [sbt(swp, f"p3_Kt{i}", [128, D], BF16) for i in range(2)]
                Kh = [sbt(swp, f"p3_Kh{i}", [128, D], BF16) for i in range(2)]
                colmask = sbt(swp, "p3_colmask", [128, 2048], BF16)
                sch.op("sp", lambda e: e.dma_start(
                    colmask[:, :], inp["colmask"].ap()),
                    writes=[("small", "colmask")])
                KhP = [sbt(swp, f"p3_KhP{i}", [128, D], BF16) for i in range(4)]
                Pt = [sbt(swp, f"p3_P{i}", [128, 2 * 1024], BF16) for i in range(2)]
                pA = p3.enter_context(nc.psum_tensor("pp3_A", [128, 1024], F32))
                pB = p3.enter_context(nc.psum_tensor("pp3_B", [128, 1024], F32))
                pP = p3.enter_context(nc.psum_tensor("pp3_P", [128, 512], F32))
                pY = p3.enter_context(nc.psum_tensor("pp3_Y", [128, 1024], F32))
                pSt = p3.enter_context(nc.psum_tensor("pp3_St", [128, 512], F32))

                def aux3(b, colidx):
                    return auxsb[b][:, :].rearrange("p (j c) -> p j c", c=4)[:, :, colidx]

                def sweep(fwd):
                    tri_c = smalls["tri_cc"] if fwd else smalls["tri_cb"]
                    tri_ck = "tri_cc" if fwd else "tri_cb"
                    tri_tok = smalls["tri_cs"] if fwd else smalls["tri_ci"]
                    tri_tk = "tri_cs" if fwd else "tri_ci"
                    mask = smalls["mask_f"] if fwd else smalls["mask_b"]
                    mkey = "mask_f" if fwd else "mask_b"
                    mid_col = 0 if fwd else 2
                    All = DnAll if fwd else BsAll
                    akey = "DnAll" if fwd else "BsAll"
                    chunks = (range(_P3_CHUNKS) if fwd
                              else range(_P3_CHUNKS - 1, -1, -1))
                    L = _P3_LEVEL

                    def op3(lvl, *a, **k):
                        if L >= lvl:
                            return sch.op(*a, **k)
                    op3(0, "dve", lambda e: e.memset(St[:, :], 0.0), writes=["St"])
                    op3(0, "dve", lambda e: e.memset(Dn[:, :], 0.0), writes=["Dn"])
                    for n in chunks:
                        b = n % 2
                        bl = n % 3
                        op3(0, "sp", lambda e, n=n, bl=bl: e.dma_start(
                            r16[bl][:, :], sc["r"].ap()[n]),
                            reads=[("scr", "r")], writes=[("r16", bl)])
                        op3(0, "sp", lambda e, n=n, bl=bl: e.dma_start(
                            k16[bl][:, :], sc["kf"].ap()[n]),
                            reads=[("scr", "kf")], writes=[("k16", bl)])
                        op3(0, "actdma", lambda e, n=n, bl=bl: e.dma_start(
                            ktok[bl][:, :], sc["k"].ap()[n], transpose=True),
                            reads=[("scr", "k")], writes=[("ktok", bl)])
                        op3(0, "sp", lambda e, n=n, bl=bl: e.dma_start(
                            vtok[bl][:, :], sc["v"].ap()[n * C:(n + 1) * C, :]),
                            reads=[("scr", "v")], writes=[("vtok", bl)])
                        op3(0, "sp", lambda e, n=n, bl=bl: e.dma_start(
                            ptok[bl][:, :], sc["p"].ap()[n * C:(n + 1) * C, :]),
                            reads=[("scr", "p")], writes=[("ptok", bl)])
                        for j in range(8):
                            op3(1, "pe", lambda e, j=j, bl=bl, tri_c=tri_c: e.matmul(
                                pA[:, j * 128:(j + 1) * 128],
                                ptok[bl][:, j * 128:(j + 1) * 128], tri_c[:, :],
                                start=(j % 4 == 0), stop=(j % 4 == 3)),
                                reads=[("ptok", bl), ("small", tri_ck)], writes=["pA"])
                        for j in range(8):
                            op3(1, "pe", lambda e, j=j, bl=bl: e.matmul(
                                pY[:, j * 4:(j + 1) * 4],
                                ptok[bl][:, j * 128:(j + 1) * 128],
                                smalls["aux_cols"][:, :],
                                start=(j == 0), stop=(j == 7)),
                                reads=[("ptok", bl), ("small", "aux_cols")],
                                writes=["pY"])
                        for half in range(2):
                            op3(1, "pe", lambda e, half=half, bl=bl, tri_tok=tri_tok: e.matmul(
                                pB[:, half * 512:half * 512 + 512], tri_tok[:, :],
                                ptok[bl][:, half * 512:half * 512 + 512],
                                start=True, stop=True),
                                reads=[("ptok", bl), ("small", tri_tk)], writes=["pB"])
                        op3(2, "dve", lambda e, b=b: e.tensor_copy(
                            auxsb[b][:, :], pY[:, 0:32]),
                            reads=["pY"], writes=[("aux", b)])
                        op3(2, "act", lambda e, mid_col=mid_col, b=b: e.activation(
                            emid[b][:, :], aux3(b, mid_col), AF.Exp),
                            reads=[("aux", b)], writes=[("emid", b)])
                        op3(2, "act", lambda e, b=b: e.activation(
                            eW8[b][:, :], aux3(b, 1), AF.Exp),
                            reads=[("aux", b)], writes=[("eW8", b)])
                        op3(2, "pool", lambda e, n=n, All=All: e.tensor_copy(
                            All[:, n * 8:(n + 1) * 8], Dn[:, :]),
                            reads=["Dn"], writes=[akey])
                        op3(2, "pool", lambda e, b=b: e.tensor_tensor(
                            Dn[:, :], Dn[:, :], aux3(b, 1), OP.add),
                            reads=["Dn", ("aux", b), akey], writes=["Dn"])
                        op3(3, "act", lambda e, b=b: e.activation(
                            E1c[b][:, :], pA[:, :], AF.Exp),
                            reads=["pA"], writes=[("E1c", b)])
                        op3(3, "act", lambda e, b=b: e.activation(
                            E1u[b][:, :], pA[:, :], AF.Exp, scale=ccol("neg1")),
                            reads=["pA", cread("neg1")], writes=[("E1u", b)])
                        op3(4, "dve", lambda e, b=b, bl=bl: e.tensor_tensor(
                            Kt[b][:, :], k16[bl][:, :], E1u[b][:, :], OP.mult),
                            reads=[("k16", bl), ("E1u", b)], writes=[("Kt", b)])
                        for par in range(2):
                            op3(5, "act", lambda e, par=par, b=b: e.activation(
                                KtP[b * 2 + par][:, :], Kt[b][:, :], AF.Identity,
                                scale=smalls["pmask"][:, par:par + 1]),
                                reads=[("Kt", b), ("small", "pmask")],
                                writes=[("KtP", b * 2 + par)])
                        op3(3, "act", lambda e, b=b: e.activation(
                            E3t[b][:, :], pB[:, :], AF.Exp),
                            reads=["pB"], writes=[("E3t", b)])
                        op3(4, "dve", lambda e, b=b, bl=bl: e.tensor_tensor(
                            Rt[b][:, :], r16[bl][:, :], E1c[b][:, :], OP.mult),
                            reads=[("r16", bl), ("E1c", b)], writes=[("Rt", b)])
                        op3(4, "dve", lambda e, b=b: e.tensor_tensor(
                            E1u[b][:, :].rearrange("p (a b) -> p a b", b=128),
                            E1c[b][:, :].rearrange("p (a b) -> p a b", b=128),
                            emid[b][:, :].unsqueeze(2).broadcast_to([128, 8, 128]),
                            OP.mult),
                            reads=[("E1c", b), ("emid", b)], writes=[("E1u", b)])
                        op3(4, "dve", lambda e, b=b, bl=bl: e.tensor_tensor(
                            R2[b][:, :], r16[bl][:, :], E1u[b][:, :], OP.mult),
                            reads=[("r16", bl), ("E1u", b)], writes=[("R2", b)])
                        for par in range(2):
                            op3(6, "dve", lambda e, par=par, b=b: e.tensor_scalar(
                                R2P[b * 2 + par][:, :], R2[b][:, :],
                                smalls["pmask"][:, par:par + 1], None, OP.mult),
                                reads=[("R2", b), ("small", "pmask")],
                                writes=[("R2P", b * 2 + par)])
                        op3(4, "dve", lambda e, b=b, bl=bl: e.tensor_tensor(
                            Kh[b][:, :], ktok[bl][:, :], E3t[b][:, :], OP.mult),
                            reads=[("ktok", bl), ("E3t", b)], writes=[("Kh", b)])
                        op3(7, "dve", lambda e, b=b: e.tensor_tensor(
                            KhP[b * 2][:, :], Kh[b][:, :],
                            colmask[:, 0:1024], OP.mult),
                            reads=[("Kh", b), ("small", "colmask")],
                            writes=[("KhP", b * 2)])
                        op3(7, "pool", lambda e, b=b: e.tensor_tensor(
                            KhP[b * 2 + 1][:, :], Kh[b][:, :],
                            colmask[:, 1024:2048], OP.mult),
                            reads=[("Kh", b), ("small", "colmask")],
                            writes=[("KhP", b * 2 + 1)])
                        op3(4, "act", lambda e: e.activation(
                            S16[:, :], St[:, :], AF.Copy),
                            reads=["St"], writes=["S16"])
                        for wave in range(2):
                            for hw_ in range(2):
                                for hh4 in range(4):
                                    hh = hw_ * 4 + hh4
                                    h = wave * 8 + hh
                                    par = h % 2
                                    jb = h // 2
                                    op3(5, "pe", lambda e, hh4=hh4, par=par, jb=jb, b=b: e.matmul(
                                        pP[:, hh4 * 128:(hh4 + 1) * 128],
                                        KtP[b * 2 + par][:, jb * 128:(jb + 1) * 128],
                                        Rt[b][:, jb * 128:(jb + 1) * 128],
                                        start=(hh4 == 0), stop=(hh4 == 3)),
                                        reads=[("KtP", b * 2 + par), ("Rt", b)],
                                        writes=["pP"])
                                op3(6, "dve", lambda e, wave=wave, hw_=hw_, mask=mask, b=b: e.tensor_tensor(
                                    Pt[b][:, wave * 1024 + hw_ * 512:
                                          wave * 1024 + hw_ * 512 + 512]
                                    .rearrange("p (a b) -> p a b", b=128),
                                    pP[:, :].rearrange("p (a b) -> p a b", b=128),
                                    mask[:, :].unsqueeze(1).broadcast_to([128, 4, 128]),
                                    OP.mult),
                                    reads=["pP", ("small", mkey)],
                                    writes=[("Pt", b, wave)])
                        for h in range(H):
                            wave, hh = h // 8, h % 8
                            jb = h // 2
                            op3(6, "pe", lambda e, h=h, wave=wave, hh=hh, b=b, bl=bl: e.matmul(
                                pY[:, h * 64:(h + 1) * 64],
                                Pt[b][:, wave * 1024 + hh * 128: wave * 1024 + (hh + 1) * 128],
                                vtok[bl][:, h * 64:(h + 1) * 64],
                                start=(h == 0 or h == 8), stop=False),
                                reads=[("Pt", b, wave), ("vtok", bl)], writes=["pY"])
                            op3(6, "pe", lambda e, h=h, jb=jb, b=b: e.matmul(
                                pY[:, h * 64:(h + 1) * 64],
                                R2P[b * 2 + h % 2][:, jb * 128:(jb + 1) * 128],
                                S16[:, jb * 64:(jb + 1) * 64],
                                start=False, stop=(h == 7 or h == 15)),
                                reads=[("R2P", b * 2 + h % 2), "S16"], writes=["pY"])
                        if fwd:
                            op3(6, "act", lambda e, n=n: e.activation(
                                ybuf[:, n * D:(n + 1) * D], pY[:, :], AF.Copy),
                                reads=["pY"], writes=[("y", n)])
                        else:
                            op3(6, "dve", lambda e, n=n: e.tensor_tensor(
                                ybuf[:, n * D:(n + 1) * D],
                                ybuf[:, n * D:(n + 1) * D], pY[:, :], OP.add),
                                reads=["pY", ("y", n)], writes=[("y", n)])
                        for h in range(H):
                            par = h % 2
                            jb = h // 2
                            op3(7, "pe", lambda e, h=h, b=b, bl=bl, par=par, jb=jb: e.matmul(
                                pSt[:, jb * 64:(jb + 1) * 64],
                                KhP[b * 2 + par][:, jb * 128:(jb + 1) * 128],
                                vtok[bl][:, h * 64:(h + 1) * 64],
                                start=(h == 0), stop=(h == 15)),
                                reads=[("KhP", b * 2 + par), ("vtok", bl)],
                                writes=["pSt"])
                        op3(8, "dve", lambda e, b=b: e.tensor_tensor(
                            Stmp[:, 0:256].rearrange("p (a b) -> p a b", b=64),
                            St[:, 0:256].rearrange("p (a b) -> p a b", b=64),
                            eW8[b][:, 0:4].unsqueeze(2).broadcast_to([128, 4, 64]),
                            OP.mult),
                            reads=["St", ("eW8", b), "S16"], writes=["Stmp"])
                        op3(8, "pool", lambda e, b=b: e.tensor_tensor(
                            Stmp[:, 256:512].rearrange("p (a b) -> p a b", b=64),
                            St[:, 256:512].rearrange("p (a b) -> p a b", b=64),
                            eW8[b][:, 4:8].unsqueeze(2).broadcast_to([128, 4, 64]),
                            OP.mult),
                            reads=["St", ("eW8", b), "S16"], writes=["Stmp"])
                        op3(8, "dve", lambda e: e.tensor_tensor(
                            St[:, :], Stmp[:, :], pSt[:, :], OP.add),
                            reads=["Stmp", "pSt"], writes=["St"])
                    rows0 = 0 if fwd else 1024
                    op3(9, "sp", lambda e, rows0=rows0: e.dma_start(
                        sc["ccin"].ap()[rows0:rows0 + 1024, :]
                        .rearrange("(j p) v -> p j v", p=128),
                        St[:, :].rearrange("p (j v) -> p j v", v=64)),
                        reads=["St"], writes=["ccin"])
                    if not fwd:
                        op3(9, "sp", lambda e: e.dma_start(
                            sc["ccin"].ap()[2048:2064, :]
                            .rearrange("t v -> (t v)")
                            .rearrange("(jj p) -> p jj", p=128),
                            Dn[:, :], allow_slow_non_contiguous=True),
                            reads=["Dn"], writes=["ccin"])

                sweep(True)
                sch.dma_drain()
                with nc.Block() as blk:
                    sch.emit(blk)
                sweep(False)
                sch.dma_drain()
                with nc.Block() as blk:
                    sch.emit(blk)
                swp.close()

                if 4 in _PHASES:
                    Sin = sbt(p3, "p3_Sin", [128, 512], F32)
                    Tin = sbt(p3, "p3_Tin", [128, 512], F32)
                    Sg = [sbt(p3, f"p3_Sg{i}", [128, 512], F32) for i in range(2)]
                    Wl = sbt(p3, "p3_Wl", [128, 8 * 8], F32)
                    runv = sbt(p3, "p3_run", [128, 8], F32)
                    eDcC = sbt(p3, "p3_eDcC", [128, 8], F32)
                    eDc = [sbt(p3, f"p3_eDc{i}", [128, 8], F32) for i in range(2)]
                    eDc2 = [sbt(p3, f"p3_eDc2{i}", [128, 8], F32) for i in range(2)]
                    Sn16 = [sbt(p3, f"p3_Sn16{i}", [128, 512], BF16) for i in range(2)]
                    Tn16 = [sbt(p3, f"p3_Tn16{i}", [128, 512], BF16) for i in range(2)]
                    sigt = [sbt(p3, f"p5_sig{i}", [128, D], F32) for i in range(2)]
                    gt = [sbt(p3, f"p5_g{i}", [128, D], F32) for i in range(2)]
                    lnw = sbt(p3, "p5_lnw", [128, D], F32)
                    lnb = sbt(p3, "p5_lnb", [128, D], F32)
                    tYs = [sbt(p3, f"p5_tY{i}", [128, D], F32) for i in range(2)]
                    tSqs = [sbt(p3, f"p5_tSq{i}", [128, D], F32) for i in range(2)]
                    st16s = [sbt(p3, f"p5_st{i}", [128, 64], F32) for i in range(2)]
                    zt = [sbt(p3, f"p5_z{i}", [128, D], F32) for i in range(2)]
                    lwg = [sbt(p3, f"p5_lwg{i}", [128, D], F32) for i in range(2)]
                    lbg = [sbt(p3, f"p5_lbg{i}", [128, D], F32) for i in range(2)]
                    if 5 in _PHASES:
                        sch.op("actdma", lambda e: e.dma_start(
                            lnw[:, :], inp["ln_w_rep"].ap()), writes=["lnw"])
                        sch.op("actdma", lambda e: e.dma_start(
                            lnb[:, :], inp["ln_b_rep"].ap()), writes=["lnb"])

                    sch.op("pool", lambda e: e.collective_compute(
                        "AllGather", OP.bypass,
                        replica_groups=[list(range(N_CORES))],
                        ins=[sc["ccin"].ap().opt()],
                        outs=[sc["ccout"].ap().opt()]),
                        reads=["ccin"], writes=["ccout"], force_inc=True)
                    for cp in range(N_CORES):
                        sch.op("sp", lambda e, cp=cp: e.dma_start(
                            Wl[:, cp * 8:(cp + 1) * 8],
                            sc["ccout"].ap()[cp * 2064 + 2048: cp * 2064 + 2064, :]
                            .rearrange("t v -> (t v)")
                            .rearrange("(jj p) -> p jj", p=128),
                            allow_slow_non_contiguous=True),
                            reads=["ccout"], writes=["Wl"])

                    def combine(out_t, maskname, order):
                        okey = ("cmb", maskname)
                        sch.op("dve", lambda e, out_t=out_t: e.memset(out_t[:, :], 0.0),
                               writes=[okey])
                        sch.op("dve", lambda e: e.memset(runv[:, :], 0.0),
                               writes=["runv"])
                        base = 0 if maskname == "cmask_f" else 1024
                        for cp in order:
                            bb = cp % 2
                            sch.op("sp", lambda e, cp=cp, bb=bb, base=base: e.dma_start(
                                Sg[bb][:, :].rearrange("p (j v) -> p j v", v=64),
                                sc["ccout"].ap()[cp * 2064 + base: cp * 2064 + base + 1024, :]
                                .rearrange("(j p) v -> p j v", p=128)),
                                reads=["ccout"], writes=[("Sg", bb)])
                            sch.op("act", lambda e: e.activation(
                                eDcC[:, :], runv[:, :], AF.Exp),
                                reads=["runv"], writes=["eDcC"])
                            sch.op("dve", lambda e, cp=cp, maskname=maskname: e.tensor_scalar(
                                eDcC[:, :], eDcC[:, :],
                                smalls[maskname][:, cp:cp + 1], None, OP.mult),
                                reads=["eDcC", ("small", maskname)], writes=["eDcC"])
                            sch.op("dve", lambda e, bb=bb: e.tensor_tensor(
                                Stmp[:, :].rearrange("p (a b) -> p a b", b=64),
                                Sg[bb][:, :].rearrange("p (a b) -> p a b", b=64),
                                eDcC[:, :].unsqueeze(2).broadcast_to([128, 8, 64]),
                                OP.mult), reads=[("Sg", bb), "eDcC"], writes=["Stmp"])
                            sch.op("dve", lambda e, out_t=out_t, okey=okey: e.tensor_tensor(
                                out_t[:, :], out_t[:, :], Stmp[:, :], OP.add),
                                reads=["Stmp", okey], writes=[okey])
                            sch.op("dve", lambda e, cp=cp, maskname=maskname: e.scalar_tensor_tensor(
                                runv[:, :], Wl[:, cp * 8:(cp + 1) * 8],
                                smalls[maskname][:, cp:cp + 1], runv[:, :],
                                OP.mult, OP.add),
                                reads=["Wl", ("small", maskname), "runv"],
                                writes=["runv"])

                    if _P4_LEVEL >= 2:
                        combine(Sin, "cmask_f", list(range(N_CORES - 1, -1, -1)))
                        combine(Tin, "cmask_b", list(range(N_CORES)))
                    sch.dma_drain()
                    with nc.Block() as blk:
                        sch.emit(blk)

                    L4 = _P4_LEVEL - 2

                    def op4(lvl, *a, **k):
                        if L4 >= lvl:
                            return sch.op(*a, **k)
                    for n in (range(_P3_CHUNKS) if _P4_LEVEL >= 3 else []):
                        b = n % 2
                        bl = n % 3
                        op4(1, "sp", lambda e, n=n, bl=bl: e.dma_start(
                            r16[bl][:, :], sc["r"].ap()[n]),
                            reads=[("scr", "r")], writes=[("r16", bl)])
                        op4(1, "sp", lambda e, n=n, bl=bl: e.dma_start(
                            ptok[bl][:, :], sc["p"].ap()[n * C:(n + 1) * C, :]),
                            reads=[("scr", "p")], writes=[("ptok", bl)])
                        for j in range(8):
                            op4(2, "pe", lambda e, j=j, bl=bl: e.matmul(
                                pA[:, j * 128:(j + 1) * 128],
                                ptok[bl][:, j * 128:(j + 1) * 128],
                                smalls["tri_ci"][:, :],
                                start=(j % 4 == 0), stop=(j % 4 == 3)),
                                reads=[("ptok", bl), ("small", "tri_ci")],
                                writes=["pA"])
                        for j in range(8):
                            op4(2, "pe", lambda e, j=j, bl=bl: e.matmul(
                                pB[:, j * 128:(j + 1) * 128],
                                ptok[bl][:, j * 128:(j + 1) * 128],
                                smalls["tri_cs"][:, :],
                                start=(j % 4 == 0), stop=(j % 4 == 3)),
                                reads=[("ptok", bl), ("small", "tri_cs")],
                                writes=["pB"])
                        op4(2, "act", lambda e, b=b: e.activation(
                            E1u[b][:, :], pA[:, :], AF.Exp),
                            reads=["pA"], writes=[("E1u", b)])
                        op4(2, "act", lambda e, b=b: e.activation(
                            E3t[b][:, :], pB[:, :], AF.Exp),
                            reads=["pB"], writes=[("E3t", b)])
                        op4(3, "dve", lambda e, b=b, bl=bl: e.tensor_tensor(
                            R2[b][:, :], r16[bl][:, :], E1u[b][:, :], OP.mult),
                            reads=[("r16", bl), ("E1u", b)], writes=[("R2", b)])
                        op4(3, "dve", lambda e, b=b, bl=bl: e.tensor_tensor(
                            Rt[b][:, :], r16[bl][:, :], E3t[b][:, :], OP.mult),
                            reads=[("r16", bl), ("E3t", b)], writes=[("Rt", b)])
                        for par in range(2):
                            op4(3, "dve", lambda e, par=par, b=b: e.tensor_scalar(
                                R2P[b * 2 + par][:, :], R2[b][:, :],
                                smalls["pmask"][:, par:par + 1], None, OP.mult),
                                reads=[("R2", b), ("small", "pmask")],
                                writes=[("R2P", b * 2 + par)])
                            op4(3, "act", lambda e, par=par, b=b: e.activation(
                                KtP[b * 2 + par][:, :], Rt[b][:, :], AF.Identity,
                                scale=smalls["pmask"][:, par:par + 1]),
                                reads=[("Rt", b), ("small", "pmask")],
                                writes=[("KtP", b * 2 + par)])
                        op4(4, "act", lambda e, n=n, b=b: e.activation(
                            eDc[b][:, :], DnAll[:, n * 8:(n + 1) * 8], AF.Exp),
                            reads=["DnAll"], writes=[("eDc", b)])
                        op4(4, "act", lambda e, n=n, b=b: e.activation(
                            eDc2[b][:, :], BsAll[:, n * 8:(n + 1) * 8], AF.Exp),
                            reads=["BsAll"], writes=[("eDc2", b)])
                        op4(4, "dve", lambda e, b=b: e.tensor_tensor(
                            Sn16[b][:, :].rearrange("p (a b) -> p a b", b=64),
                            Sin[:, :].rearrange("p (a b) -> p a b", b=64),
                            eDc[b][:, :].unsqueeze(2).broadcast_to([128, 8, 64]),
                            OP.mult),
                            reads=[("cmb", "cmask_f"), ("eDc", b)],
                            writes=[("Sn16", b)])
                        op4(4, "dve", lambda e, b=b: e.tensor_tensor(
                            Tn16[b][:, :].rearrange("p (a b) -> p a b", b=64),
                            Tin[:, :].rearrange("p (a b) -> p a b", b=64),
                            eDc2[b][:, :].unsqueeze(2).broadcast_to([128, 8, 64]),
                            OP.mult),
                            reads=[("cmb", "cmask_b"), ("eDc2", b)],
                            writes=[("Tn16", b)])
                        for h in range(H):
                            jb = h // 2
                            op4(5, "pe", lambda e, h=h, jb=jb, b=b: e.matmul(
                                pY[:, h * 64:(h + 1) * 64],
                                R2P[b * 2 + h % 2][:, jb * 128:(jb + 1) * 128],
                                Sn16[b][:, jb * 64:(jb + 1) * 64],
                                start=(h == 0 or h == 8), stop=False),
                                reads=[("R2P", b * 2 + h % 2), ("Sn16", b)],
                                writes=["pY"])
                            op4(5, "pe", lambda e, h=h, jb=jb, b=b: e.matmul(
                                pY[:, h * 64:(h + 1) * 64],
                                KtP[b * 2 + h % 2][:, jb * 128:(jb + 1) * 128],
                                Tn16[b][:, jb * 64:(jb + 1) * 64],
                                start=False, stop=(h == 7 or h == 15)),
                                reads=[("KtP", b * 2 + h % 2), ("Tn16", b)],
                                writes=["pY"])
                        op4(6, "act", lambda e, b=b: e.activation(
                            E1u[b][:, :], pY[:, :], AF.Identity, bias=ccol("zero")),
                            reads=["pY", cread("zero")], writes=[("E1u", b)])
                        op4(6, "dve", lambda e, n=n, b=b: e.tensor_tensor(
                            ybuf[:, n * D:(n + 1) * D],
                            ybuf[:, n * D:(n + 1) * D], E1u[b][:, :], OP.add),
                            reads=[("E1u", b), ("y", n)], writes=[("y", n)])
                        if 5 not in _PHASES or L4 < 6:
                            continue
                        tY, tSq, st16 = tYs[b], tSqs[b], st16s[b]
                        kY, kSq, kSt = ("tY", b), ("tSq", b), ("st", b)
                        sch.op("sp", lambda e, n=n, b=b: e.dma_start(
                            sigt[b][:, :], sc["sig"].ap()[n * C:(n + 1) * C, :]),
                            reads=[("scr", "sig")], writes=[("sigt", b)])
                        sch.op("sp", lambda e, n=n, b=b: e.dma_start(
                            gt[b][:, :], sc["g"].ap()[n * C:(n + 1) * C, :]),
                            reads=[("scr", "g")], writes=[("gt", b)])
                        sch.op("pool", lambda e, b=b: e.tensor_tensor(
                            lwg[b][:, :], lnw[:, :], gt[b][:, :], OP.mult),
                            reads=["lnw", ("gt", b)], writes=[("lwg", b)])
                        sch.op("pool", lambda e, b=b: e.tensor_tensor(
                            lbg[b][:, :], lnb[:, :], gt[b][:, :], OP.mult),
                            reads=["lnb", ("gt", b)], writes=[("lbg", b)])
                        sch.op("dve", lambda e, n=n, b=b, tY=tY: e.tensor_tensor(
                            tY[:, :], ybuf[:, n * D:(n + 1) * D], sigt[b][:, :],
                            OP.mult),
                            reads=[("y", n), ("sigt", b)], writes=[kY])
                        sch.op("act", lambda e, tY=tY, tSq=tSq: e.activation(
                            tSq[:, :], tY[:, :], AF.Square), reads=[kY],
                            writes=[kSq])
                        sch.op("dve", lambda e, tY=tY, st16=st16: e.reduce_sum(
                            st16[:, 0:16], tY[:, :].rearrange("p (g k) -> p g k", k=64),
                            axis=AX), reads=[kY], writes=[kSt])
                        sch.op("dve", lambda e, tSq=tSq, st16=st16: e.reduce_sum(
                            st16[:, 16:32],
                            tSq[:, :].rearrange("p (g k) -> p g k", k=64),
                            axis=AX), reads=[kSq], writes=[kSt])
                        sch.op("dve", lambda e, st16=st16: e.tensor_scalar(
                            st16[:, 32:48], st16[:, 0:16], ccol("i64"), None, OP.mult),
                            reads=[kSt, cread("i64")], writes=[kSt])
                        sch.op("dve", lambda e, st16=st16: e.tensor_tensor(
                            st16[:, 48:64], st16[:, 32:48], st16[:, 32:48], OP.mult),
                            reads=[kSt], writes=[kSt])
                        sch.op("dve", lambda e, st16=st16: e.scalar_tensor_tensor(
                            st16[:, 48:64], st16[:, 16:32], ccol("i64"),
                            st16[:, 48:64], OP.mult, OP.subtract),
                            reads=[kSt, cread("i64")], writes=[kSt])
                        sch.op("act", lambda e, st16=st16: e.activation(
                            st16[:, 48:64], st16[:, 48:64], AF.Ln,
                            bias=ccol("eps_gn")),
                            reads=[kSt, cread("eps_gn")], writes=[kSt])
                        sch.op("act", lambda e, st16=st16: e.activation(
                            st16[:, 48:64], st16[:, 48:64], AF.Exp,
                            scale=ccol("neg_half")),
                            reads=[kSt, cread("neg_half")], writes=[kSt])
                        for hf, q_ in ((0, "dve"), (1, "pool")):
                            sch.op(q_, lambda e, tY=tY, st16=st16, hf=hf: e.tensor_tensor(
                                tY[:, hf * 512:hf * 512 + 512]
                                .rearrange("p (g k) -> p g k", k=64),
                                tY[:, hf * 512:hf * 512 + 512]
                                .rearrange("p (g k) -> p g k", k=64),
                                st16[:, 32 + hf * 8:32 + hf * 8 + 8]
                                .unsqueeze(2).broadcast_to([128, 8, 64]),
                                OP.subtract),
                                reads=[kSt, kY], writes=[("tYh", b, hf)])
                        for hf, q_ in ((0, "dve"), (1, "pool")):
                            sch.op(q_, lambda e, tY=tY, st16=st16, hf=hf: e.tensor_tensor(
                                tY[:, hf * 512:hf * 512 + 512]
                                .rearrange("p (g k) -> p g k", k=64),
                                tY[:, hf * 512:hf * 512 + 512]
                                .rearrange("p (g k) -> p g k", k=64),
                                st16[:, 48 + hf * 8:48 + hf * 8 + 8]
                                .unsqueeze(2).broadcast_to([128, 8, 64]),
                                OP.mult),
                                reads=[kSt, ("tYh", b, hf)],
                                writes=[("tYh", b, hf)])
                        sch.op("dve", lambda e, b=b, tY=tY: e.tensor_tensor(
                            tY[:, :], tY[:, :], lwg[b][:, :], OP.mult),
                            reads=[kY, ("tYh", b, 0), ("tYh", b, 1), ("lwg", b)],
                            writes=[kY])
                        sch.op("dve", lambda e, b=b, tY=tY: e.tensor_tensor(
                            zt[b][:, :], tY[:, :], lbg[b][:, :], OP.add),
                            reads=[kY, ("lbg", b)], writes=[("zt", b)])
                        sch.op("pooldma", lambda e, n=n, b=b: e.dma_start(
                            sc["zb"].ap()[n * C:(n + 1) * C, :], zt[b][:, :]),
                            reads=[("zt", b)], writes=[("scr", "zb")])
                    sch.dma_drain()
                    with nc.Block() as blk:
                        sch.emit(blk)
                p3.close()

            if "sc_y" in _DEBUG_SCRATCH and ybuf is not None:
                for n in range(NCH):
                    sch.op("sp", lambda e, n=n: e.dma_start(
                        sc["y"].ap()[n * C:(n + 1) * C, :],
                        ybuf[:, n * D:(n + 1) * D]),
                        reads=[("y", n)], force_inc=True)
                sch.dma_drain()
                with nc.Block() as blk:
                    sch.emit(blk)

            if 5 in _PHASES:
                with contextlib.ExitStack() as phs:
                    zf = sbt(phs, "p5_zf", [128, 8 * TC], BF16)
                    wo = sbt(phs, "p5_wo", [128, 8 * D], BF16)
                    xtt = [sbt(phs, f"p5_xt{i}", [128, D], F32) for i in range(2)]
                    x2t = [sbt(phs, f"p5_x2{i}", [128, D], F32) for i in range(2)]
                    sqWs = [sbt(phs, f"p5_sqW{i}", [128, D], F32) for i in range(2)]
                    lnsts = [sbt(phs, f"p5_lnst{i}", [128, 4], F32) for i in range(2)]
                    ppw = [phs.enter_context(
                        nc.psum_tensor(f"pp5_{i}", [128, 512], F32)) for i in range(4)]
                    sch.op("pooldma", lambda e: e.dma_start(wo[:, :], inp["Wo"].ap()),
                           writes=["wo"])
                    for jb in range(8):
                        sch.op("actdma", lambda e, jb=jb: e.dma_start(
                            zf[:, jb * TC:(jb + 1) * TC],
                            sc["zb"].ap()[:, jb * 128:(jb + 1) * 128], transpose=True),
                            reads=[("scr", "zb")], writes=[("zf", jb)])
                    for n in range(NCH):
                        b = n % 2
                        sqW, lnst = sqWs[b], lnsts[b]
                        kSq, kLn = ("sqW", b), ("lnst", b)
                        sch.op("sp", lambda e, n=n, b=b: e.dma_start(
                            xtt[b][:, :], inp["xt"].ap()[n * C:(n + 1) * C, :]),
                            writes=[("xtt", b)])
                        for half in range(2):
                            pb = ppw[2 * b + half]
                            for ic in range(8):
                                sch.op("pe", lambda e, n=n, half=half, ic=ic, pb=pb: e.matmul(
                                    pb[:, :],
                                    zf[:, ic * TC + n * 128: ic * TC + (n + 1) * 128],
                                    wo[:, ic * D + half * 512: ic * D + half * 512 + 512],
                                    start=(ic == 0), stop=(ic == 7)),
                                    reads=[("zf", ic), "wo"],
                                    writes=[("ppw", 2 * b + half)])
                            sch.op("dve", lambda e, half=half, b=b, pb=pb: e.tensor_tensor(
                                x2t[b][:, half * 512:half * 512 + 512], pb[:, :],
                                xtt[b][:, half * 512:half * 512 + 512], OP.add),
                                reads=[("ppw", 2 * b + half), ("xtt", b)],
                                writes=[("x2t", b)])
                        sch.op("dve", lambda e, b=b, lnst=lnst: e.reduce_sum(
                            lnst[:, 0:1], x2t[b][:, :], axis=AX),
                            reads=[("x2t", b)], writes=[kLn])
                        sch.op("dve", lambda e, lnst=lnst: e.tensor_scalar(
                            lnst[:, 0:1], lnst[:, 0:1], ccol("i1024"), None, OP.mult),
                            reads=[kLn, cread("i1024")], writes=[kLn])
                        sch.op("dve", lambda e, b=b, lnst=lnst: e.tensor_scalar(
                            x2t[b][:, :], x2t[b][:, :], lnst[:, 0:1], None,
                            OP.subtract),
                            reads=[("x2t", b), kLn], writes=[("x2t", b)])
                        sch.op("act", lambda e, b=b, sqW=sqW: e.activation(
                            sqW[:, :], x2t[b][:, :], AF.Square),
                            reads=[("x2t", b)], writes=[kSq])
                        sch.op("dve", lambda e, sqW=sqW, lnst=lnst: e.reduce_sum(
                            lnst[:, 1:2], sqW[:, :], axis=AX),
                            reads=[kSq], writes=[kLn])
                        sch.op("dve", lambda e, lnst=lnst: e.tensor_scalar(
                            lnst[:, 1:2], lnst[:, 1:2], ccol("i1024"), None, OP.mult),
                            reads=[kLn, cread("i1024")], writes=[kLn])
                        sch.op("act", lambda e, lnst=lnst: e.activation(
                            lnst[:, 1:2], lnst[:, 1:2], AF.Ln, bias=ccol("eps_ln")),
                            reads=[kLn, cread("eps_ln")], writes=[kLn])
                        sch.op("act", lambda e, lnst=lnst: e.activation(
                            lnst[:, 1:2], lnst[:, 1:2], AF.Exp,
                            scale=ccol("neg_half")),
                            reads=[kLn, cread("neg_half")], writes=[kLn])
                        sch.op("dve", lambda e, b=b, lnst=lnst: e.tensor_scalar(
                            x2t[b][:, :], x2t[b][:, :], lnst[:, 1:2], None, OP.mult),
                            reads=[("x2t", b), kLn], writes=[("x2t", b)])
                        sch.op("actdma", lambda e, n=n, b=b: e.dma_start(
                            sc["x2"].ap()[n * C:(n + 1) * C, :], x2t[b][:, :]),
                            reads=[("x2t", b)], writes=[("scr", "x2")])
                        sch.op("pooldma", lambda e, n=n, b=b: e.dma_start(
                            sc["x2b"].ap()[n * C:(n + 1) * C, :], x2t[b][:, :]),
                            reads=[("x2t", b)], writes=[("scr", "x2b")])
                    sch.dma_drain()
                    with nc.Block() as blk:
                        sch.emit(blk)

        if 6 in _PHASES:
            with contextlib.ExitStack() as phs:
                TT6 = 1024
                x2f = sbt(phs, "p6_x2f", [128, 8 * TT6], BF16)
                hsb = sbt(phs, "p6_h", [128, 32 * TT6], BF16)
                usb = sbt(phs, "p6_u", [128, 32 * TT6], BF16)
                wA = [sbt(phs, f"p6_wA{i}", [128, DH], BF16) for i in range(2)]
                wB = [sbt(phs, f"p6_wB{i}", [128, D], BF16) for i in range(3)]
                w1s = [sbt(phs, f"p6_w1s{i}", [128, D], BF16) for i in range(3)]
                w3blk = [sbt(phs, f"p6_w3b{i}", [128, 1024], BF16) for i in range(3)]
                s1s = sbt(phs, "p6_s1", [128, 32], F32)
                s2s = sbt(phs, "p6_s2", [128, 32], F32)
                sh1s = sbt(phs, "p6_sh1", [128, 32], F32)
                sh2s = sbt(phs, "p6_sh2", [128, 32], F32)
                s3r = sbt(phs, "p6_s3r", [128, D], F32)
                silu_ts = [sbt(phs, f"p6_silu{i}", [128, 512], F32) for i in range(2)]
                x2l = [sbt(phs, "p6_x2l0", [128, D], F32)]
                otW = sbt(phs, "p6_otW", [128, 2 * D], F32)
                sq6 = sbt(phs, "p6_sq", [128, D], F32)
                lnst6 = sbt(phs, "p6_lnst", [128, 4], F32)
                pf = [phs.enter_context(
                    nc.psum_tensor(f"pp6_{i}", [128, 512], F32)) for i in range(8)]
                for nm, t_ in [("s1c", s1s), ("s2c", s2s), ("sh1c", sh1s),
                               ("sh2c", sh2s)]:
                    sch.op("sp", lambda e, nm=nm, t_=t_: e.dma_start(
                        t_[:, :], inp[nm].ap()), writes=[("p6s", nm)])
                sch.op("sp", lambda e: e.dma_start(s3r[:, :], inp["s3_rep"].ap()),
                       writes=["s3r"])

                for tt6 in range(TC // TT6):
                    for jb in range(8):
                        sch.op("actdma", lambda e, tt6=tt6, jb=jb: e.dma_start(
                            x2f[:, jb * TT6:(jb + 1) * TT6],
                            sc["x2b"].ap()[tt6 * TT6:(tt6 + 1) * TT6,
                                           jb * 128:(jb + 1) * 128], transpose=True),
                            reads=[("scr", "x2b")], writes=[("x2f", jb)])
                    for oc in range(32):
                        bw = oc % 3
                        sch.op("pooldma", lambda e, oc=oc, bw=bw: e.dma_start(
                            w1s[bw][:, :], inp["w1b"].ap()[oc]),
                            writes=[("w1s", bw)])
                        sch.op("pooldma", lambda e, oc=oc, bw=bw: e.dma_start(
                            wB[bw][:, :], inp["w2b"].ap()[oc]),
                            writes=[("wB", bw)])
                        po = (oc % 2) * 4
                        for ic in range(8):
                            for half in range(2):
                                sch.op("pe", lambda e, half=half, ic=ic, bw=bw, po=po: e.matmul(
                                    pf[po + half][:, :],
                                    w1s[bw][:, ic * 128:(ic + 1) * 128],
                                    x2f[:, ic * TT6 + half * 512: ic * TT6 + half * 512 + 512],
                                    start=(ic == 0), stop=(ic == 7)),
                                    reads=[("w1s", bw), ("x2f", ic)],
                                    writes=[("pf", po + half)])
                            for half in range(2):
                                sch.op("pe", lambda e, half=half, ic=ic, bw=bw, po=po: e.matmul(
                                    pf[po + 2 + half][:, :],
                                    wB[bw][:, ic * 128:(ic + 1) * 128],
                                    x2f[:, ic * TT6 + half * 512: ic * TT6 + half * 512 + 512],
                                    start=(ic == 0), stop=(ic == 7)),
                                    reads=[("wB", bw), ("x2f", ic)],
                                    writes=[("pf", po + 2 + half)])
                        for half in range(2):
                            st_ = silu_ts[(oc + half) % 2]
                            sch.op("act", lambda e, oc=oc, half=half, po=po, st_=st_: e.activation(
                                st_[:, :], pf[po + half][:, :], AF.Silu,
                                scale=s1s[:, oc:oc + 1]),
                                reads=[("pf", po + half), ("p6s", "s1c")],
                                writes=[("silu_t", (oc + half) % 2)])
                            sch.op("dve", lambda e, oc=oc, half=half, po=po, st_=st_: e.scalar_tensor_tensor(
                                hsb[:, oc * TT6 + half * 512: oc * TT6 + half * 512 + 512],
                                pf[po + 2 + half][:, :], s2s[:, oc:oc + 1],
                                st_[:, :],
                                OP.mult, OP.mult),
                                reads=[("pf", po + 2 + half), ("p6s", "s2c"),
                                       ("silu_t", (oc + half) % 2)],
                                writes=[("hsb", oc)])
                    for stage, (wkey, src, dst, dkey, scl, sclk) in enumerate([
                            ("h1b", "hsb", usb, "usb", sh1s, "sh1c"),
                            ("h2b", "usb", hsb, "hsb", sh2s, "sh2c")]):
                        srcsb = hsb if src == "hsb" else usb
                        for oc in range(32):
                            bw = oc % 2
                            sch.op("pooldma", lambda e, oc=oc, bw=bw, wkey=wkey: e.dma_start(
                                wA[bw][:, :], inp[wkey].ap()[oc]),
                                writes=[("wA", bw)])
                            boff = 4 - 4 * stage
                            for ic in range(32):
                                for half in range(2):
                                    pi = boff + (oc % 2) * 2 + half
                                    sch.op("pe", lambda e, half=half, ic=ic, pi=pi, bw=bw, srcsb=srcsb: e.matmul(
                                        pf[pi][:, :], wA[bw][:, ic * 128:(ic + 1) * 128],
                                        srcsb[:, ic * TT6 + half * 512: ic * TT6 + half * 512 + 512],
                                        start=(ic == 0), stop=(ic == 31)),
                                        reads=[("wA", bw), (src, ic)],
                                        writes=[("pf", pi)])
                            for half in range(2):
                                pi = boff + (oc % 2) * 2 + half
                                sch.op("act", lambda e, oc=oc, half=half, pi=pi, dst=dst, scl=scl: e.activation(
                                    dst[:, oc * TT6 + half * 512: oc * TT6 + half * 512 + 512],
                                    pf[pi][:, :], AF.Gelu, scale=scl[:, oc:oc + 1]),
                                    reads=[("pf", pi), ("p6s", sclk)],
                                    writes=[(dkey, oc)])
                    for wave in range(2):
                        for ic in range(32):
                            wb3 = ic % 3
                            sch.op("pooldma", lambda e, ic=ic, wb3=wb3: e.dma_start(
                                w3blk[wb3][:, :],
                                inp["w3t"].ap()[:, ic * D:(ic + 1) * D]),
                                writes=[("w3b", wb3)])
                            for q in range(4):
                                tc8 = wave * 4 + q
                                for half in range(2):
                                    sch.op("pe", lambda e, tc8=tc8, ic=ic, q=q, half=half, wb3=wb3: e.matmul(
                                        pf[q * 2 + half][:, :],
                                        hsb[:, ic * TT6 + tc8 * 128: ic * TT6 + (tc8 + 1) * 128],
                                        w3blk[wb3][:, half * 512:half * 512 + 512],
                                        start=(ic == 0), stop=(ic == 31)),
                                        reads=[("hsb", ic), ("w3b", wb3)],
                                        writes=[("pf", q * 2 + half)])
                        for q in range(4):
                            tc8 = wave * 4 + q
                            n = tt6 * (TT6 // 128) + tc8
                            b = n % 2
                            ow = otW[:, (q % 2) * D:(q % 2 + 1) * D]
                            for half in range(2):
                                sch.op("dve", lambda e, q=q, half=half: e.tensor_tensor(
                                    otW[:, (q % 2) * D + half * 512:
                                        (q % 2) * D + half * 512 + 512],
                                    pf[q * 2 + half][:, :],
                                    s3r[:, half * 512:half * 512 + 512], OP.mult),
                                    reads=[("pf", q * 2 + half), "s3r"],
                                    writes=[("otW", q % 2)])
                            sch.op("sp", lambda e, n=n: e.dma_start(
                                x2l[0][:, :], sc["x2"].ap()[n * C:(n + 1) * C, :]),
                                reads=[("scr", "x2")], writes=["x2l"])
                            sch.op("dve", lambda e, ow=ow: e.tensor_tensor(
                                ow, ow, x2l[0][:, :], OP.add),
                                reads=[("otW", q % 2), "x2l"], writes=[("otW", q % 2)])
                            sch.op("dve", lambda e, ow=ow: e.reduce_sum(
                                lnst6[:, 0:1], ow, axis=AX),
                                reads=[("otW", q % 2)], writes=["lnst6"])
                            sch.op("dve", lambda e: e.tensor_scalar(
                                lnst6[:, 0:1], lnst6[:, 0:1], ccol("i1024"), None,
                                OP.mult),
                                reads=["lnst6", cread("i1024")], writes=["lnst6"])
                            sch.op("dve", lambda e, ow=ow: e.tensor_scalar(
                                ow, ow, lnst6[:, 0:1], None, OP.subtract),
                                reads=[("otW", q % 2), "lnst6"], writes=[("otW", q % 2)])
                            sch.op("act", lambda e, ow=ow: e.activation(
                                sq6[:, :], ow, AF.Square),
                                reads=[("otW", q % 2)], writes=["sq6"])
                            sch.op("dve", lambda e: e.reduce_sum(
                                lnst6[:, 1:2], sq6[:, :], axis=AX),
                                reads=["sq6"], writes=["lnst6"])
                            sch.op("dve", lambda e: e.tensor_scalar(
                                lnst6[:, 1:2], lnst6[:, 1:2], ccol("i1024"), None,
                                OP.mult),
                                reads=["lnst6", cread("i1024")], writes=["lnst6"])
                            sch.op("act", lambda e: e.activation(
                                lnst6[:, 1:2], lnst6[:, 1:2], AF.Sqrt,
                                bias=ccol("eps_ln")),
                                reads=["lnst6", cread("eps_ln")], writes=["lnst6"])
                            sch.op("dve", lambda e: e.reciprocal(
                                lnst6[:, 1:2], lnst6[:, 1:2]),
                                reads=["lnst6"], writes=["lnst6"])
                            sch.op("dve", lambda e, ow=ow: e.tensor_scalar(
                                ow, ow, lnst6[:, 1:2], None, OP.mult),
                                reads=[("otW", q % 2), "lnst6"], writes=[("otW", q % 2)])
                            sch.op("sp", lambda e, n=n, ow=ow: e.dma_start(
                                out.ap()[n * C:(n + 1) * C, :], ow),
                                reads=[("otW", q % 2)], force_inc=True)
                sch.dma_drain()
                with nc.Block() as blk:
                    sch.emit(blk)

        with nc.Block() as blk:
            def final_waits(q):
                n_dma = sch.dma_count[q]
                vals = []
                for idx in range(_DMA_SUBSEMS):
                    cnt = (n_dma - idx + _DMA_SUBSEMS - 1) // _DMA_SUBSEMS
                    if cnt > 0:
                        vals.append((idx, 16 * cnt))
                return vals

            sp_waits = final_waits("sp")
            pd_waits = final_waits("pooldma")
            ad_waits = final_waits("actdma")

            @blk.sync
            def _(e):
                for idx, v in sp_waits:
                    e.wait_ge(dma_sems["sp"][idx], v)

            @blk.gpsimd
            def _(e):
                for idx, v in pd_waits:
                    e.wait_ge(dma_sems["pooldma"][idx], v)

            @blk.scalar
            def _(e):
                for idx, v in ad_waits:
                    e.wait_ge(dma_sems["actdma"][idx], v)

    return nc


def kernel(**inputs):
    in_maps = prep_inputs(inputs)
    nc = build_nc()
    res = run_bass_kernel_spmd(nc, in_maps, core_ids=list(range(N_CORES)))
    outp = np.concatenate(
        [np.asarray(r["out"], np.float32) for r in res.results], axis=0)
    return outp.reshape(1, T, D)
